# version 01
# baseline (speedup 1.0000x reference)
import math
import numpy as np
import ml_dtypes
import concourse.bass as bass
import concourse.mybir as mybir
from concourse.bass_utils import run_bass_kernel_spmd

F32 = mybir.dt.float32
BF16 = mybir.dt.bfloat16
I8 = mybir.dt.int8
AF = mybir.ActivationFunctionType
ALU = mybir.AluOpType
AX = mybir.AxisListType

RND = 12582912.0
T, V, S, G, SY = "T", "V", "S", "G", "SY"
ENGS = (T, V, S, G, SY)
_s0 = 2.0 ** (-2.0 ** -1.0)
ALIBI16 = [_s0 * _s0 ** i for i in range(16)]
SKIP = 98.0
DBG = {}


class MK:
    DPOOL = 24

    def __init__(self, nc):
        self.nc = nc
        self.prog = {e: [] for e in ENGS}
        self.cnt = {}
        self.sem = {}
        self.mark = {e: {} for e in ENGS}
        for name in ("st", "sv", "ss"):
            self.sem[name] = nc.alloc_semaphore(name)
            self.cnt[name] = 0
        self.esem = {T: "st", V: "sv", S: "ss"}
        self.dpool = {}
        self.didx = {G: 0, SY: 0}
        for eng, pfx in ((G, "g"), (SY, "y")):
            for i in range(self.DPOOL):
                name = f"d{pfx}{i}"
                self.sem[name] = nc.alloc_semaphore(name)
                self.cnt[name] = 0
            self.dpool[eng] = [f"d{pfx}{i}" for i in range(self.DPOOL)]

    def emit(self, eng, fn, waits=(), inc=None, inc_n=1):
        if DBG.get("serialize") and eng in (T, V, S):
            own = self.esem[eng]
            waits = list(waits) + [(own, self.cnt[own])]
            if inc is None or inc is False:
                inc = True
        wl = []
        m = self.mark[eng]
        for tok in waits:
            if tok is None:
                continue
            semn, val = tok
            assert val <= self.cnt[semn], f"future token {semn}={val}>{self.cnt[semn]}"
            if m.get(semn, 0) < val:
                m[semn] = val
                wl.append((self.sem[semn], val))
        tok = None
        semh = None
        n = 0
        if inc is None and eng in (V, S):
            inc = True
        if inc is True:
            inc = self.esem[eng]
        if inc:
            self.cnt[inc] += inc_n
            tok = (inc, self.cnt[inc])
            semh = self.sem[inc]
            n = inc_n

        if DBG.get("trace_em") and eng == DBG.get("trace_eng", "T"):
            print(f"[{eng} #{len(self.prog[eng])}] waits={[(s, v) for (s, v) in [(tok2[0], tok2[1]) for tok2 in []]] }", end="")
            print(f" wl={[(h.name if hasattr(h, 'name') else h, v) for (h, v) in wl]} inc={inc}")
        def thunk(e, wl=wl, fn=fn, semh=semh, n=n):
            for s, v in wl:
                e.wait_ge(s, v)
            ins = fn(e)
            if semh is not None:
                ins.then_inc(semh, n)
        self.prog[eng].append(thunk)
        return tok

    def dma(self, eng, out, in_, waits=()):
        semn = self.dpool[eng][self.didx[eng] % self.DPOOL]
        self.didx[eng] += 1
        waits = list(waits) + ([(semn, self.cnt[semn])] if self.cnt[semn] else [])
        try:
            nm = out.tensor.name
        except Exception:
            nm = None
        if nm is not None:
            if not hasattr(self, "last_dma"):
                self.last_dma = {}
            waits.append(self.last_dma.get(nm))
        tok = self.emit(eng, lambda e: e.dma_start(out, in_), waits=waits, inc=semn, inc_n=16)
        if nm is not None:
            self.last_dma[nm] = tok
        return tok

    def wait_only(self, eng, toks):
        m = self.mark[eng]
        wl = []
        for tok in toks:
            if tok is None:
                continue
            semn, val = tok
            assert val <= self.cnt[semn]
            if m.get(semn, 0) < val:
                m[semn] = val
                wl.append((self.sem[semn], val))
        if wl:
            def thunk(e, wl=wl):
                for s, v in wl:
                    e.wait_ge(s, v)
            self.prog[eng].append(thunk)

    def run_block(self):
        nc = self.nc
        with nc.Block() as blk:
            @blk.tensor
            def _(e):
                for f in self.prog[T]:
                    f(e)

            @blk.vector
            def _(e):
                for f in self.prog[V]:
                    f(e)

            @blk.scalar
            def _(e):
                for f in self.prog[S]:
                    f(e)

            @blk.gpsimd
            def _(e):
                for f in self.prog[G]:
                    f(e)

            @blk.sync
            def _(e):
                for f in self.prog[SY]:
                    f(e)


def build(stages=(1, 2, 3, 4)):
    stages = set(stages)
    nc = bass.Bass("TRN2", target_bir_lowering=False, detect_race_conditions=not DBG.get("no_rd", True))
    mk = MK(nc)
    em = mk.emit
    dma = mk.dma

    def din(name, shape, dt=F32):
        return nc.dram_tensor(name, shape, dt, kind="ExternalInput").ap()

    x_in = din("x", [2, 2048, 1024])
    out_t = nc.dram_tensor("out", [2, 2048, 1024], F32, kind="ExternalOutput").ap()
    W = dict(
        l0_wq=din("L0_attn_Wq", [1024, 1024]), l0_wk=din("L0_attn_Wk", [1024, 1024]),
        l0_wv=din("L0_attn_Wv", [1024, 1024]), l0_wo=din("L0_attn_Wo", [1024, 1024]),
        l0_bo=din("L0_attn_bo", [1024]),
        f0_wq=din("L0_ffn_weight_q", [1024, 1024], I8), f0_ws=din("L0_ffn_weight_scale", [1024]),
        f0_b=din("L0_ffn_bias", [1024]),
        l1_g=din("L1_attn_gamma", [1024]), l1_wq=din("L1_attn_w_q", [1024, 1024]),
        l1_wkv=din("L1_attn_w_kv", [1024, 2048]), l1_nkv=din("L1_attn_null_kv", [16, 4, 64]),
        l1_qs=din("L1_attn_q_scale", [64]), l1_ks=din("L1_attn_k_scale", [64]),
        l1_wo=din("L1_attn_w_out", [1024, 1024]),
        f1_wqkv=din("L1_ffn_wqkv", [2, 1024, 1536]), f1_qs=din("L1_ffn_q_scale", [2, 64]),
        f1_ks=din("L1_ffn_k_scale", [2, 64]), f1_wo=din("L1_ffn_wo", [2, 512, 1024]),
        f1_lng=din("L1_ffn_ln_g", [2, 1024]), f1_lnb=din("L1_ffn_ln_b", [2, 1024]),
        f1_w1a=din("w1a_pad", [2, 1024, 2816]), f1_w1g=din("w1g_pad", [2, 1024, 2816]),
        f1_b1a=din("b1a_pad", [2, 2816]), f1_b1g=din("b1g_pad", [2, 2816]),
        f1_w2=din("w2_pad", [2, 2816, 1024]), f1_b2=din("L1_ffn_b2", [2, 1024]),
        c_ident=din("c_ident", [128, 128], BF16),
        c_m0=din("c_m0", [128, 128], BF16),
        c_mleft=din("c_mleft", [128, 128], BF16),
        c_alibi=din("c_alibi", [16, 5, 2050], BF16),
        c_ci=din("c_ci", [16, 2048], BF16),
        c_rot=din("c_rot", [4, 2048, 64], BF16),
    )
    xout_sp = nc.dram_tensor("xouter_scratch", [2048, 1024], F32, kind="Internal").ap()

    sb = nc.alloc_sbuf_tensor

    def at(name, shape, dt, off):
        return nc.alloc_sbuf_tensor_at(name, shape, dt, offset=off)

    resid = sb("resid", [128, 16, 1024], F32)
    xt = sb("xt", [128, 8, 2048], BF16)
    stage_b = sb("stage_b", [128, 1024], BF16)
    stage_b2 = sb("stage_b2", [128, 1024], BF16)
    stage_f = sb("stage_f", [128, 1024], F32)
    wld = sb("wld", [128, 3072], BF16)
    bc1 = sb("bc1", [128, 1024], F32)
    rotc = sb("rotc", [128, 4, 16, 64], BF16)
    rowf = sb("rowf", [128, 1024], F32)
    rowb = sb("rowb", [128, 1024], BF16)
    ones_r = sb("ones_r", [1, 128], BF16)
    ones_c = sb("ones_c", [128, 2], BF16)
    m0_t = sb("m0_t", [128, 128], BF16)
    mleft_t = sb("mleft_t", [128, 128], BF16)
    ident_t = sb("ident_t", [128, 128], BF16)
    stats = sb("stats", [128, 4, 16], F32)
    statw = sb("statw", [128, 32], F32)
    sqb = sb("sqb", [128, 16, 4], F32)
    knt = sb("knt", [64, 32], BF16)
    nkst = sb("nkst", [32, 64], F32)
    vnaug = sb("vnaug", [2, 2, 128], BF16)
    qs_bc = sb("qs_bc", [128, 64], F32)
    ks_bc = sb("ks_bc", [128, 64], F32)
    b1a_c = sb("b1a_c", [128, 22], F32)
    b1g_c = sb("b1g_c", [128, 22], F32)
    g_c = sb("g_c", [128, 8], F32)
    b_c = sb("b_c", [128, 8], F32)

    U1 = nc.bump_sbuf(61536)[0]
    ot = at("ot", [128, 8, 2048], BF16, U1)
    vaug = at("vaug", [128, 16, 2, 128], BF16, U1 + 32768)
    qt = at("qt", [128, 2048], BF16, U1 + 40960)
    kt = at("kt", [128, 2080], BF16, U1 + 45056)
    ebuf0 = at("ebuf0", [128, 1024], BF16, U1 + 49248)
    ebuf1 = at("ebuf1", [128, 1024], BF16, U1 + 51296)
    wt1 = at("wt1", [128, 8, 128], BF16, U1 + 53344)
    wt2 = at("wt2", [128, 8, 128], BF16, U1 + 55392)
    wt3 = at("wt3", [128, 8, 128], BF16, U1 + 57440)
    rbt = at("rbt", [128, 1024], BF16, U1 + 59488)
    ebuf = [ebuf0, ebuf1]
    fbuf = at("fbuf", [128, 11, 2048], BF16, U1)
    w2res = at("w2res", [128, 6, 1024], BF16, U1 + 45056)
    xosb = at("xosb", [128, 1024], F32, U1 + 57344)
    bc2 = at("bc2", [128, 1024], F32, U1)
    bc3 = at("bc3", [128, 1024], F32, U1 + 4096)

    U2 = nc.bump_sbuf(16384)[0]
    wres = at("wres", [128, 8, 1024], BF16, U2)
    qtok = at("qtok", [128, 16, 128], BF16, U2)
    ktok = at("ktok", [128, 16, 128], BF16, U2 + 4096)
    w1buf0 = at("w1buf0", [128, 8, 256], BF16, U2 + 8192)
    w1buf1 = at("w1buf1", [128, 8, 256], BF16, U2 + 12288)
    w1buf = [w1buf0, w1buf1]
    cbias = sb("cbias", [128, 1], F32)
    print("SBUF remaining bytes/partition:", nc.sbuf_bytes_remaining // 128)

    pA = nc.alloc_psum_tensor("pA", [128, 512], F32)
    pC = nc.alloc_psum_tensor("pC", [128, 1024], F32)
    pD = nc.alloc_psum_tensor("pD", [128, 1024], F32)
    pE = nc.alloc_psum_tensor("pE", [128, 1024], F32)
    pF = nc.alloc_psum_tensor("pF", [128, 1024], BF16)

    class St:
        pA_free = None
        pC_free = None
        pD_free = None
        pE_free = None
        eb_free = [None, None]
        vaug_free = None
        qk_free = None
        xt_free = None
        wres_free = None
        wld_free = None
        scb_free = None
        rbt_free = None
        fbuf_free = None
        stageb_free = None
        stageb2_free = None
        stagef_free = None
        rowf_free = None
        rbt_w_free = None
        w2res_free = None
        bc1_free = None
        spill_toks = [None] * 4
        tr_free = [None, None, None]
    st = St()

    t_ident = dma(SY, ident_t[:, :], W["c_ident"])
    t_m0 = dma(SY, m0_t[:, :], W["c_m0"])
    t_mleft = dma(SY, mleft_t[:, :], W["c_mleft"])
    t_rot = dma(SY, rotc[:, :, :, :], W["c_rot"].rearrange("t (i p) d -> p t i d", p=128))
    em(V, lambda e: e.memset(ones_r[:, :], 1.0))
    t_ones = em(V, lambda e: e.memset(ones_c[:, :], 1.0), inc=True)
    for p in (pA, pC, pD):
        em(V, lambda e, p=p: e.memset(p[:, :], 0.0))
    t_psz = em(V, lambda e: e.memset(pE[:, :], 0.0), inc=True)
    em(V, lambda e: e.memset(cbias[:, :], -12.0))
    em(V, lambda e: e.memset(vnaug[:, :, :], 0.0))
    em(V, lambda e: e.memset(vnaug[:, 0, 64:66], 1.0))
    t_vn0 = em(V, lambda e: e.memset(vnaug[:, 1, 32:34], 1.0), inc=True)

    final_toks = []

    def init_attn_arena():
        em(V, lambda e: e.memset(vaug[:, :, :, :], 0.0), waits=[st.vaug_free])
        em(V, lambda e: e.memset(vaug[:, :, 0, 64:66], 1.0))
        t_a = em(V, lambda e: e.memset(vaug[:, :, 1, 32:34], 1.0), inc=True)
        t_b = em(V, lambda e: e.memset(qt[64:68, 0:2048], 1.0), waits=[st.qk_free], inc=True)
        return t_b, t_a

    def bcast_row(dst, src_row, waits=()):
        return dma(SY, dst, src_row.unsqueeze(0).partition_broadcast(128), waits=waits)

    def trans_group(srcs, writers, waits=(), n_out=128):
        assert len(srcs) <= 8
        t_t = None
        for i, s in enumerate(srcs):
            kp = s.shape[0]
            ncol = s.shape[-1]
            t_t = em(T, lambda e, s=s, i=i, kp=kp, ncol=ncol: e.transpose(
                pF[0:ncol, i * 128:i * 128 + kp], s, ident_t[0:kp, 0:kp]),
                waits=list(waits) + [st.tr_free[0], t_ident],
                inc=(i == len(srcs) - 1))
        t_e = None
        for w in writers:
            t_e = em(V, lambda e, w=w: w(e), waits=[t_t], inc=True)
        st.tr_free[0] = t_e
        return t_e

    def ln_stats(tile_toks, dummy=None):
        if dummy is None:
            dummy = stage_f[:, :]
        t_red = None
        for it in range(16):
            w = [tile_toks[it]] if tile_toks else []
            em(V, lambda e, it=it: e.tensor_reduce(stats[:, 0, it:it + 1], resid[:, it, :],
                                                   AX.X, ALU.add), waits=w, inc=True)
        for it in range(16):
            t_red = em(V, lambda e, it=it, dummy=dummy: e.scalar_tensor_tensor(
                dummy, resid[:, it, :], 1.0, resid[:, it, :], ALU.mult, ALU.mult,
                accum_out=stats[:, 1, it:it + 1]), inc=True)
        t_mu = em(V, lambda e: e.tensor_scalar(stats[:, 2, :], stats[:, 0, :], 1.0 / 1024.0,
                                               None, ALU.mult), waits=[t_red])
        t_m2 = em(V, lambda e: e.tensor_tensor(statw[:, 0:16], stats[:, 2, :],
                                               stats[:, 2, :], ALU.mult), waits=[t_mu])
        t_va = em(V, lambda e: e.scalar_tensor_tensor(statw[:, 16:32], stats[:, 1, :],
                                                      1.0 / 1024.0, statw[:, 0:16],
                                                      ALU.mult, ALU.subtract), waits=[t_m2])
        t_v = em(V, lambda e: e.tensor_scalar(statw[:, 16:32], statw[:, 16:32], 1e-5, None,
                                              ALU.add), waits=[t_va], inc=True)
        t_sq = em(S, lambda e: e.activation(statw[:, 0:16], statw[:, 16:32], AF.Sqrt),
                  waits=[t_v], inc=True)
        return em(V, lambda e: e.reciprocal(stats[:, 3, :], statw[:, 0:16]),
                  waits=[t_sq], inc=True)

    def ln_apply_inplace(t_rs):
        out = []
        for it in range(16):
            out.append(em(V, lambda e, it=it: e.tensor_scalar(
                resid[:, it, :], resid[:, it, :], stats[:, 2, it:it + 1], stats[:, 3, it:it + 1],
                ALU.subtract, ALU.mult), waits=[t_rs], inc=True))
        return out

    def build_xt(tile_toks, shift=False, pre=None):
        last = None
        t_shift = None
        if shift:
            t_shift = em(V, lambda e: e.memset(xt[:, 4:8, 0:2], 0.0), waits=[st.xt_free], inc=True)
        for it in range(16):
            if pre is not None:
                t_c = pre(it)
            else:
                w = [tile_toks[it]] if tile_toks else []
                t_c = em(V, lambda e, it=it: e.tensor_copy(stage_b[:, :], resid[:, it, :]),
                         waits=w, inc=True)
            srcs = [stage_b[:, c * 128:(c + 1) * 128] for c in range(8)]
            if not shift:
                def w0(e, it=it):
                    return e.tensor_copy(xt[:, 0:8, it * 128:(it + 1) * 128],
                                         pF[:, 0:1024].rearrange("p (c n) -> p c n", n=128))
                writers = [w0]
            else:
                def w0(e, it=it):
                    return e.tensor_copy(xt[:, 0:4, it * 128:(it + 1) * 128],
                                         pF[:, 0:512].rearrange("p (c n) -> p c n", n=128))
                n2 = 128 if it < 15 else 127

                def w1(e, it=it, n2=n2):
                    return e.tensor_copy(
                        xt[:, 4:8, it * 128 + 1:it * 128 + 1 + n2],
                        pF[:, 512:1024].rearrange("p (c n) -> p c n", n=128)[:, :, 0:n2])
                writers = [w0, w1]
            last = trans_group(srcs, writers, waits=[t_c, t_shift, st.xt_free])
        return last

    def load_x(seq):
        toks = []
        prev_stores = final_toks[-4:] if final_toks else []
        for q in range(4):
            src = x_in[seq].rearrange("(t p) d -> p t d", p=128)[:, 4 * q:4 * q + 4, :]
            tk = dma(SY, resid[:, 4 * q:4 * q + 4, :], src, waits=list(prev_stores))
            toks += [tk] * 4
        return toks

    def store_out(seq, tile_toks):
        for q in range(4):
            w = [tile_toks[it] for it in range(4 * q, 4 * q + 4)]
            dst = out_t[seq].rearrange("(t p) d -> p t d", p=128)[:, 4 * q:4 * q + 4, :]
            final_toks.append(dma(SY, dst, resid[:, 4 * q:4 * q + 4, :], waits=w))

    def build_wresT(src_ap_rows, gate=None):
        last = None
        for rc in range(8):
            t_l = dma(G, wld[:, 0:1024], src_ap_rows(rc), waits=[last, gate, st.wld_free])
            srcs = [wld[:, c * 128:(c + 1) * 128] for c in range(8)]

            def w0(e, rc=rc):
                return e.tensor_copy(wres[:, :, rc * 128:(rc + 1) * 128],
                                     pF[:, 0:1024].rearrange("p (c n) -> p c n", n=128))
            last = trans_group(srcs, [w0], waits=[t_l, st.wres_free])
            st.wld_free = last
        return last

    def attn_head(*, kbuf_ap, qbuf_ap, chunks, parity, ot_dst_fn,
                  gate_tok=None, exp_bias=0.0):
        prow = 65 if parity == 0 else 128
        lrow = 64 if parity == 0 else 32
        orows = (0, 64) if parity == 0 else (64, 128)
        enull = stage_b2[0:2, :]
        def pieces(col_lo, col_n):
            out = []
            off = col_lo
            end = col_lo + col_n
            while off < end:
                n = min(end - off, 512 - (off % 512))
                out.append((off, n))
                off += n
            return out

        for ici, (ic_off, ic_n, jt_list, do_null) in enumerate(chunks):
            n_halves = (ic_n + 511) // 512
            seq_banks = []
            for (_, cl, cn, _) in jt_list:
                seq_banks += [off // 512 for off, n in pieces(cl, cn)]
            if do_null:
                seq_banks += list(range(n_halves))
            n_out_mms = len(seq_banks)
            last_of_bank = {}
            for i_b, b in enumerate(seq_banks):
                last_of_bank[b] = i_b
            oi = 0
            out_last = None
            touched = set()
            for idx, (jt, col_lo, col_n, mask_kind) in enumerate(jt_list):
                psim = pC if idx % 2 == 0 else pD
                psim_free = st.pC_free if idx % 2 == 0 else st.pD_free
                eb = ebuf[idx % 2]
                mm = None
                pcs = pieces(col_lo, col_n)
                for pi, (off, n) in enumerate(pcs):
                    mm = em(T, lambda e, psim=psim, jt=jt, off=off, n=n, ic_off=ic_off: e.matmul(
                        psim[:, off:off + n], kbuf_ap[:, jt * 128:(jt + 1) * 128],
                        qbuf_ap[:, ic_off + off:ic_off + off + n], start=True, stop=True),
                        waits=[psim_free, gate_tok, t_psz], inc=(pi == len(pcs) - 1))
                bias_ap = cbias[:, 0:1] if exp_bias != 0.0 else 0.0
                t_exp = em(S, lambda e, psim=psim, eb=eb, col_lo=col_lo, col_n=col_n,
                           bias_ap=bias_ap: e.activation(
                    eb[:, col_lo:col_lo + col_n], psim[:, col_lo:col_lo + col_n], AF.Exp,
                    bias=bias_ap), waits=[mm, st.eb_free[idx % 2]], inc="ss")
                if idx % 2 == 0:
                    st.pC_free = t_exp
                else:
                    st.pD_free = t_exp
                t_mask = t_exp
                if mask_kind is not None:
                    mt = m0_t if mask_kind == "m0" else mleft_t
                    t_mld = t_m0 if mask_kind == "m0" else t_mleft
                    mlo = col_lo if mask_kind == "m0" else col_lo + col_n - 128
                    t_mask = em(V, lambda e, eb=eb, mlo=mlo, mt=mt: e.tensor_tensor(
                        eb[:, mlo:mlo + 128], eb[:, mlo:mlo + 128], mt[:, :], ALU.mult),
                        waits=[t_exp, t_mld], inc=True)
                for off, n in pcs:
                    bank = off // 512
                    fo = bank not in touched
                    touched.add(bank)
                    out_last = em(T, lambda e, jt=jt, eb=eb, off=off, n=n,
                                  fo=fo, lo=(last_of_bank[bank] == oi): e.matmul(
                        pE[0:prow, off:off + n], vaug[:, jt, parity, 0:prow],
                        eb[:, off:off + n], start=fo, stop=lo),
                        waits=[t_mask, st.pE_free if oi == 0 else None], inc=True)
                    oi += 1
                st.eb_free[idx % 2] = out_last
            if do_null:
                mmn = []
                for half in range(n_halves):
                    mmn.append(em(T, lambda e, half=half, ic_off=ic_off: e.matmul(
                        pA[half * 32:half * 32 + 2, 0:512], kbuf_ap[:, 2048:2050],
                        qbuf_ap[:, ic_off + half * 512:ic_off + (half + 1) * 512],
                        start=True, stop=True), waits=[st.pA_free, t_psz, gate_tok], inc=True))
                t_en = None
                for half in range(n_halves):
                    nbias = cbias[:, 0:1] if exp_bias != 0.0 else 0.0
                    t_en = em(S, lambda e, half=half, nbias=nbias: e.activation(
                        enull[:, half * 512:(half + 1) * 512],
                        pA[half * 32:half * 32 + 2, 0:512], AF.Exp, bias=nbias),
                        waits=[mmn[-1]], inc="ss")
                st.pA_free = t_en
                for half in range(n_halves):
                    fo = half not in touched
                    touched.add(half)
                    out_last = em(T, lambda e, half=half, fo=fo,
                                  lo=(last_of_bank[half] == oi): e.matmul(
                        pE[0:prow, half * 512:(half + 1) * 512], vnaug[:, parity, 0:prow],
                        enull[:, half * 512:(half + 1) * 512], start=fo, stop=lo),
                        waits=[t_en, st.pE_free if oi == 0 else None], inc=True)
                    oi += 1
            st.vaug_free = out_last
            st.qk_free = out_last
            t_rc = em(V, lambda e, ic_n=ic_n, lrow=lrow: e.reciprocal(
                rowf[0:1, 0:ic_n], pE[lrow:lrow + 1, 0:ic_n]), waits=[out_last], inc=True)
            t_rb = em(V, lambda e, ic_n=ic_n: e.tensor_copy(rowb[0:1, 0:ic_n], rowf[0:1, 0:ic_n]),
                      waits=[t_rc], inc=True)
            for half in range(n_halves):
                n = min(512, ic_n - half * 512)
                t_bc = em(T, lambda e, half=half, n=n: e.matmul(
                    pA[:, 0:n], ones_r[0:1, 0:128], rowb[0:1, half * 512:half * 512 + n],
                    start=True, stop=True), waits=[t_rb, st.pA_free, t_ones], inc=True)
                t_rbt = em(V, lambda e, half=half, n=n: e.tensor_copy(
                    rbt[:, half * 512:half * 512 + n], pA[:, 0:n]), waits=[t_bc], inc=True)
                st.pA_free = t_rbt
            ot_ap = ot_dst_fn(ici)
            if DBG.get("trace_evac"):
                print("EVAC ici", ici, "ot offset", ot_ap.offset, "pA_free", st.pA_free,
                      "next sv", mk.cnt["sv"] + 1)
            if DBG.get("skip_evac_ici") == ici:
                t_ot = em(V, lambda e: e.memset(stage_f[0:1, 0:4], 0.0), waits=[st.pA_free], inc=True)
            else:
                t_ot = em(V, lambda e, ot_ap=ot_ap, ic_n=ic_n: e.tensor_tensor(
                    ot_ap, pE[orows[0]:orows[1], 0:ic_n], rbt[orows[0]:orows[1], 0:ic_n], ALU.mult),
                    waits=[st.pA_free], inc=True)
            st.pE_free = t_ot
            if DBG.get("chunk_barrier"):
                mk.wait_only(T, [t_ot])
                mk.wait_only(S, [t_ot])

    def proj_tok_collect(rhs_ap, ncols, evac, gate=(), lhs_buf=None):
        if lhs_buf is None:
            lhs_buf = xt
        toks = []
        for it in range(16):
            use_pC = (it % 2 == 0)
            ps = pC if use_pC else pD
            ps_free = st.pC_free if use_pC else st.pD_free
            mm = None
            for c in range(8):
                off, left = 0, ncols
                while left > 0:
                    n = min(512, left)
                    mm = em(T, lambda e, ps=ps, c=c, it=it, off=off, n=n, lhs_buf=lhs_buf: e.matmul(
                        ps[:, off:off + n], lhs_buf[:, c, it * 128:(it + 1) * 128],
                        rhs_ap(c)[:, off:off + n], start=(c == 0), stop=(c == 7)),
                        waits=[ps_free, t_psz] + list(gate), inc=(c == 7 and left == n))
                    off += n
                    left -= n
            if lhs_buf is xt:
                st.xt_free = mm
            st.wres_free = mm
            t_e = evac(it, ps, mm)
            toks.append(t_e)
            if use_pC:
                st.pC_free = t_e
            else:
                st.pD_free = t_e
        return toks

    def run_stage1(seq, rtoks):
        t_arena = init_attn_arena()
        if len(tuple(DBG.get("hps", range(8)))) < 8 or DBG.get("memset_ot"):
            em(V, lambda e: e.memset(ot[:, :, :], 0.0), waits=[st.pE_free])
        if DBG.get("no_xt"):
            t_xt = None
        else:
            t_xt = build_xt(rtoks, shift=False)
        t_bo = bcast_row(bc1[:, :], W["l0_bo"], waits=[st.bc1_free])
        if DBG.get("no_wres"):
            t_wo = em(V, lambda e: e.memset(wres[:, :, :], 0.0), waits=[st.wres_free], inc=True)
        else:
            t_wo = build_wresT(lambda rc: W["l0_wo"][rc * 128:(rc + 1) * 128, :])
        wt_last = None
        for hp in DBG.get("hps", range(8)):
            for wname, wbuf in (() if hp in DBG.get("wt_off_hps", ()) else (("l0_wq", wt1), ("l0_wk", wt2), ("l0_wv", wt3))):
                t_l = dma(G, wld[:, 0:1024], W[wname][hp * 128:(hp + 1) * 128, :],
                          waits=[wt_last, st.wld_free, t_wo if hp == 0 else None])
                srcs = [wld[:, c * 128:(c + 1) * 128] for c in range(8)]

                def w0(e, wbuf=wbuf):
                    return e.tensor_copy(wbuf[:, :, :],
                                         pF[:, 0:1024].rearrange("p (c n) -> p c n", n=128))
                wt_last = trans_group(srcs, [w0], waits=[t_l, st.qk_free])
                st.wld_free = wt_last
            t_wt = wt_last
            prj_last = None
            for which, wbuf in (() if hp in DBG.get("proj_off_hps", ()) else (("q", wt1), ("k", wt2), ("v", wt3))):
                for nchunk in range(4):
                    use_pC = (nchunk % 2 == 0)
                    ps = pC if use_pC else pD
                    ps_free = st.pC_free if use_pC else st.pD_free
                    mm = None
                    for c in range(8):
                        mm = em(T, lambda e, ps=ps, wbuf=wbuf, c=c, nchunk=nchunk: e.matmul(
                            ps[:, 0:512], wbuf[:, c, :], xt[:, c, nchunk * 512:(nchunk + 1) * 512],
                            start=(c == 0), stop=(c == 7)),
                            waits=[t_wt, t_xt, ps_free, t_psz,
                                   st.qk_free if c == 0 else None], inc=(c == 7))
                    st.xt_free = mm
                    if which == "q":
                        t_e = em(S, lambda e, ps=ps, nchunk=nchunk: e.activation(
                            qt[0:128, nchunk * 512:(nchunk + 1) * 512], ps[:, 0:512], AF.Copy,
                            scale=0.125), waits=[mm, t_arena[0]], inc="ss")
                    elif which == "k":
                        t_e = em(S, lambda e, ps=ps, nchunk=nchunk: e.copy(
                            kt[0:128, nchunk * 512:(nchunk + 1) * 512], ps[:, 0:512]),
                            waits=[mm], inc="ss")
                    else:
                        t_e = em(S, lambda e, ps=ps: e.copy(stage_b2[:, 0:512], ps[:, 0:512]),
                                 waits=[mm, prj_last], inc="ss")
                        it0 = nchunk * 4
                        srcs = [stage_b2[:, k * 128:(k + 1) * 128] for k in range(4)]

                        def w0(e, it0=it0):
                            return e.tensor_copy(
                                vaug[:, it0:it0 + 4, 0, 0:64],
                                pF[:, 0:512].rearrange("p (c n) -> p c n", n=128)[:, :, 0:64])

                        def w1(e, it0=it0):
                            return e.tensor_copy(
                                vaug[:, it0:it0 + 4, 1, 64:128],
                                pF[:, 0:512].rearrange("p (c n) -> p c n", n=128)[:, :, 64:128])
                        t_e = trans_group(srcs, [w0, w1],
                                          waits=[t_e, t_arena[1], st.vaug_free])
                    if use_pC:
                        st.pC_free = t_e
                    else:
                        st.pD_free = t_e
                    prj_last = t_e
            for h in (DBG.get("hs", range(2)) if hp not in DBG.get("attn_off_hps", ()) else ()):
                jtl = [(jt, 0, 1024, None) for jt in DBG.get("jts", range(16))]
                chunks = [(0, 1024, jtl, False), (1024, 1024, jtl, False)]
                chunks = [chunks[i] for i in DBG.get("ics", range(2))]

                def ot_dst(ici, hp=hp, h=h):
                    r0 = 0 if h == 0 else 64
                    return ot[r0:r0 + 64, hp, ici * 1024:(ici + 1) * 1024]
                attn_head(kbuf_ap=kt[64 * h:64 * h + 64, 0:2048],
                          qbuf_ap=qt[64 * h:64 * h + 64, 0:2048],
                          chunks=chunks, parity=h, ot_dst_fn=ot_dst, gate_tok=prj_last)
        t_ot_all = st.pE_free

        def evac_z(it, ps, mm):
            t_s = em(V, lambda e, ps=ps: e.tensor_tensor(stage_f[:, :], ps[:, 0:1024], bc1[:, :],
                                                         ALU.add), waits=[mm, t_bo])
            return em(V, lambda e, it=it: e.tensor_tensor(resid[:, it, :], resid[:, it, :],
                                                          stage_f[:, :], ALU.add),
                      waits=[t_s], inc=True)
        ztoks = proj_tok_collect(lambda c: wres[:, c, 0:1024], 1024, evac_z,
                                 gate=[t_wo, t_ot_all], lhs_buf=ot)
        if DBG.get("no_ln"):
            return ztoks
        return ln_apply_inplace(ln_stats(ztoks))

    def run_stage2(seq, rtoks):
        t_ws = bcast_row(bc2[:, :], W["f0_ws"], waits=[st.vaug_free, rtoks[0]])
        t_bi = bcast_row(bc3[:, :], W["f0_b"], waits=[rtoks[0]])
        t_wq = build_wresT(lambda rc: W["f0_wq"][rc * 128:(rc + 1) * 128, :])
        ztoks = []
        for it in range(16):
            w = [rtoks[it]]
            t_am = em(V, lambda e, it=it: e.tensor_reduce(statw[:, it:it + 1], resid[:, it, :],
                                                          AX.X, ALU.max,
                                                          apply_absolute_value=True),
                      waits=w, inc=True)
            t_ts = em(V, lambda e, it=it: e.tensor_scalar(statw[:, it:it + 1],
                                                          statw[:, it:it + 1],
                                                          1e-6, 1.0 / 127.0, ALU.max, ALU.mult),
                      waits=[t_am])
            t_r = em(V, lambda e, it=it: e.reciprocal(statw[:, 16 + it:17 + it],
                                                      statw[:, it:it + 1]),
                     waits=[t_ts], inc=True)
            t_m = em(V, lambda e, it=it: e.tensor_scalar(stage_f[:, :], resid[:, it, :],
                                                         statw[:, 16 + it:17 + it], None,
                                                         ALU.mult), waits=[t_r], inc=True)
            t_rnd = em(S, lambda e: e.activation(stage_f[:, :], stage_f[:, :], AF.Copy, bias=RND),
                       waits=[t_m], inc=True)
            t_rnd = em(S, lambda e: e.activation(stage_b[:, :], stage_f[:, :], AF.Copy, bias=-RND),
                       waits=[t_rnd], inc=True)
            srcs = [stage_b[:, c * 128:(c + 1) * 128] for c in range(8)]

            def wxq(e, it=it):
                return e.tensor_copy(xt[:, 0:8, it * 128:(it + 1) * 128],
                                     pF[:, 0:1024].rearrange("p (c n) -> p c n", n=128))
            t_tr = trans_group(srcs, [wxq], waits=[t_rnd, st.xt_free])
            use_pC = (it % 2 == 0)
            ps = pC if use_pC else pD
            ps_free = st.pC_free if use_pC else st.pD_free
            mm = None
            for c in range(8):
                for half in range(2):
                    mm = em(T, lambda e, ps=ps, c=c, it=it, half=half: e.matmul(
                        ps[:, half * 512:(half + 1) * 512], xt[:, c, it * 128:(it + 1) * 128],
                        wres[:, c, half * 512:(half + 1) * 512], start=(c == 0), stop=(c == 7)),
                        waits=[t_tr, t_wq, ps_free, t_psz], inc=(c == 7 and half == 1))
            st.xt_free = mm
            st.wres_free = mm
            t_dq = em(V, lambda e, ps=ps, it=it: e.scalar_tensor_tensor(
                stage_f[:, :], ps[:, 0:1024], statw[:, it:it + 1], bc2[:, :], ALU.mult, ALU.mult),
                waits=[mm, t_ws, t_bi])
            t_b3 = em(V, lambda e: e.tensor_tensor(stage_f[:, :], stage_f[:, :], bc3[:, :],
                                                   ALU.add), waits=[t_dq])
            t_e = em(V, lambda e, it=it: e.tensor_tensor(resid[:, it, :], resid[:, it, :],
                                                         stage_f[:, :], ALU.add),
                     waits=[t_b3], inc=True)
            ztoks.append(t_e)
            if use_pC:
                st.pC_free = t_e
            else:
                st.pD_free = t_e
        return ln_apply_inplace(ln_stats(ztoks))

    wldv = wld[:, 0:3072].rearrange("p (c n) -> p c n", n=384)

    def qkv_pair(wq_ap, wk_ap, wv_ap, t_xt, t_arena, first_pair, t_qsb, t_ksb):
        t_l1 = dma(G, wldv[:, :, 0:128], wq_ap.rearrange("(c p) n -> p c n", p=128),
                   waits=[st.qk_free, st.wld_free, st.xt_free if first_pair else None])
        t_l2 = dma(G, wldv[:, :, 128:256], wk_ap.rearrange("(c p) n -> p c n", p=128))
        t_l3 = dma(G, wldv[:, :, 256:384], wv_ap.rearrange("(c p) n -> p c n", p=128),
                   waits=[st.vaug_free])
        prj_last = None
        for it in range(16):
            use_pC = (it % 2 == 0)
            ps = pC if use_pC else pD
            ps_free = st.pC_free if use_pC else st.pD_free
            mm = None
            for c in range(8):
                mm = em(T, lambda e, ps=ps, c=c, it=it: e.matmul(
                    ps[:, 0:384], xt[:, c, it * 128:(it + 1) * 128], wldv[:, c, :],
                    start=(c == 0), stop=(c == 7)),
                    waits=[t_l1, t_l2, t_l3, t_xt, ps_free, t_psz], inc=(c == 7))
            st.xt_free = mm
            st.wld_free = mm
            em(V, lambda e, ps=ps, it=it: e.tensor_copy(qtok[:, it, :], ps[:, 0:128]),
               waits=[mm])
            em(V, lambda e, ps=ps, it=it: e.tensor_copy(ktok[:, it, :], ps[:, 128:256]))
            em(V, lambda e, ps=ps, it=it: e.tensor_copy(vaug[:, it, 0, 0:64], ps[:, 256:320]),
               waits=[t_arena[1], st.vaug_free if it == 0 else None])
            t_v = em(V, lambda e, ps=ps, it=it: e.tensor_copy(vaug[:, it, 1, 64:128],
                                                              ps[:, 320:384]), inc=True)
            t_e = t_v
            if use_pC:
                st.pC_free = t_e
            else:
                st.pD_free = t_e
            prj_last = t_e
        t_red2 = None
        for which, tokb, col0 in (("q", qtok, 0), ("k", ktok, 2)):
            for it in range(16):
                t_sb = em(V, lambda e, tokb=tokb, it=it: e.tensor_tensor(
                    stage_b[:, 0:128], tokb[:, it, :], tokb[:, it, :], ALU.mult),
                    waits=[prj_last])
                t_red2 = em(V, lambda e, it=it, col0=col0: e.tensor_reduce(
                    sqb[:, it, col0:col0 + 2],
                    stage_b[:, 0:128].rearrange("p (s c) -> p s c", c=64), AX.X, ALU.add),
                    waits=[t_sb], inc=True)
        sqf = sqb[:, :, :].rearrange("p a b -> p (a b)")
        t_sq2 = em(S, lambda e: e.activation(sqf, sqf, AF.Sqrt), waits=[t_red2], inc=True)
        t_mx = em(V, lambda e: e.tensor_scalar(sqf, sqf, 1e-12, None, ALU.max),
                  waits=[t_sq2])
        t_rcp = em(V, lambda e: e.reciprocal(sqf, sqf), waits=[t_mx], inc=True)
        t_nq = {}
        for which, tokb, col0, scb in (("q", qtok, 0, qs_bc), ("k", ktok, 2, ks_bc)):
            lastn = None
            for it in range(16):
                t_n1 = em(V, lambda e, tokb=tokb, it=it, col0=col0: e.tensor_tensor(
                    tokb[:, it, :].rearrange("p (s c) -> p s c", c=64),
                    tokb[:, it, :].rearrange("p (s c) -> p s c", c=64),
                    sqb[:, it, col0:col0 + 2].rearrange("p (s c) -> p s c", c=1)
                    .broadcast_to([128, 2, 64]), ALU.mult), waits=[t_rcp])
                lastn = em(V, lambda e, tokb=tokb, it=it, scb=scb: e.tensor_tensor(
                    tokb[:, it, :].rearrange("p (s c) -> p s c", c=64),
                    tokb[:, it, :].rearrange("p (s c) -> p s c", c=64),
                    scb[:, 0:64].rearrange("p (a c) -> p a c", a=1).broadcast_to([128, 2, 64]),
                    ALU.mult), waits=[t_qsb, t_ksb, t_n1], inc=True)
            t_nq[which] = lastn
        st.scb_free = t_nq["k"]
        return prj_last, t_nq

    def run_stage3(seq, rtoks):
        t_arena = init_attn_arena()
        t_g = bcast_row(bc1[:, :], W["l1_g"], waits=[rtoks[0]])
        t_qsb = bcast_row(qs_bc[:, :], W["l1_qs"])
        t_ksb = bcast_row(ks_bc[:, :], W["l1_ks"])
        t_rs = ln_stats(rtoks)

        def pre_ln(it):
            t_p = em(V, lambda e, it=it: e.tensor_scalar(stage_f[:, :], resid[:, it, :],
                                                         stats[:, 2, it:it + 1],
                                                         stats[:, 3, it:it + 1],
                                                         ALU.subtract, ALU.mult),
                     waits=[t_rs, t_g])
            return em(V, lambda e: e.tensor_tensor(stage_b[:, :], stage_f[:, :], bc1[:, :],
                                                   ALU.mult), waits=[t_p], inc=True)
        t_xt = build_xt(None, shift=False, pre=pre_ln)

        t_nk = dma(G, nkst[0:32, 0:64],
                   W["l1_nkv"].rearrange("h (j c) d -> (h j) c d", c=2)[:, 0, :])
        t_nacc = em(V, lambda e: e.scalar_tensor_tensor(stage_f[0:32, 0:64], nkst[0:32, :], 1.0,
                                                        nkst[0:32, :], ALU.mult, ALU.mult,
                                                        accum_out=statw[0:32, 0:1]),
                    waits=[t_nk], inc=True)
        t_sqn = em(S, lambda e: e.activation(statw[0:32, 1:2], statw[0:32, 0:1], AF.Sqrt),
                   waits=[t_nacc], inc=True)
        t_nm = em(V, lambda e: e.tensor_scalar(statw[0:32, 1:2], statw[0:32, 1:2], 1e-12, None,
                                               ALU.max), waits=[t_sqn])
        t_nr = em(V, lambda e: e.reciprocal(statw[0:32, 2:3], statw[0:32, 1:2]), waits=[t_nm])
        t_ns = em(V, lambda e: e.tensor_scalar(stage_f[0:32, 0:64], nkst[0:32, :],
                                               statw[0:32, 2:3], None, ALU.mult), waits=[t_nr])
        t_khn = em(V, lambda e: e.tensor_tensor(stage_b2[0:32, 0:64], stage_f[0:32, 0:64],
                                                ks_bc[0:32, :], ALU.mult),
                   waits=[t_ksb, t_ns], inc=True)
        def wknt(e):
            return e.tensor_copy(knt[0:64, 0:32], pF[0:64, 0:32])
        t_knt = trans_group([stage_b2[0:32, 0:64]], [wknt], waits=[t_khn], n_out=64)

        for pair in range(8):
            prj_last, t_nq = qkv_pair(
                W["l1_wq"][:, pair * 128:(pair + 1) * 128],
                W["l1_wkv"][:, pair * 128:(pair + 1) * 128],
                W["l1_wkv"][:, 1024 + pair * 128:1024 + (pair + 1) * 128],
                t_xt, t_arena, pair == 0, t_qsb, t_ksb)
            for hh in range(2):
                h = pair * 2 + hh
                parity = hh
                t_al = dma(SY, kt[64:69, 0:2050], W["c_alibi"][h], waits=[st.qk_free])
                t_ci = dma(SY, qt[68:69, 0:2048], W["c_ci"][h].unsqueeze(0))
                t_kn = em(V, lambda e, h=h: e.tensor_copy(kt[0:64, 2048:2050],
                                                          knt[0:64, 2 * h:2 * h + 2]),
                          waits=[t_knt, st.qk_free], inc=True)
                tq_last = None
                for g8 in range(2):
                    srcs = [qtok[:, g8 * 8 + i, hh * 64:hh * 64 + 64] for i in range(8)]

                    def wq8(e, g8=g8):
                        return e.tensor_scalar(
                            qt[0:64, g8 * 1024:(g8 + 1) * 1024], pF[0:64, 0:1024],
                            8.0, None, ALU.mult)
                    tq_last = trans_group(srcs, [wq8], waits=[t_nq["q"], st.qk_free], n_out=64)
                    srcs = [ktok[:, g8 * 8 + i, hh * 64:hh * 64 + 64] for i in range(8)]

                    def wk8(e, g8=g8):
                        return e.tensor_copy(kt[0:64, g8 * 1024:(g8 + 1) * 1024], pF[0:64, 0:1024])
                    tq_last = trans_group(srcs, [wk8], waits=[t_nq["k"], st.qk_free], n_out=64)
                t_vn = dma(G, vnaug[:, parity, (0 if parity == 0 else 64):
                                    (64 if parity == 0 else 128)],
                           W["l1_nkv"].rearrange("h (j c) d -> c j h d", c=2)[1][:, h, :],
                           waits=[t_vn0, st.vaug_free])
                mk.wait_only(T, [t_al, t_ci, t_kn, t_vn, t_arena[0]])
                slope = ALIBI16[h]
                chunks = []
                for ic in range(2):
                    i_lo, i_hi = ic * 1024, ic * 1024 + 1023
                    jtl = []
                    for jt in range(16):
                        if 128 * jt > i_hi:
                            continue
                        if slope * (i_lo - (128 * jt + 127)) > SKIP:
                            continue
                        col_lo = max(0, 128 * jt - i_lo)
                        mask = "m0" if 128 * jt >= i_lo else None
                        jtl.append((jt, col_lo, 1024 - col_lo, mask))
                    do_null = slope * (i_lo + 2) <= SKIP
                    chunks.append((ic * 1024, 1024, jtl, do_null))

                def ot_dst(ici, h=h, parity=parity):
                    r0 = 0 if parity == 0 else 64
                    return ot[r0:r0 + 64, h // 2, ici * 1024:(ici + 1) * 1024]
                attn_head(kbuf_ap=kt[0:69, 0:2080], qbuf_ap=qt[0:69, 0:2048],
                          chunks=chunks, parity=parity, ot_dst_fn=ot_dst, gate_tok=tq_last)

        t_ot_all = st.pE_free
        wo_dma = {}
        for c in range(8):
            wo_dma[c] = dma(G, wres[:, c, 0:1024], W["l1_wo"][c * 128:(c + 1) * 128, :],
                            waits=[t_ot_all if c == 0 else None])

        def evac_z3(it, ps, mm):
            return em(V, lambda e, ps=ps, it=it: e.tensor_tensor(
                resid[:, it, :], resid[:, it, :], ps[:, 0:1024], ALU.add), waits=[mm], inc=True)

        ztoks = []
        for it in range(16):
            use_pC = (it % 2 == 0)
            ps = pC if use_pC else pD
            ps_free = st.pC_free if use_pC else st.pD_free
            mm = None
            for c in range(8):
                for half in range(2):
                    mm = em(T, lambda e, ps=ps, c=c, it=it, half=half: e.matmul(
                        ps[:, half * 512:(half + 1) * 512], ot[:, c, it * 128:(it + 1) * 128],
                        wres[:, c, half * 512:(half + 1) * 512], start=(c == 0), stop=(c == 7)),
                        waits=[ps_free, t_psz, wo_dma[c], t_ot_all],
                        inc=(c == 7 and half == 1))
            st.wres_free = mm
            t_e = evac_z3(it, ps, mm)
            ztoks.append(t_e)
            if use_pC:
                st.pC_free = t_e
            else:
                st.pD_free = t_e
        out_toks = ln_apply_inplace(ln_stats(ztoks))
        st.spill_toks = []
        if 4 in stages:
            for q in range(4):
                w = [out_toks[i] for i in range(4 * q, 4 * q + 4)]
                t_sp = dma(SY, xout_sp.rearrange("(t p) d -> p t d", p=128)[:, 4 * q:4 * q + 4, :],
                           resid[:, 4 * q:4 * q + 4, :], waits=w)
                st.spill_toks.append(t_sp)
        return out_toks

    def s4_rotary(t_nq):
        toks = {}
        for which, tokb, tc, ts_ in (("q", qtok, 0, 1), ("k", ktok, 2, 3)):
            last = None
            for g8 in range(2):
                tv = tokb[:, g8 * 8:(g8 + 1) * 8, :].rearrange("p t (s c) -> p t s c", c=64)
                sv = stage_b[:, :].rearrange("p (t s c) -> p t s c", s=2, c=64)

                def rv(tab, g8=g8):
                    return rotc[:, tab, g8 * 8:(g8 + 1) * 8, :] \
                        .rearrange("p t (o c) -> p t o c", o=1).broadcast_to([128, 8, 2, 64])
                t_a = em(V, lambda e, tv=tv, sv=sv: e.tensor_scalar(
                    sv[:, :, :, 0:32], tv[:, :, :, 32:64], -1.0, None, ALU.mult),
                    waits=[t_nq[which], st.stageb_free])
                t_b = em(V, lambda e, tv=tv, sv=sv: e.tensor_copy(
                    sv[:, :, :, 32:64], tv[:, :, :, 0:32]))
                rvs = rv(ts_)
                rvc = rv(tc)
                t_c = em(V, lambda e, sv=sv, rvs=rvs: e.tensor_tensor(
                    sv[:, :, :, :], sv[:, :, :, :], rvs, ALU.mult),
                    waits=[t_a, t_b, t_rot])
                t_d = em(V, lambda e, tv=tv, rvc=rvc: e.tensor_tensor(
                    tv[:, :, :, :], tv[:, :, :, :], rvc, ALU.mult), waits=[t_rot])
                last = em(V, lambda e, tv=tv, sv=sv: e.tensor_tensor(
                    tv[:, :, :, :], tv[:, :, :, :], sv[:, :, :, :], ALU.add),
                    waits=[t_c, t_d], inc=True)
                st.stageb_free = last
            toks[which] = last
        return toks

    def s4_attention(seq, l, rtoks):
        t_arena = init_attn_arena()
        t_qsb = bcast_row(qs_bc[:, :], W["f1_qs"][l], waits=[st.scb_free])
        t_ksb = bcast_row(ks_bc[:, :], W["f1_ks"][l])
        t_xt = build_xt(rtoks, shift=True)
        for pair in range(4):
            _, t_nq = qkv_pair(
                W["f1_wqkv"][l][:, pair * 128:(pair + 1) * 128],
                W["f1_wqkv"][l][:, 512 + pair * 128:512 + (pair + 1) * 128],
                W["f1_wqkv"][l][:, 1024 + pair * 128:1024 + (pair + 1) * 128],
                t_xt, t_arena, pair == 0, t_qsb, t_ksb)
            t_rq = s4_rotary(t_nq)
            for hh in range(2):
                tq_last = None
                for g8 in range(2):
                    srcs = [qtok[:, g8 * 8 + i, hh * 64:hh * 64 + 64] for i in range(8)]

                    def wq4(e, g8=g8):
                        return e.tensor_copy(qt[0:64, g8 * 1024:(g8 + 1) * 1024],
                                             pF[0:64, 0:1024])
                    tq_last = trans_group(srcs, [wq4], waits=[t_rq["q"], st.qk_free], n_out=64)
                    srcs = [ktok[:, g8 * 8 + i, hh * 64:hh * 64 + 64] for i in range(8)]

                    def wk4(e, g8=g8):
                        return e.tensor_copy(kt[0:64, g8 * 1024:(g8 + 1) * 1024],
                                             pF[0:64, 0:1024])
                    tq_last = trans_group(srcs, [wk4], waits=[t_rq["k"], st.qk_free], n_out=64)
                chunks = []
                for w in range(4):
                    jtl = []
                    for jb in [4 * w] + [j for j in range(max(0, 4 * w - 4), 4 * w + 4)
                                         if j != 4 * w]:
                        col_lo = max(0, 128 * jb - 512 * w)
                        col_hi = min(512, 128 * jb + 640 - 512 * w)
                        mask = "mleft" if jb < 4 * w else "m0"
                        jtl.append((jb, col_lo, col_hi - col_lo, mask))
                    chunks.append((512 * w, 512, jtl, False))

                def ot_dst(ici, pair=pair, hh=hh):
                    r0 = 0 if hh == 0 else 64
                    return ot[r0:r0 + 64, pair, ici * 512:(ici + 1) * 512]
                attn_head(kbuf_ap=kt[0:64, 0:2048], qbuf_ap=qt[0:64, 0:2048],
                          chunks=chunks, parity=hh, ot_dst_fn=ot_dst, gate_tok=tq_last,
                          exp_bias=-12.0)
        t_ot_all = st.pE_free
        wo_dma = {}
        for c in range(4):
            wo_dma[c] = dma(G, wres[:, c, 0:1024], W["f1_wo"][l][c * 128:(c + 1) * 128, :],
                            waits=[t_ot_all if c == 0 else None,
                                   st.wres_free if c == 0 else None])
        ztoks = []
        for it in range(16):
            use_pC = (it % 2 == 0)
            ps = pC if use_pC else pD
            ps_free = st.pC_free if use_pC else st.pD_free
            mm = None
            for c in range(4):
                for half in range(2):
                    mm = em(T, lambda e, ps=ps, c=c, it=it, half=half: e.matmul(
                        ps[:, half * 512:(half + 1) * 512], ot[:, c, it * 128:(it + 1) * 128],
                        wres[:, c, half * 512:(half + 1) * 512], start=(c == 0), stop=(c == 3)),
                        waits=[ps_free, t_psz, wo_dma[c], t_ot_all],
                        inc=(c == 3 and half == 1))
            st.wres_free = mm
            t_e = em(V, lambda e, ps=ps, it=it: e.tensor_tensor(
                resid[:, it, :], resid[:, it, :], ps[:, 0:1024], ALU.add),
                waits=[mm, st.spill_toks[it // 4] if l == 0 else None], inc=True)
            ztoks.append(t_e)
            if use_pC:
                st.pC_free = t_e
            else:
                st.pD_free = t_e
        return ztoks

    def load_colvec(dst, src1d, r, waits=()):
        t_l = dma(G, stage_b2[0:r, 0:128], src1d.rearrange("(r p) -> r p", r=r),
                  waits=list(waits) + [st.stageb2_free])

        def wr(e, dst=dst, r=r):
            return e.tensor_copy(dst[:, 0:r], pF[0:128, 0:r])
        t = trans_group([stage_b2[0:r, 0:128]], [wr], waits=[t_l])
        st.stageb2_free = t
        return t

    def s4_ffn(seq, l, rtoks):
        t_xt = build_xt(rtoks, shift=True)
        t_gc = load_colvec(g_c, W["f1_lng"][l], 8)
        t_bc = load_colvec(b_c, W["f1_lnb"][l], 8)
        t_b1a = load_colvec(b1a_c, W["f1_b1a"][l], 22)
        t_b1g = load_colvec(b1g_c, W["f1_b1g"][l], 22)
        t_b2 = bcast_row(bc1[:, :], W["f1_b2"][l], waits=[st.bc1_free, rtoks[0]])
        t_app = None
        for hf in range(2):
            mm_s = mm_q = None
            for w2 in range(2):
                for c in range(8):
                    mm_s = em(T, lambda e, w2=w2, c=c, hf=hf: e.matmul(
                        pC[0:1, w2 * 512:(w2 + 1) * 512], ones_c[:, 0:1],
                        xt[:, c, hf * 1024 + w2 * 512:hf * 1024 + (w2 + 1) * 512],
                        start=(c == 0), stop=(c == 7)),
                        waits=[t_xt, t_ones, st.pC_free if w2 == 0 and c == 0 else None,
                               t_psz], inc=(c == 7))
            for w2 in range(2):
                for c in range(8):
                    t_sq = em(S, lambda e, w2=w2, c=c, hf=hf: e.activation(
                        stage_b2[:, (c % 2) * 512:(c % 2 + 1) * 512],
                        xt[:, c, hf * 1024 + w2 * 512:hf * 1024 + (w2 + 1) * 512],
                        AF.Square), waits=[t_xt, st.stageb2_free], inc="ss")
                    mm_q = em(T, lambda e, w2=w2, c=c: e.matmul(
                        pD[0:1, w2 * 512:(w2 + 1) * 512], ones_c[:, 0:1],
                        stage_b2[:, (c % 2) * 512:(c % 2 + 1) * 512],
                        start=(c == 0), stop=(c == 7)),
                        waits=[t_sq, st.pD_free if w2 == 0 and c == 0 else None, t_psz],
                        inc=True)
                    st.stageb2_free = mm_q
            t_mu = em(V, lambda e: e.tensor_scalar(rowf[0:1, 0:1024], pC[0:1, 0:1024],
                                                   1.0 / 1024.0, None, ALU.mult),
                      waits=[mm_s, st.rowf_free])
            t_ss = em(V, lambda e: e.tensor_copy(stage_f[0:1, 0:1024], pD[0:1, 0:1024]),
                      waits=[mm_q, st.stagef_free])
            st.pC_free = t_mu
            st.pD_free = t_ss
            t_m2 = em(V, lambda e: e.tensor_tensor(rowb[0:1, 0:1024], rowf[0:1, 0:1024],
                                                   rowf[0:1, 0:1024], ALU.mult),
                      waits=[t_mu])
            t_va = em(V, lambda e: e.scalar_tensor_tensor(
                stage_f[0:1, 0:1024], stage_f[0:1, 0:1024], 1.0 / 1024.0, rowb[0:1, 0:1024],
                ALU.mult, ALU.subtract), waits=[t_ss, t_m2])
            t_ve = em(V, lambda e: e.tensor_scalar(stage_f[0:1, 0:1024], stage_f[0:1, 0:1024],
                                                   1e-5, None, ALU.add), waits=[t_va], inc=True)
            t_sg = em(S, lambda e: e.activation(stage_f[0:1, 0:1024], stage_f[0:1, 0:1024],
                                                AF.Sqrt), waits=[t_ve], inc="ss")
            t_rc = em(V, lambda e: e.reciprocal(stage_f[0:1, 0:1024], stage_f[0:1, 0:1024]),
                      waits=[t_sg])
            t_mb = em(V, lambda e: e.tensor_copy(rowb[0:1, 0:1024], rowf[0:1, 0:1024]),
                      waits=[t_m2], inc=True)
            t_rb2 = em(V, lambda e: e.tensor_copy(rbt[0:1, 0:1024], stage_f[0:1, 0:1024]),
                      waits=[t_rc, st.rbt_w_free], inc=True)
            st.stagef_free = t_rb2
            st.rowf_free = t_mb
            t_bm = t_br = None
            for w2 in range(2):
                t1 = em(T, lambda e, w2=w2: e.matmul(
                    pC[:, w2 * 512:(w2 + 1) * 512], ones_r[0:1, 0:128],
                    rowb[0:1, w2 * 512:(w2 + 1) * 512], start=True, stop=True),
                    waits=[t_mb, st.pC_free, t_ones, t_psz], inc=True)
                t_bm = em(V, lambda e, w2=w2: e.tensor_copy(
                    stage_b[:, w2 * 512:(w2 + 1) * 512], pC[:, w2 * 512:(w2 + 1) * 512]),
                    waits=[t1, st.stageb_free], inc=True)
                t2 = em(T, lambda e, w2=w2: e.matmul(
                    pD[:, w2 * 512:(w2 + 1) * 512], ones_r[0:1, 0:128],
                    rbt[0:1, w2 * 512:(w2 + 1) * 512], start=True, stop=True),
                    waits=[t_rb2, st.pD_free, t_ones, t_psz], inc=True)
                st.rbt_free = t2
                t_br = em(V, lambda e, w2=w2: e.tensor_copy(
                    stage_b2[:, w2 * 512:(w2 + 1) * 512], pD[:, w2 * 512:(w2 + 1) * 512]),
                    waits=[t2, st.stageb2_free], inc=True)
            st.pC_free = t_bm
            st.pD_free = t_br
            for c in range(8):
                ta = em(V, lambda e, c=c, hf=hf: e.tensor_tensor(
                    xt[:, c, hf * 1024:(hf + 1) * 1024], xt[:, c, hf * 1024:(hf + 1) * 1024],
                    stage_b[:, :], ALU.subtract), waits=[t_bm, t_br])
                tb = em(V, lambda e, c=c, hf=hf: e.tensor_tensor(
                    xt[:, c, hf * 1024:(hf + 1) * 1024], xt[:, c, hf * 1024:(hf + 1) * 1024],
                    stage_b2[:, :], ALU.mult), waits=[ta])
                t_app = em(V, lambda e, c=c, hf=hf: e.tensor_scalar(
                    xt[:, c, hf * 1024:(hf + 1) * 1024], xt[:, c, hf * 1024:(hf + 1) * 1024],
                    g_c[:, c:c + 1], b_c[:, c:c + 1], ALU.mult, ALU.add),
                    waits=[tb, t_gc, t_bc], inc=True)
            st.stageb_free = t_app
            st.stageb2_free = t_app
        st.xt_free = t_app
        ztoks = {}
        w1_free = [st.wres_free, st.wres_free]
        for hgrp in range(2):
            t_fb = None
            for hl in range(11):
                hc = hgrp * 11 + hl
                wb = w1buf[hc % 2]
                t_wa = dma(G, wb[:, :, 0:128],
                           W["f1_w1a"][l][:, hc * 128:(hc + 1) * 128]
                           .rearrange("(c p) n -> p c n", p=128), waits=[w1_free[hc % 2]])
                t_wg = dma(G, wb[:, :, 128:256],
                           W["f1_w1g"][l][:, hc * 128:(hc + 1) * 128]
                           .rearrange("(c p) n -> p c n", p=128))
                for tc in range(4):
                    par = tc % 2
                    mm_g = mm_a = None
                    for c in range(8):
                        mm_g = em(T, lambda e, wb=wb, tc=tc, c=c, par=par: e.matmul(
                            pD[:, par * 512:(par + 1) * 512], wb[:, c, 128:256],
                            xt[:, c, tc * 512:(tc + 1) * 512], start=(c == 0), stop=(c == 7)),
                            waits=[t_wg, t_app, t_psz,
                                   st.pD_free if c == 0 else None], inc=(c == 7))
                    t_gel = em(S, lambda e, tc=tc, par=par, hc=hc: e.activation(
                        stage_b[:, par * 512:(par + 1) * 512], pD[:, par * 512:(par + 1) * 512],
                        AF.Gelu, bias=b1g_c[:, hc:hc + 1]),
                        waits=[mm_g, t_b1g, st.stageb_free], inc="ss")
                    st.pD_free = t_gel
                    for c in range(8):
                        mm_a = em(T, lambda e, wb=wb, tc=tc, c=c, par=par: e.matmul(
                            pC[:, par * 512:(par + 1) * 512], wb[:, c, 0:128],
                            xt[:, c, tc * 512:(tc + 1) * 512], start=(c == 0), stop=(c == 7)),
                            waits=[t_wa, t_app, t_psz,
                                   st.pC_free if c == 0 else None], inc=(c == 7))
                    t_fb = em(V, lambda e, tc=tc, par=par, hl=hl, hc=hc: e.scalar_tensor_tensor(
                        fbuf[:, hl, tc * 512:(tc + 1) * 512], pC[:, par * 512:(par + 1) * 512],
                        b1a_c[:, hc:hc + 1], stage_b[:, par * 512:(par + 1) * 512],
                        ALU.add, ALU.mult),
                        waits=[mm_a, t_gel, t_b1a,
                               st.fbuf_free if hl == 0 and tc == 0 else None], inc=True)
                    st.stageb_free = t_fb
                    st.pC_free = t_fb
                w1_free[hc % 2] = t_fb
            for wg0, wgn in ((0, 6), (6, 5)):
                w2d = {}
                for i in range(wgn):
                    hc = hgrp * 11 + wg0 + i
                    w2d[i] = dma(G, w2res[:, i, :],
                                 W["f1_w2"][l][hc * 128:(hc + 1) * 128, :],
                                 waits=[st.w2res_free if i == 0 else None])
                for t8 in range(16):
                    use_pC = (t8 % 2 == 0)
                    ps = pC if use_pC else pD
                    ps_free = st.pC_free if use_pC else st.pD_free
                    mm = None
                    for i in range(wgn):
                        for half in range(2):
                            mm = em(T, lambda e, ps=ps, i=i, t8=t8, half=half, wg0=wg0,
                                    wgn=wgn: e.matmul(
                                ps[:, half * 512:(half + 1) * 512],
                                fbuf[:, wg0 + i, t8 * 128:(t8 + 1) * 128],
                                w2res[:, i, half * 512:(half + 1) * 512],
                                start=(i == 0), stop=(i == wgn - 1)),
                                waits=[ps_free, t_psz, w2d[i], t_fb],
                                inc=(i == wgn - 1 and half == 1))
                    st.w2res_free = mm
                    st.fbuf_free = mm
                    t_e = em(V, lambda e, ps=ps, t8=t8: e.tensor_tensor(
                        resid[:, t8, :], resid[:, t8, :], ps[:, 0:1024], ALU.add),
                        waits=[mm, rtoks[t8] if hgrp == 0 and wg0 == 0 else ztoks.get(t8)],
                        inc=True)
                    ztoks[t8] = t_e
                    if use_pC:
                        st.pC_free = t_e
                    else:
                        st.pD_free = t_e
        out = []
        for t8 in range(16):
            out.append(em(V, lambda e, t8=t8: e.tensor_tensor(
                resid[:, t8, :], resid[:, t8, :], bc1[:, :], ALU.add),
                waits=[ztoks[t8], t_b2], inc=True))
        st.bc1_free = out[-1]
        st.vaug_free = st.w2res_free
        st.qk_free = st.w2res_free
        st.eb_free = [st.w2res_free, st.w2res_free]
        st.wres_free = st.w2res_free
        st.xt_free = t_fb
        return out

    def run_stage4(seq, rtoks):
        for l in (0, 1):
            ztoks = s4_attention(seq, l, rtoks)
            rtoks = s4_ffn(seq, l, ztoks)
        toks = []
        for it in range(16):
            xb = xosb if it % 2 == 0 else stage_f
            gate = toks[it - 2] if it >= 2 else (st.rbt_free if it % 2 == 0
                                                 else st.stagef_free)
            t_l = dma(SY, xb[:, :], xout_sp.rearrange("(t p) d -> p t d", p=128)[:, it, :],
                      waits=[st.spill_toks[it // 4], gate,
                             st.rbt_free if it == 0 else None])
            toks.append(em(V, lambda e, it=it, xb=xb: e.tensor_tensor(
                resid[:, it, :], resid[:, it, :], xb[:, :], ALU.add),
                waits=[t_l, rtoks[it]], inc=True))
        return ln_apply_inplace(ln_stats(toks, dummy=fbuf[:, 0, 0:1024]))

    for seq in (0, 1):
        rtoks = load_x(seq)
        if 1 in stages:
            rtoks = run_stage1(seq, rtoks)
        if 2 in stages:
            rtoks = run_stage2(seq, rtoks)
        if 3 in stages:
            rtoks = run_stage3(seq, rtoks)
        if 4 in stages:
            rtoks = run_stage4(seq, rtoks)
        store_out(seq, rtoks)

    mk.wait_only(SY, final_toks)
    mk.run_block()
    n_ins = {e: len(mk.prog[e]) for e in ENGS}
    print("program sizes:", n_ins, "total", sum(n_ins.values()))
    return nc


def make_consts():
    c = {}
    c["c_ident"] = np.eye(128, dtype=ml_dtypes.bfloat16)
    i = np.arange(128)
    c["c_m0"] = (i[None, :] >= i[:, None]).astype(ml_dtypes.bfloat16)
    c["c_mleft"] = (i[None, :] <= i[:, None]).astype(ml_dtypes.bfloat16)
    jeff = np.concatenate([np.arange(2048, dtype=np.float64), [-2.0, -1.0]])
    al = np.zeros((16, 5, 2050), dtype=ml_dtypes.bfloat16)
    for h in range(16):
        r = ALIBI16[h] * jeff
        for t in range(4):
            hi = r.astype(ml_dtypes.bfloat16)
            al[h, t] = hi
            r = r - hi.astype(np.float64)
        al[h, 4] = 1.0
    c["c_alibi"] = al
    ci = np.zeros((16, 2048), dtype=ml_dtypes.bfloat16)
    for h in range(16):
        ci[h] = (-(ALIBI16[h] * np.arange(2048, dtype=np.float64) + 8.0)).astype(ml_dtypes.bfloat16)
    c["c_ci"] = ci
    DH, WIN, N = 64, 512, 2048
    half = np.arange(0, DH, 2, dtype=np.float64)
    inv_freq = 1.0 / 10000.0 ** (half / DH)
    t = np.arange(N, dtype=np.float64)
    freqs = t[:, None] * inv_freq[None, :]
    freqs = np.concatenate([freqs, freqs], axis=-1)
    base = (half + 0.4 * DH) / (1.4 * DH)
    power = (t - N // 2) / (WIN // 2)
    scale = base[None, :] ** power[:, None]
    scale = np.concatenate([scale, scale], axis=-1)
    cos, sin = np.cos(freqs), np.sin(freqs)
    rot = np.zeros((4, 2048, 64), dtype=ml_dtypes.bfloat16)
    rot[0] = (8.0 * cos * scale).astype(ml_dtypes.bfloat16)
    rot[1] = (8.0 * sin * scale).astype(ml_dtypes.bfloat16)
    rot[2] = (cos / scale).astype(ml_dtypes.bfloat16)
    rot[3] = (sin / scale).astype(ml_dtypes.bfloat16)
    c["c_rot"] = rot
    return c


def pad_w1(inputs):
    w1 = np.asarray(inputs["L1_ffn_w1"])
    b1 = np.asarray(inputs["L1_ffn_b1"])
    w2 = np.asarray(inputs["L1_ffn_w2"])
    HID = 2730
    w1a = np.zeros((2, 1024, 2816), np.float32)
    w1g = np.zeros((2, 1024, 2816), np.float32)
    w1a[:, :, :HID] = w1[:, :, :HID]
    w1g[:, :, :HID] = w1[:, :, HID:]
    b1a = np.zeros((2, 2816), np.float32)
    b1g = np.zeros((2, 2816), np.float32)
    b1a[:, :HID] = b1[:, :HID]
    b1g[:, :HID] = b1[:, HID:]
    w2p = np.zeros((2, 2816, 1024), np.float32)
    w2p[:, :HID, :] = w2
    return dict(w1a_pad=w1a, w1g_pad=w1g, b1a_pad=b1a, b1g_pad=b1g, w2_pad=w2p)


def kernel(**inputs):
    from concourse.bass_utils import run_bass_kernel_spmd
    x = np.ascontiguousarray(np.asarray(inputs["L0_attn_x"]))
    consts = make_consts()
    pads = pad_w1(inputs)
    nc = build(stages=(1, 2, 3, 4))
    base = {}
    for k in inputs:
        if k not in ("L0_attn_x", "L1_ffn_w1", "L1_ffn_b1", "L1_ffn_w2"):
            base[k] = np.ascontiguousarray(np.asarray(inputs[k]))
    base.update(consts)
    base.update(pads)
    in_maps = []
    for core in range(8):
        m = dict(base)
        m["x"] = np.ascontiguousarray(x[2 * core:2 * core + 2])
        in_maps.append(m)
    res = run_bass_kernel_spmd(nc, in_maps, core_ids=list(range(8)))
    out = np.concatenate([np.asarray(r["out"]) for r in res.results], axis=0)
    return out.astype(np.float32)


# version 02
# speedup vs baseline: 1.2088x; 1.2088x over previous
import math
import numpy as np
import ml_dtypes
import concourse.bass as bass
import concourse.mybir as mybir
from concourse.bass_utils import run_bass_kernel_spmd

F32 = mybir.dt.float32
BF16 = mybir.dt.bfloat16
I8 = mybir.dt.int8
AF = mybir.ActivationFunctionType
ALU = mybir.AluOpType
AX = mybir.AxisListType

RND = 12582912.0
T, V, S, G, SY = "T", "V", "S", "G", "SY"
ENGS = (T, V, S, G, SY)
_s0 = 2.0 ** (-2.0 ** -1.0)
ALIBI16 = [_s0 * _s0 ** i for i in range(16)]
SKIP = 98.0
DBG = {}


class MK:
    DPOOL = 24

    def __init__(self, nc):
        self.nc = nc
        self.prog = {e: [] for e in ENGS}
        self.cnt = {}
        self.sem = {}
        self.mark = {e: {} for e in ENGS}
        for name in ("st", "sv", "ss"):
            self.sem[name] = nc.alloc_semaphore(name)
            self.cnt[name] = 0
        self.esem = {T: "st", V: "sv", S: "ss"}
        self.dpool = {}
        self.didx = {G: 0, SY: 0}
        for eng, pfx in ((G, "g"), (SY, "y")):
            for i in range(self.DPOOL):
                name = f"d{pfx}{i}"
                self.sem[name] = nc.alloc_semaphore(name)
                self.cnt[name] = 0
            self.dpool[eng] = [f"d{pfx}{i}" for i in range(self.DPOOL)]

    def emit(self, eng, fn, waits=(), inc=None, inc_n=1):
        if DBG.get("serialize") and eng in (T, V, S):
            own = self.esem[eng]
            waits = list(waits) + [(own, self.cnt[own])]
            if inc is None or inc is False:
                inc = True
        wl = []
        m = self.mark[eng]
        for tok in waits:
            if tok is None:
                continue
            semn, val = tok
            assert val <= self.cnt[semn], f"future token {semn}={val}>{self.cnt[semn]}"
            if m.get(semn, 0) < val:
                m[semn] = val
                wl.append((self.sem[semn], val))
        tok = None
        semh = None
        n = 0
        if inc is None and eng in (V, S):
            inc = True
        if inc is True:
            inc = self.esem[eng]
        if inc:
            self.cnt[inc] += inc_n
            tok = (inc, self.cnt[inc])
            semh = self.sem[inc]
            n = inc_n

        if DBG.get("trace_em") and eng == DBG.get("trace_eng", "T"):
            print(f"[{eng} #{len(self.prog[eng])}] waits={[(s, v) for (s, v) in [(tok2[0], tok2[1]) for tok2 in []]] }", end="")
            print(f" wl={[(h.name if hasattr(h, 'name') else h, v) for (h, v) in wl]} inc={inc}")
        def thunk(e, wl=wl, fn=fn, semh=semh, n=n):
            for s, v in wl:
                e.wait_ge(s, v)
            ins = fn(e)
            if semh is not None:
                ins.then_inc(semh, n)
        self.prog[eng].append(thunk)
        return tok

    def dma(self, eng, out, in_, waits=()):
        semn = self.dpool[eng][self.didx[eng] % self.DPOOL]
        self.didx[eng] += 1
        waits = list(waits) + ([(semn, self.cnt[semn])] if self.cnt[semn] else [])
        try:
            nm = out.tensor.name
        except Exception:
            nm = None
        if nm is not None:
            if not hasattr(self, "last_dma"):
                self.last_dma = {}
            waits.append(self.last_dma.get(nm))
        tok = self.emit(eng, lambda e: e.dma_start(out, in_), waits=waits, inc=semn, inc_n=16)
        if nm is not None:
            self.last_dma[nm] = tok
        return tok

    def wait_only(self, eng, toks):
        m = self.mark[eng]
        wl = []
        for tok in toks:
            if tok is None:
                continue
            semn, val = tok
            assert val <= self.cnt[semn]
            if m.get(semn, 0) < val:
                m[semn] = val
                wl.append((self.sem[semn], val))
        if wl:
            def thunk(e, wl=wl):
                for s, v in wl:
                    e.wait_ge(s, v)
            self.prog[eng].append(thunk)

    def run_block(self):
        nc = self.nc
        with nc.Block() as blk:
            @blk.tensor
            def _(e):
                for f in self.prog[T]:
                    f(e)

            @blk.vector
            def _(e):
                for f in self.prog[V]:
                    f(e)

            @blk.scalar
            def _(e):
                for f in self.prog[S]:
                    f(e)

            @blk.gpsimd
            def _(e):
                for f in self.prog[G]:
                    f(e)

            @blk.sync
            def _(e):
                for f in self.prog[SY]:
                    f(e)


def build(stages=(1, 2, 3, 4)):
    stages = set(stages)
    nc = bass.Bass("TRN2", target_bir_lowering=False, detect_race_conditions=not DBG.get("no_rd", True))
    mk = MK(nc)
    em = mk.emit
    dma = mk.dma

    def din(name, shape, dt=F32):
        return nc.dram_tensor(name, shape, dt, kind="ExternalInput").ap()

    x_in = din("x", [2, 2048, 1024])
    out_t = nc.dram_tensor("out", [2, 2048, 1024], F32, kind="ExternalOutput").ap()
    W = dict(
        l0_wq=din("L0_attn_Wq", [1024, 1024]), l0_wk=din("L0_attn_Wk", [1024, 1024]),
        l0_wv=din("L0_attn_Wv", [1024, 1024]), l0_wo=din("L0_attn_Wo", [1024, 1024]),
        l0_bo=din("L0_attn_bo", [1024]),
        f0_wq=din("L0_ffn_weight_q", [1024, 1024], I8), f0_ws=din("L0_ffn_weight_scale", [1024]),
        f0_b=din("L0_ffn_bias", [1024]),
        l1_g=din("L1_attn_gamma", [1024]), l1_wq=din("L1_attn_w_q", [1024, 1024]),
        l1_wkv=din("L1_attn_w_kv", [1024, 2048]), l1_nkv=din("L1_attn_null_kv", [16, 4, 64]),
        l1_qs=din("L1_attn_q_scale", [64]), l1_ks=din("L1_attn_k_scale", [64]),
        l1_wo=din("L1_attn_w_out", [1024, 1024]),
        f1_wqkv=din("L1_ffn_wqkv", [2, 1024, 1536]), f1_qs=din("L1_ffn_q_scale", [2, 64]),
        f1_ks=din("L1_ffn_k_scale", [2, 64]), f1_wo=din("L1_ffn_wo", [2, 512, 1024]),
        f1_lng=din("L1_ffn_ln_g", [2, 1024]), f1_lnb=din("L1_ffn_ln_b", [2, 1024]),
        f1_w1a=din("w1a_pad", [2, 1024, 2816]), f1_w1g=din("w1g_pad", [2, 1024, 2816]),
        f1_b1a=din("b1a_pad", [2, 2816]), f1_b1g=din("b1g_pad", [2, 2816]),
        f1_w2=din("w2_pad", [2, 2816, 1024]), f1_b2=din("L1_ffn_b2", [2, 1024]),
        c_ident=din("c_ident", [128, 128], BF16),
        c_m0=din("c_m0", [128, 128], BF16),
        c_mleft=din("c_mleft", [128, 128], BF16),
        c_alibi=din("c_alibi", [16, 5, 2050], BF16),
        c_ci=din("c_ci", [16, 2048], BF16),
        c_rot=din("c_rot", [4, 2048, 64], BF16),
    )
    xout_sp = nc.dram_tensor("xouter_scratch", [2048, 1024], F32, kind="Internal").ap()

    sb = nc.alloc_sbuf_tensor

    def at(name, shape, dt, off):
        return nc.alloc_sbuf_tensor_at(name, shape, dt, offset=off)

    resid = sb("resid", [128, 16, 1024], F32)
    xt = sb("xt", [128, 8, 2048], BF16)
    stage_b = sb("stage_b", [128, 1024], BF16)
    stage_b2 = sb("stage_b2", [128, 1024], BF16)
    stage_f = sb("stage_f", [128, 1024], F32)
    wld = sb("wld", [128, 3072], BF16)
    bc1 = sb("bc1", [128, 1024], F32)
    rotc = sb("rotc", [128, 4, 16, 64], BF16)
    rowf = sb("rowf", [128, 1024], F32)
    rowb = sb("rowb", [128, 1024], BF16)
    ones_r = sb("ones_r", [1, 128], BF16)
    ones_c = sb("ones_c", [128, 2], BF16)
    m0_t = sb("m0_t", [128, 128], BF16)
    mleft_t = sb("mleft_t", [128, 128], BF16)
    ident_t = sb("ident_t", [128, 128], BF16)
    stats = sb("stats", [128, 4, 16], F32)
    statw = sb("statw", [128, 32], F32)
    sqb = sb("sqb", [128, 16, 4], F32)
    knt = sb("knt", [64, 32], BF16)
    nkst = sb("nkst", [32, 64], F32)
    vnaug = sb("vnaug", [2, 2, 128], BF16)
    qs_bc = sb("qs_bc", [128, 64], F32)
    ks_bc = sb("ks_bc", [128, 64], F32)
    b1a_c = sb("b1a_c", [128, 22], F32)
    b1g_c = sb("b1g_c", [128, 22], F32)
    g_c = sb("g_c", [128, 8], F32)
    b_c = sb("b_c", [128, 8], F32)

    U1 = nc.bump_sbuf(61536)[0]
    ot = at("ot", [128, 8, 2048], BF16, U1)
    vaug = at("vaug", [128, 16, 2, 128], BF16, U1 + 32768)
    qt = at("qt", [128, 2048], BF16, U1 + 40960)
    kt = at("kt", [128, 2080], BF16, U1 + 45056)
    ebuf0 = at("ebuf0", [128, 1024], BF16, U1 + 49248)
    ebuf1 = at("ebuf1", [128, 1024], BF16, U1 + 51296)
    wt1 = at("wt1", [128, 8, 128], BF16, U1 + 53344)
    wt2 = at("wt2", [128, 8, 128], BF16, U1 + 55392)
    wt3 = at("wt3", [128, 8, 128], BF16, U1 + 57440)
    rbt = at("rbt", [128, 1024], BF16, U1 + 59488)
    ebuf = [ebuf0, ebuf1]
    fbuf = at("fbuf", [128, 11, 2048], BF16, U1)
    w2res = at("w2res", [128, 6, 1024], BF16, U1 + 45056)
    xosb = at("xosb", [128, 1024], F32, U1 + 57344)
    bc2 = at("bc2", [128, 1024], F32, U1)
    bc3 = at("bc3", [128, 1024], F32, U1 + 4096)

    U2 = nc.bump_sbuf(16384)[0]
    wres = at("wres", [128, 8, 1024], BF16, U2)
    qtok = at("qtok", [128, 16, 128], BF16, U2)
    ktok = at("ktok", [128, 16, 128], BF16, U2 + 4096)
    w1buf0 = at("w1buf0", [128, 8, 256], BF16, U2 + 8192)
    w1buf1 = at("w1buf1", [128, 8, 256], BF16, U2 + 12288)
    w1buf = [w1buf0, w1buf1]
    cbias = sb("cbias", [128, 1], F32)
    print("SBUF remaining bytes/partition:", nc.sbuf_bytes_remaining // 128)

    pA = nc.alloc_psum_tensor("pA", [128, 512], F32)
    pC = nc.alloc_psum_tensor("pC", [128, 1024], F32)
    pD = nc.alloc_psum_tensor("pD", [128, 1024], F32)
    pE = nc.alloc_psum_tensor("pE", [128, 1024], F32)
    pF = nc.alloc_psum_tensor("pF", [128, 1024], BF16)

    class St:
        pA_free = None
        pC_free = None
        pD_free = None
        pE_free = None
        eb_free = [None, None]
        vaug_free = None
        qk_free = None
        xt_free = None
        wres_free = None
        wld_free = None
        scb_free = None
        rbt_free = None
        fbuf_free = None
        stageb_free = None
        stageb2_free = None
        stagef_free = None
        rowf_free = None
        rbt_w_free = None
        w2res_free = None
        bc1_free = None
        spill_toks = [None] * 4
        tr_free = [None, None, None]
    st = St()

    t_ident = dma(SY, ident_t[:, :], W["c_ident"])
    t_m0 = dma(SY, m0_t[:, :], W["c_m0"])
    t_mleft = dma(SY, mleft_t[:, :], W["c_mleft"])
    t_rot = dma(SY, rotc[:, :, :, :], W["c_rot"].rearrange("t (i p) d -> p t i d", p=128))
    em(V, lambda e: e.memset(ones_r[:, :], 1.0))
    t_ones = em(V, lambda e: e.memset(ones_c[:, :], 1.0), inc=True)
    for p in (pA, pC, pD):
        em(V, lambda e, p=p: e.memset(p[:, :], 0.0))
    t_psz = em(V, lambda e: e.memset(pE[:, :], 0.0), inc=True)
    em(V, lambda e: e.memset(cbias[:, :], -12.0))
    em(V, lambda e: e.memset(vnaug[:, :, :], 0.0))
    em(V, lambda e: e.memset(vnaug[:, 0, 64:66], 1.0))
    t_vn0 = em(V, lambda e: e.memset(vnaug[:, 1, 32:34], 1.0), inc=True)

    final_toks = []

    def init_attn_arena():
        em(V, lambda e: e.memset(vaug[:, :, :, :], 0.0), waits=[st.vaug_free])
        em(V, lambda e: e.memset(vaug[:, :, 0, 64:66], 1.0))
        t_a = em(V, lambda e: e.memset(vaug[:, :, 1, 32:34], 1.0), inc=True)
        t_b = em(V, lambda e: e.memset(qt[64:68, 0:2048], 1.0), waits=[st.qk_free], inc=True)
        return t_b, t_a

    def bcast_row(dst, src_row, waits=()):
        return dma(SY, dst, src_row.unsqueeze(0).partition_broadcast(128), waits=waits)

    def trans_group(srcs, writers, waits=(), n_out=128):
        assert len(srcs) <= 8
        t_t = None
        for i, s in enumerate(srcs):
            kp = s.shape[0]
            ncol = s.shape[-1]
            t_t = em(T, lambda e, s=s, i=i, kp=kp, ncol=ncol: e.transpose(
                pF[0:ncol, i * 128:i * 128 + kp], s, ident_t[0:kp, 0:kp]),
                waits=list(waits) + [st.tr_free[0], t_ident],
                inc=(i == len(srcs) - 1))
        t_e = None
        for w in writers:
            t_e = em(V, lambda e, w=w: w(e), waits=[t_t], inc=True)
        st.tr_free[0] = t_e
        return t_e

    def ln_stats(tile_toks, dummy=None):
        if dummy is None:
            dummy = stage_f[:, :]
        t_red = None
        for it in range(16):
            w = [tile_toks[it]] if tile_toks else []
            em(V, lambda e, it=it: e.tensor_reduce(stats[:, 0, it:it + 1], resid[:, it, :],
                                                   AX.X, ALU.add), waits=w, inc=True)
        for it in range(16):
            t_red = em(V, lambda e, it=it, dummy=dummy: e.scalar_tensor_tensor(
                dummy, resid[:, it, :], 1.0, resid[:, it, :], ALU.mult, ALU.mult,
                accum_out=stats[:, 1, it:it + 1]), inc=True)
        t_mu = em(V, lambda e: e.tensor_scalar(stats[:, 2, :], stats[:, 0, :], 1.0 / 1024.0,
                                               None, ALU.mult), waits=[t_red])
        t_m2 = em(V, lambda e: e.tensor_tensor(statw[:, 0:16], stats[:, 2, :],
                                               stats[:, 2, :], ALU.mult), waits=[t_mu])
        t_va = em(V, lambda e: e.scalar_tensor_tensor(statw[:, 16:32], stats[:, 1, :],
                                                      1.0 / 1024.0, statw[:, 0:16],
                                                      ALU.mult, ALU.subtract), waits=[t_m2])
        t_v = em(V, lambda e: e.tensor_scalar(statw[:, 16:32], statw[:, 16:32], 1e-5, None,
                                              ALU.add), waits=[t_va], inc=True)
        t_sq = em(S, lambda e: e.activation(statw[:, 0:16], statw[:, 16:32], AF.Sqrt),
                  waits=[t_v], inc=True)
        return em(V, lambda e: e.reciprocal(stats[:, 3, :], statw[:, 0:16]),
                  waits=[t_sq], inc=True)

    def ln_apply_inplace(t_rs):
        out = []
        for it in range(16):
            out.append(em(V, lambda e, it=it: e.tensor_scalar(
                resid[:, it, :], resid[:, it, :], stats[:, 2, it:it + 1], stats[:, 3, it:it + 1],
                ALU.subtract, ALU.mult), waits=[t_rs], inc=True))
        return out

    def build_xt(tile_toks, shift=False, pre=None):
        last = None
        t_shift = None
        if shift:
            t_shift = em(V, lambda e: e.memset(xt[:, 4:8, 0:2], 0.0), waits=[st.xt_free], inc=True)
        for it in range(16):
            if pre is not None:
                t_c = pre(it)
            else:
                w = [tile_toks[it]] if tile_toks else []
                t_c = em(V, lambda e, it=it: e.tensor_copy(stage_b[:, :], resid[:, it, :]),
                         waits=w, inc=True)
            srcs = [stage_b[:, c * 128:(c + 1) * 128] for c in range(8)]
            if not shift:
                def w0(e, it=it):
                    return e.tensor_copy(xt[:, 0:8, it * 128:(it + 1) * 128],
                                         pF[:, 0:1024].rearrange("p (c n) -> p c n", n=128))
                writers = [w0]
            else:
                def w0(e, it=it):
                    return e.tensor_copy(xt[:, 0:4, it * 128:(it + 1) * 128],
                                         pF[:, 0:512].rearrange("p (c n) -> p c n", n=128))
                n2 = 128 if it < 15 else 127

                def w1(e, it=it, n2=n2):
                    return e.tensor_copy(
                        xt[:, 4:8, it * 128 + 1:it * 128 + 1 + n2],
                        pF[:, 512:1024].rearrange("p (c n) -> p c n", n=128)[:, :, 0:n2])
                writers = [w0, w1]
            last = trans_group(srcs, writers, waits=[t_c, t_shift, st.xt_free])
        return last

    def load_x(seq):
        toks = []
        prev_stores = final_toks[-4:] if final_toks else []
        for q in range(4):
            src = x_in[seq].rearrange("(t p) d -> p t d", p=128)[:, 4 * q:4 * q + 4, :]
            tk = dma(SY, resid[:, 4 * q:4 * q + 4, :], src, waits=list(prev_stores))
            toks += [tk] * 4
        return toks

    def store_out(seq, tile_toks):
        for q in range(4):
            w = [tile_toks[it] for it in range(4 * q, 4 * q + 4)]
            dst = out_t[seq].rearrange("(t p) d -> p t d", p=128)[:, 4 * q:4 * q + 4, :]
            final_toks.append(dma(SY, dst, resid[:, 4 * q:4 * q + 4, :], waits=w))

    def build_wresT(src_ap_rows, gate=None):
        last = None
        for rc in range(8):
            t_l = dma(G, wld[:, 0:1024], src_ap_rows(rc), waits=[last, gate, st.wld_free])
            srcs = [wld[:, c * 128:(c + 1) * 128] for c in range(8)]

            def w0(e, rc=rc):
                return e.tensor_copy(wres[:, :, rc * 128:(rc + 1) * 128],
                                     pF[:, 0:1024].rearrange("p (c n) -> p c n", n=128))
            last = trans_group(srcs, [w0], waits=[t_l, st.wres_free])
            st.wld_free = last
        return last

    def attn_head(*, kbuf_ap, qbuf_ap, chunks, parity, ot_dst_fn,
                  gate_tok=None, exp_bias=0.0):
        prow = 65 if parity == 0 else 128
        lrow = 64 if parity == 0 else 32
        orows = (0, 64) if parity == 0 else (64, 128)
        enull = stage_b2[0:2, :]
        def pieces(col_lo, col_n):
            out = []
            off = col_lo
            end = col_lo + col_n
            while off < end:
                n = min(end - off, 512 - (off % 512))
                out.append((off, n))
                off += n
            return out

        for ici, (ic_off, ic_n, jt_list, do_null) in enumerate(chunks):
            n_halves = (ic_n + 511) // 512
            seq_banks = []
            for (_, cl, cn, _) in jt_list:
                seq_banks += [off // 512 for off, n in pieces(cl, cn)]
            if do_null:
                seq_banks += list(range(n_halves))
            n_out_mms = len(seq_banks)
            last_of_bank = {}
            for i_b, b in enumerate(seq_banks):
                last_of_bank[b] = i_b
            box = {"oi": 0, "out_last": None}
            touched = set()

            def emit_out(idx, jt, eb, pcs, t_mask):
                for off, n in pcs:
                    bank = off // 512
                    fo = bank not in touched
                    touched.add(bank)
                    oi = box["oi"]
                    box["out_last"] = em(T, lambda e, jt=jt, eb=eb, off=off, n=n,
                                         fo=fo, lo=(last_of_bank[bank] == oi): e.matmul(
                        pE[0:prow, off:off + n], vaug[:, jt, parity, 0:prow],
                        eb[:, off:off + n], start=fo, stop=lo),
                        waits=[t_mask, st.pE_free if oi == 0 else None], inc=True)
                    box["oi"] += 1
                st.eb_free[idx % 2] = box["out_last"]

            pend = None
            for idx, (jt, col_lo, col_n, mask_kind) in enumerate(jt_list):
                psim = pC if idx % 2 == 0 else pD
                psim_free = st.pC_free if idx % 2 == 0 else st.pD_free
                eb = ebuf[idx % 2]
                mm = None
                pcs = pieces(col_lo, col_n)
                for pi, (off, n) in enumerate(pcs):
                    mm = em(T, lambda e, psim=psim, jt=jt, off=off, n=n, ic_off=ic_off: e.matmul(
                        psim[:, off:off + n], kbuf_ap[:, jt * 128:(jt + 1) * 128],
                        qbuf_ap[:, ic_off + off:ic_off + off + n], start=True, stop=True),
                        waits=[psim_free, gate_tok, t_psz], inc=(pi == len(pcs) - 1))
                bias_ap = cbias[:, 0:1] if exp_bias != 0.0 else 0.0
                t_exp = em(S, lambda e, psim=psim, eb=eb, col_lo=col_lo, col_n=col_n,
                           bias_ap=bias_ap: e.activation(
                    eb[:, col_lo:col_lo + col_n], psim[:, col_lo:col_lo + col_n], AF.Exp,
                    bias=bias_ap), waits=[mm, st.eb_free[idx % 2]], inc="ss")
                if idx % 2 == 0:
                    st.pC_free = t_exp
                else:
                    st.pD_free = t_exp
                t_mask = t_exp
                if mask_kind is not None:
                    mt = m0_t if mask_kind == "m0" else mleft_t
                    t_mld = t_m0 if mask_kind == "m0" else t_mleft
                    mlo = col_lo if mask_kind == "m0" else col_lo + col_n - 128
                    t_mask = em(V, lambda e, eb=eb, mlo=mlo, mt=mt: e.tensor_tensor(
                        eb[:, mlo:mlo + 128], eb[:, mlo:mlo + 128], mt[:, :], ALU.mult),
                        waits=[t_exp, t_mld], inc=True)
                if pend is not None:
                    emit_out(*pend)
                pend = (idx, jt, eb, pcs, t_mask)
            if pend is not None:
                emit_out(*pend)
            out_last = box["out_last"]
            oi = box["oi"]
            if do_null:
                mmn = []
                for half in range(n_halves):
                    mmn.append(em(T, lambda e, half=half, ic_off=ic_off: e.matmul(
                        pA[half * 32:half * 32 + 2, 0:512], kbuf_ap[:, 2048:2050],
                        qbuf_ap[:, ic_off + half * 512:ic_off + (half + 1) * 512],
                        start=True, stop=True), waits=[st.pA_free, t_psz, gate_tok], inc=True))
                t_en = None
                for half in range(n_halves):
                    nbias = cbias[:, 0:1] if exp_bias != 0.0 else 0.0
                    t_en = em(S, lambda e, half=half, nbias=nbias: e.activation(
                        enull[:, half * 512:(half + 1) * 512],
                        pA[half * 32:half * 32 + 2, 0:512], AF.Exp, bias=nbias),
                        waits=[mmn[-1]], inc="ss")
                st.pA_free = t_en
                for half in range(n_halves):
                    fo = half not in touched
                    touched.add(half)
                    out_last = em(T, lambda e, half=half, fo=fo,
                                  lo=(last_of_bank[half] == oi): e.matmul(
                        pE[0:prow, half * 512:(half + 1) * 512], vnaug[:, parity, 0:prow],
                        enull[:, half * 512:(half + 1) * 512], start=fo, stop=lo),
                        waits=[t_en, st.pE_free if oi == 0 else None], inc=True)
                    oi += 1
            st.vaug_free = out_last
            st.qk_free = out_last
            t_rc = em(V, lambda e, ic_n=ic_n, lrow=lrow: e.reciprocal(
                rowf[0:1, 0:ic_n], pE[lrow:lrow + 1, 0:ic_n]), waits=[out_last], inc=True)
            t_rb = em(V, lambda e, ic_n=ic_n: e.tensor_copy(rowb[0:1, 0:ic_n], rowf[0:1, 0:ic_n]),
                      waits=[t_rc], inc=True)
            for half in range(n_halves):
                n = min(512, ic_n - half * 512)
                t_bc = em(T, lambda e, half=half, n=n: e.matmul(
                    pA[:, 0:n], ones_r[0:1, 0:128], rowb[0:1, half * 512:half * 512 + n],
                    start=True, stop=True), waits=[t_rb, st.pA_free, t_ones], inc=True)
                t_rbt = em(V, lambda e, half=half, n=n: e.tensor_copy(
                    rbt[:, half * 512:half * 512 + n], pA[:, 0:n]), waits=[t_bc], inc=True)
                st.pA_free = t_rbt
            ot_ap = ot_dst_fn(ici)
            if DBG.get("trace_evac"):
                print("EVAC ici", ici, "ot offset", ot_ap.offset, "pA_free", st.pA_free,
                      "next sv", mk.cnt["sv"] + 1)
            if DBG.get("skip_evac_ici") == ici:
                t_ot = em(V, lambda e: e.memset(stage_f[0:1, 0:4], 0.0), waits=[st.pA_free], inc=True)
            else:
                t_ot = em(V, lambda e, ot_ap=ot_ap, ic_n=ic_n: e.tensor_tensor(
                    ot_ap, pE[orows[0]:orows[1], 0:ic_n], rbt[orows[0]:orows[1], 0:ic_n], ALU.mult),
                    waits=[st.pA_free], inc=True)
            st.pE_free = t_ot
            if DBG.get("chunk_barrier"):
                mk.wait_only(T, [t_ot])
                mk.wait_only(S, [t_ot])

    def proj_tok_collect(rhs_ap, ncols, evac, gate=(), lhs_buf=None):
        if lhs_buf is None:
            lhs_buf = xt
        toks = []
        for it in range(16):
            use_pC = (it % 2 == 0)
            ps = pC if use_pC else pD
            ps_free = st.pC_free if use_pC else st.pD_free
            mm = None
            for c in range(8):
                off, left = 0, ncols
                while left > 0:
                    n = min(512, left)
                    mm = em(T, lambda e, ps=ps, c=c, it=it, off=off, n=n, lhs_buf=lhs_buf: e.matmul(
                        ps[:, off:off + n], lhs_buf[:, c, it * 128:(it + 1) * 128],
                        rhs_ap(c)[:, off:off + n], start=(c == 0), stop=(c == 7)),
                        waits=[ps_free, t_psz] + list(gate), inc=(c == 7 and left == n))
                    off += n
                    left -= n
            if lhs_buf is xt:
                st.xt_free = mm
            st.wres_free = mm
            t_e = evac(it, ps, mm)
            toks.append(t_e)
            if use_pC:
                st.pC_free = t_e
            else:
                st.pD_free = t_e
        return toks

    def run_stage1(seq, rtoks):
        t_arena = init_attn_arena()
        if len(tuple(DBG.get("hps", range(8)))) < 8 or DBG.get("memset_ot"):
            em(V, lambda e: e.memset(ot[:, :, :], 0.0), waits=[st.pE_free])
        if DBG.get("no_xt"):
            t_xt = None
        else:
            t_xt = build_xt(rtoks, shift=False)
        t_bo = bcast_row(bc1[:, :], W["l0_bo"], waits=[st.bc1_free])
        if DBG.get("no_wres"):
            t_wo = em(V, lambda e: e.memset(wres[:, :, :], 0.0), waits=[st.wres_free], inc=True)
        else:
            t_wo = build_wresT(lambda rc: W["l0_wo"][rc * 128:(rc + 1) * 128, :])
        wt_last = None
        for hp in DBG.get("hps", range(8)):
            for wname, wbuf in (() if hp in DBG.get("wt_off_hps", ()) else (("l0_wq", wt1), ("l0_wk", wt2), ("l0_wv", wt3))):
                t_l = dma(G, wld[:, 0:1024], W[wname][hp * 128:(hp + 1) * 128, :],
                          waits=[wt_last, st.wld_free, t_wo if hp == 0 else None])
                srcs = [wld[:, c * 128:(c + 1) * 128] for c in range(8)]

                def w0(e, wbuf=wbuf):
                    return e.tensor_copy(wbuf[:, :, :],
                                         pF[:, 0:1024].rearrange("p (c n) -> p c n", n=128))
                wt_last = trans_group(srcs, [w0], waits=[t_l, st.qk_free])
                st.wld_free = wt_last
            t_wt = wt_last
            prj_last = None
            for which, wbuf in (() if hp in DBG.get("proj_off_hps", ()) else (("q", wt1), ("k", wt2), ("v", wt3))):
                for nchunk in range(4):
                    use_pC = (nchunk % 2 == 0)
                    ps = pC if use_pC else pD
                    ps_free = st.pC_free if use_pC else st.pD_free
                    mm = None
                    for c in range(8):
                        mm = em(T, lambda e, ps=ps, wbuf=wbuf, c=c, nchunk=nchunk: e.matmul(
                            ps[:, 0:512], wbuf[:, c, :], xt[:, c, nchunk * 512:(nchunk + 1) * 512],
                            start=(c == 0), stop=(c == 7)),
                            waits=[t_wt, t_xt, ps_free, t_psz,
                                   st.qk_free if c == 0 else None], inc=(c == 7))
                    st.xt_free = mm
                    if which == "q":
                        t_e = em(S, lambda e, ps=ps, nchunk=nchunk: e.activation(
                            qt[0:128, nchunk * 512:(nchunk + 1) * 512], ps[:, 0:512], AF.Copy,
                            scale=0.125), waits=[mm, t_arena[0]], inc="ss")
                    elif which == "k":
                        t_e = em(S, lambda e, ps=ps, nchunk=nchunk: e.copy(
                            kt[0:128, nchunk * 512:(nchunk + 1) * 512], ps[:, 0:512]),
                            waits=[mm], inc="ss")
                    else:
                        t_e = em(S, lambda e, ps=ps: e.copy(stage_b2[:, 0:512], ps[:, 0:512]),
                                 waits=[mm, prj_last], inc="ss")
                        it0 = nchunk * 4
                        srcs = [stage_b2[:, k * 128:(k + 1) * 128] for k in range(4)]

                        def w0(e, it0=it0):
                            return e.tensor_copy(
                                vaug[:, it0:it0 + 4, 0, 0:64],
                                pF[:, 0:512].rearrange("p (c n) -> p c n", n=128)[:, :, 0:64])

                        def w1(e, it0=it0):
                            return e.tensor_copy(
                                vaug[:, it0:it0 + 4, 1, 64:128],
                                pF[:, 0:512].rearrange("p (c n) -> p c n", n=128)[:, :, 64:128])
                        t_e = trans_group(srcs, [w0, w1],
                                          waits=[t_e, t_arena[1], st.vaug_free])
                    if use_pC:
                        st.pC_free = t_e
                    else:
                        st.pD_free = t_e
                    prj_last = t_e
            for h in (DBG.get("hs", range(2)) if hp not in DBG.get("attn_off_hps", ()) else ()):
                jtl = [(jt, 0, 1024, None) for jt in DBG.get("jts", range(16))]
                chunks = [(0, 1024, jtl, False), (1024, 1024, jtl, False)]
                chunks = [chunks[i] for i in DBG.get("ics", range(2))]

                def ot_dst(ici, hp=hp, h=h):
                    r0 = 0 if h == 0 else 64
                    return ot[r0:r0 + 64, hp, ici * 1024:(ici + 1) * 1024]
                attn_head(kbuf_ap=kt[64 * h:64 * h + 64, 0:2048],
                          qbuf_ap=qt[64 * h:64 * h + 64, 0:2048],
                          chunks=chunks, parity=h, ot_dst_fn=ot_dst, gate_tok=prj_last)
        t_ot_all = st.pE_free

        def evac_z(it, ps, mm):
            t_s = em(V, lambda e, ps=ps: e.tensor_tensor(stage_f[:, :], ps[:, 0:1024], bc1[:, :],
                                                         ALU.add), waits=[mm, t_bo])
            return em(V, lambda e, it=it: e.tensor_tensor(resid[:, it, :], resid[:, it, :],
                                                          stage_f[:, :], ALU.add),
                      waits=[t_s], inc=True)
        ztoks = proj_tok_collect(lambda c: wres[:, c, 0:1024], 1024, evac_z,
                                 gate=[t_wo, t_ot_all], lhs_buf=ot)
        if DBG.get("no_ln"):
            return ztoks
        return ln_apply_inplace(ln_stats(ztoks))

    def run_stage2(seq, rtoks):
        t_ws = bcast_row(bc2[:, :], W["f0_ws"], waits=[st.vaug_free, rtoks[0]])
        t_bi = bcast_row(bc3[:, :], W["f0_b"], waits=[rtoks[0]])
        t_wq = build_wresT(lambda rc: W["f0_wq"][rc * 128:(rc + 1) * 128, :])
        ztoks = []
        for it in range(16):
            w = [rtoks[it]]
            t_am = em(V, lambda e, it=it: e.tensor_reduce(statw[:, it:it + 1], resid[:, it, :],
                                                          AX.X, ALU.max,
                                                          apply_absolute_value=True),
                      waits=w, inc=True)
            t_ts = em(V, lambda e, it=it: e.tensor_scalar(statw[:, it:it + 1],
                                                          statw[:, it:it + 1],
                                                          1e-6, 1.0 / 127.0, ALU.max, ALU.mult),
                      waits=[t_am])
            t_r = em(V, lambda e, it=it: e.reciprocal(statw[:, 16 + it:17 + it],
                                                      statw[:, it:it + 1]),
                     waits=[t_ts], inc=True)
            t_m = em(V, lambda e, it=it: e.tensor_scalar(stage_f[:, :], resid[:, it, :],
                                                         statw[:, 16 + it:17 + it], None,
                                                         ALU.mult), waits=[t_r], inc=True)
            t_rnd = em(S, lambda e: e.activation(stage_f[:, :], stage_f[:, :], AF.Copy, bias=RND),
                       waits=[t_m], inc=True)
            t_rnd = em(S, lambda e: e.activation(stage_b[:, :], stage_f[:, :], AF.Copy, bias=-RND),
                       waits=[t_rnd], inc=True)
            srcs = [stage_b[:, c * 128:(c + 1) * 128] for c in range(8)]

            def wxq(e, it=it):
                return e.tensor_copy(xt[:, 0:8, it * 128:(it + 1) * 128],
                                     pF[:, 0:1024].rearrange("p (c n) -> p c n", n=128))
            t_tr = trans_group(srcs, [wxq], waits=[t_rnd, st.xt_free])
            use_pC = (it % 2 == 0)
            ps = pC if use_pC else pD
            ps_free = st.pC_free if use_pC else st.pD_free
            mm = None
            for c in range(8):
                for half in range(2):
                    mm = em(T, lambda e, ps=ps, c=c, it=it, half=half: e.matmul(
                        ps[:, half * 512:(half + 1) * 512], xt[:, c, it * 128:(it + 1) * 128],
                        wres[:, c, half * 512:(half + 1) * 512], start=(c == 0), stop=(c == 7)),
                        waits=[t_tr, t_wq, ps_free, t_psz], inc=(c == 7 and half == 1))
            st.xt_free = mm
            st.wres_free = mm
            t_dq = em(V, lambda e, ps=ps, it=it: e.scalar_tensor_tensor(
                stage_f[:, :], ps[:, 0:1024], statw[:, it:it + 1], bc2[:, :], ALU.mult, ALU.mult),
                waits=[mm, t_ws, t_bi])
            t_b3 = em(V, lambda e: e.tensor_tensor(stage_f[:, :], stage_f[:, :], bc3[:, :],
                                                   ALU.add), waits=[t_dq])
            t_e = em(V, lambda e, it=it: e.tensor_tensor(resid[:, it, :], resid[:, it, :],
                                                         stage_f[:, :], ALU.add),
                     waits=[t_b3], inc=True)
            ztoks.append(t_e)
            if use_pC:
                st.pC_free = t_e
            else:
                st.pD_free = t_e
        return ln_apply_inplace(ln_stats(ztoks))

    wldv = wld[:, 0:3072].rearrange("p (c n) -> p c n", n=384)

    def qkv_pair(wq_ap, wk_ap, wv_ap, t_xt, t_arena, first_pair, t_qsb, t_ksb):
        t_l1 = dma(G, wldv[:, :, 0:128], wq_ap.rearrange("(c p) n -> p c n", p=128),
                   waits=[st.qk_free, st.wld_free, st.xt_free if first_pair else None])
        t_l2 = dma(G, wldv[:, :, 128:256], wk_ap.rearrange("(c p) n -> p c n", p=128))
        t_l3 = dma(G, wldv[:, :, 256:384], wv_ap.rearrange("(c p) n -> p c n", p=128),
                   waits=[st.vaug_free])
        prj_last = None
        for it in range(16):
            use_pC = (it % 2 == 0)
            ps = pC if use_pC else pD
            ps_free = st.pC_free if use_pC else st.pD_free
            mm = None
            for c in range(8):
                mm = em(T, lambda e, ps=ps, c=c, it=it: e.matmul(
                    ps[:, 0:384], xt[:, c, it * 128:(it + 1) * 128], wldv[:, c, :],
                    start=(c == 0), stop=(c == 7)),
                    waits=[t_l1, t_l2, t_l3, t_xt, ps_free, t_psz], inc=(c == 7))
            st.xt_free = mm
            st.wld_free = mm
            em(V, lambda e, ps=ps, it=it: e.tensor_copy(qtok[:, it, :], ps[:, 0:128]),
               waits=[mm])
            em(V, lambda e, ps=ps, it=it: e.tensor_copy(ktok[:, it, :], ps[:, 128:256]))
            em(V, lambda e, ps=ps, it=it: e.tensor_copy(vaug[:, it, 0, 0:64], ps[:, 256:320]),
               waits=[t_arena[1], st.vaug_free if it == 0 else None])
            t_v = em(V, lambda e, ps=ps, it=it: e.tensor_copy(vaug[:, it, 1, 64:128],
                                                              ps[:, 320:384]), inc=True)
            t_e = t_v
            if use_pC:
                st.pC_free = t_e
            else:
                st.pD_free = t_e
            prj_last = t_e
        t_red2 = None
        for which, tokb, col0 in (("q", qtok, 0), ("k", ktok, 2)):
            for it in range(16):
                t_sb = em(V, lambda e, tokb=tokb, it=it: e.tensor_tensor(
                    stage_b[:, 0:128], tokb[:, it, :], tokb[:, it, :], ALU.mult),
                    waits=[prj_last])
                t_red2 = em(V, lambda e, it=it, col0=col0: e.tensor_reduce(
                    sqb[:, it, col0:col0 + 2],
                    stage_b[:, 0:128].rearrange("p (s c) -> p s c", c=64), AX.X, ALU.add),
                    waits=[t_sb], inc=True)
        sqf = sqb[:, :, :].rearrange("p a b -> p (a b)")
        t_sq2 = em(S, lambda e: e.activation(sqf, sqf, AF.Sqrt), waits=[t_red2], inc=True)
        t_mx = em(V, lambda e: e.tensor_scalar(sqf, sqf, 1e-12, None, ALU.max),
                  waits=[t_sq2])
        t_rcp = em(V, lambda e: e.reciprocal(sqf, sqf), waits=[t_mx], inc=True)
        t_nq = {}
        for which, tokb, col0, scb in (("q", qtok, 0, qs_bc), ("k", ktok, 2, ks_bc)):
            lastn = None
            for it in range(16):
                t_n1 = em(V, lambda e, tokb=tokb, it=it, col0=col0: e.tensor_tensor(
                    tokb[:, it, :].rearrange("p (s c) -> p s c", c=64),
                    tokb[:, it, :].rearrange("p (s c) -> p s c", c=64),
                    sqb[:, it, col0:col0 + 2].rearrange("p (s c) -> p s c", c=1)
                    .broadcast_to([128, 2, 64]), ALU.mult), waits=[t_rcp])
                lastn = em(V, lambda e, tokb=tokb, it=it, scb=scb: e.tensor_tensor(
                    tokb[:, it, :].rearrange("p (s c) -> p s c", c=64),
                    tokb[:, it, :].rearrange("p (s c) -> p s c", c=64),
                    scb[:, 0:64].rearrange("p (a c) -> p a c", a=1).broadcast_to([128, 2, 64]),
                    ALU.mult), waits=[t_qsb, t_ksb, t_n1], inc=True)
            t_nq[which] = lastn
        st.scb_free = t_nq["k"]
        return prj_last, t_nq

    def run_stage3(seq, rtoks):
        t_arena = init_attn_arena()
        t_g = bcast_row(bc1[:, :], W["l1_g"], waits=[rtoks[0]])
        t_qsb = bcast_row(qs_bc[:, :], W["l1_qs"])
        t_ksb = bcast_row(ks_bc[:, :], W["l1_ks"])
        t_rs = ln_stats(rtoks)

        def pre_ln(it):
            t_p = em(V, lambda e, it=it: e.tensor_scalar(stage_f[:, :], resid[:, it, :],
                                                         stats[:, 2, it:it + 1],
                                                         stats[:, 3, it:it + 1],
                                                         ALU.subtract, ALU.mult),
                     waits=[t_rs, t_g])
            return em(V, lambda e: e.tensor_tensor(stage_b[:, :], stage_f[:, :], bc1[:, :],
                                                   ALU.mult), waits=[t_p], inc=True)
        t_xt = build_xt(None, shift=False, pre=pre_ln)

        t_nk = dma(G, nkst[0:32, 0:64],
                   W["l1_nkv"].rearrange("h (j c) d -> (h j) c d", c=2)[:, 0, :])
        t_nacc = em(V, lambda e: e.scalar_tensor_tensor(stage_f[0:32, 0:64], nkst[0:32, :], 1.0,
                                                        nkst[0:32, :], ALU.mult, ALU.mult,
                                                        accum_out=statw[0:32, 0:1]),
                    waits=[t_nk], inc=True)
        t_sqn = em(S, lambda e: e.activation(statw[0:32, 1:2], statw[0:32, 0:1], AF.Sqrt),
                   waits=[t_nacc], inc=True)
        t_nm = em(V, lambda e: e.tensor_scalar(statw[0:32, 1:2], statw[0:32, 1:2], 1e-12, None,
                                               ALU.max), waits=[t_sqn])
        t_nr = em(V, lambda e: e.reciprocal(statw[0:32, 2:3], statw[0:32, 1:2]), waits=[t_nm])
        t_ns = em(V, lambda e: e.tensor_scalar(stage_f[0:32, 0:64], nkst[0:32, :],
                                               statw[0:32, 2:3], None, ALU.mult), waits=[t_nr])
        t_khn = em(V, lambda e: e.tensor_tensor(stage_b2[0:32, 0:64], stage_f[0:32, 0:64],
                                                ks_bc[0:32, :], ALU.mult),
                   waits=[t_ksb, t_ns], inc=True)
        def wknt(e):
            return e.tensor_copy(knt[0:64, 0:32], pF[0:64, 0:32])
        t_knt = trans_group([stage_b2[0:32, 0:64]], [wknt], waits=[t_khn], n_out=64)

        for pair in range(8):
            prj_last, t_nq = qkv_pair(
                W["l1_wq"][:, pair * 128:(pair + 1) * 128],
                W["l1_wkv"][:, pair * 128:(pair + 1) * 128],
                W["l1_wkv"][:, 1024 + pair * 128:1024 + (pair + 1) * 128],
                t_xt, t_arena, pair == 0, t_qsb, t_ksb)
            for hh in range(2):
                h = pair * 2 + hh
                parity = hh
                t_al = dma(SY, kt[64:69, 0:2050], W["c_alibi"][h], waits=[st.qk_free])
                t_ci = dma(SY, qt[68:69, 0:2048], W["c_ci"][h].unsqueeze(0))
                t_kn = em(V, lambda e, h=h: e.tensor_copy(kt[0:64, 2048:2050],
                                                          knt[0:64, 2 * h:2 * h + 2]),
                          waits=[t_knt, st.qk_free], inc=True)
                tq_last = None
                for g8 in range(2):
                    srcs = [qtok[:, g8 * 8 + i, hh * 64:hh * 64 + 64] for i in range(8)]

                    def wq8(e, g8=g8):
                        return e.tensor_scalar(
                            qt[0:64, g8 * 1024:(g8 + 1) * 1024], pF[0:64, 0:1024],
                            8.0, None, ALU.mult)
                    tq_last = trans_group(srcs, [wq8], waits=[t_nq["q"], st.qk_free], n_out=64)
                    srcs = [ktok[:, g8 * 8 + i, hh * 64:hh * 64 + 64] for i in range(8)]

                    def wk8(e, g8=g8):
                        return e.tensor_copy(kt[0:64, g8 * 1024:(g8 + 1) * 1024], pF[0:64, 0:1024])
                    tq_last = trans_group(srcs, [wk8], waits=[t_nq["k"], st.qk_free], n_out=64)
                t_vn = dma(G, vnaug[:, parity, (0 if parity == 0 else 64):
                                    (64 if parity == 0 else 128)],
                           W["l1_nkv"].rearrange("h (j c) d -> c j h d", c=2)[1][:, h, :],
                           waits=[t_vn0, st.vaug_free])
                mk.wait_only(T, [t_al, t_ci, t_kn, t_vn, t_arena[0]])
                slope = ALIBI16[h]
                chunks = []
                for ic in range(2):
                    i_lo, i_hi = ic * 1024, ic * 1024 + 1023
                    jtl = []
                    for jt in range(16):
                        if 128 * jt > i_hi:
                            continue
                        if slope * (i_lo - (128 * jt + 127)) > SKIP:
                            continue
                        col_lo = max(0, 128 * jt - i_lo)
                        mask = "m0" if 128 * jt >= i_lo else None
                        jtl.append((jt, col_lo, 1024 - col_lo, mask))
                    do_null = slope * (i_lo + 2) <= SKIP
                    chunks.append((ic * 1024, 1024, jtl, do_null))

                def ot_dst(ici, h=h, parity=parity):
                    r0 = 0 if parity == 0 else 64
                    return ot[r0:r0 + 64, h // 2, ici * 1024:(ici + 1) * 1024]
                attn_head(kbuf_ap=kt[0:69, 0:2080], qbuf_ap=qt[0:69, 0:2048],
                          chunks=chunks, parity=parity, ot_dst_fn=ot_dst, gate_tok=tq_last)

        t_ot_all = st.pE_free
        wo_dma = {}
        for c in range(8):
            wo_dma[c] = dma(G, wres[:, c, 0:1024], W["l1_wo"][c * 128:(c + 1) * 128, :],
                            waits=[t_ot_all if c == 0 else None])

        def evac_z3(it, ps, mm):
            return em(V, lambda e, ps=ps, it=it: e.tensor_tensor(
                resid[:, it, :], resid[:, it, :], ps[:, 0:1024], ALU.add), waits=[mm], inc=True)

        ztoks = []
        for it in range(16):
            use_pC = (it % 2 == 0)
            ps = pC if use_pC else pD
            ps_free = st.pC_free if use_pC else st.pD_free
            mm = None
            for c in range(8):
                for half in range(2):
                    mm = em(T, lambda e, ps=ps, c=c, it=it, half=half: e.matmul(
                        ps[:, half * 512:(half + 1) * 512], ot[:, c, it * 128:(it + 1) * 128],
                        wres[:, c, half * 512:(half + 1) * 512], start=(c == 0), stop=(c == 7)),
                        waits=[ps_free, t_psz, wo_dma[c], t_ot_all],
                        inc=(c == 7 and half == 1))
            st.wres_free = mm
            t_e = evac_z3(it, ps, mm)
            ztoks.append(t_e)
            if use_pC:
                st.pC_free = t_e
            else:
                st.pD_free = t_e
        out_toks = ln_apply_inplace(ln_stats(ztoks))
        st.spill_toks = []
        if 4 in stages:
            for q in range(4):
                w = [out_toks[i] for i in range(4 * q, 4 * q + 4)]
                t_sp = dma(SY, xout_sp.rearrange("(t p) d -> p t d", p=128)[:, 4 * q:4 * q + 4, :],
                           resid[:, 4 * q:4 * q + 4, :], waits=w)
                st.spill_toks.append(t_sp)
        return out_toks

    def s4_rotary(t_nq):
        toks = {}
        for which, tokb, tc, ts_ in (("q", qtok, 0, 1), ("k", ktok, 2, 3)):
            last = None
            for g8 in range(2):
                tv = tokb[:, g8 * 8:(g8 + 1) * 8, :].rearrange("p t (s c) -> p t s c", c=64)
                sv = stage_b[:, :].rearrange("p (t s c) -> p t s c", s=2, c=64)

                def rv(tab, g8=g8):
                    return rotc[:, tab, g8 * 8:(g8 + 1) * 8, :] \
                        .rearrange("p t (o c) -> p t o c", o=1).broadcast_to([128, 8, 2, 64])
                t_a = em(V, lambda e, tv=tv, sv=sv: e.tensor_scalar(
                    sv[:, :, :, 0:32], tv[:, :, :, 32:64], -1.0, None, ALU.mult),
                    waits=[t_nq[which], st.stageb_free])
                t_b = em(V, lambda e, tv=tv, sv=sv: e.tensor_copy(
                    sv[:, :, :, 32:64], tv[:, :, :, 0:32]))
                rvs = rv(ts_)
                rvc = rv(tc)
                t_c = em(V, lambda e, sv=sv, rvs=rvs: e.tensor_tensor(
                    sv[:, :, :, :], sv[:, :, :, :], rvs, ALU.mult),
                    waits=[t_a, t_b, t_rot])
                t_d = em(V, lambda e, tv=tv, rvc=rvc: e.tensor_tensor(
                    tv[:, :, :, :], tv[:, :, :, :], rvc, ALU.mult), waits=[t_rot])
                last = em(V, lambda e, tv=tv, sv=sv: e.tensor_tensor(
                    tv[:, :, :, :], tv[:, :, :, :], sv[:, :, :, :], ALU.add),
                    waits=[t_c, t_d], inc=True)
                st.stageb_free = last
            toks[which] = last
        return toks

    def s4_attention(seq, l, rtoks):
        t_arena = init_attn_arena()
        t_qsb = bcast_row(qs_bc[:, :], W["f1_qs"][l], waits=[st.scb_free])
        t_ksb = bcast_row(ks_bc[:, :], W["f1_ks"][l])
        t_xt = build_xt(rtoks, shift=True)
        for pair in range(4):
            _, t_nq = qkv_pair(
                W["f1_wqkv"][l][:, pair * 128:(pair + 1) * 128],
                W["f1_wqkv"][l][:, 512 + pair * 128:512 + (pair + 1) * 128],
                W["f1_wqkv"][l][:, 1024 + pair * 128:1024 + (pair + 1) * 128],
                t_xt, t_arena, pair == 0, t_qsb, t_ksb)
            t_rq = s4_rotary(t_nq)
            for hh in range(2):
                tq_last = None
                for g8 in range(2):
                    srcs = [qtok[:, g8 * 8 + i, hh * 64:hh * 64 + 64] for i in range(8)]

                    def wq4(e, g8=g8):
                        return e.tensor_copy(qt[0:64, g8 * 1024:(g8 + 1) * 1024],
                                             pF[0:64, 0:1024])
                    tq_last = trans_group(srcs, [wq4], waits=[t_rq["q"], st.qk_free], n_out=64)
                    srcs = [ktok[:, g8 * 8 + i, hh * 64:hh * 64 + 64] for i in range(8)]

                    def wk4(e, g8=g8):
                        return e.tensor_copy(kt[0:64, g8 * 1024:(g8 + 1) * 1024],
                                             pF[0:64, 0:1024])
                    tq_last = trans_group(srcs, [wk4], waits=[t_rq["k"], st.qk_free], n_out=64)
                chunks = []
                for w in range(4):
                    jtl = []
                    for jb in [4 * w] + [j for j in range(max(0, 4 * w - 4), 4 * w + 4)
                                         if j != 4 * w]:
                        col_lo = max(0, 128 * jb - 512 * w)
                        col_hi = min(512, 128 * jb + 640 - 512 * w)
                        mask = "mleft" if jb < 4 * w else "m0"
                        jtl.append((jb, col_lo, col_hi - col_lo, mask))
                    chunks.append((512 * w, 512, jtl, False))

                def ot_dst(ici, pair=pair, hh=hh):
                    r0 = 0 if hh == 0 else 64
                    return ot[r0:r0 + 64, pair, ici * 512:(ici + 1) * 512]
                attn_head(kbuf_ap=kt[0:64, 0:2048], qbuf_ap=qt[0:64, 0:2048],
                          chunks=chunks, parity=hh, ot_dst_fn=ot_dst, gate_tok=tq_last,
                          exp_bias=-12.0)
        t_ot_all = st.pE_free
        wo_dma = {}
        for c in range(4):
            wo_dma[c] = dma(G, wres[:, c, 0:1024], W["f1_wo"][l][c * 128:(c + 1) * 128, :],
                            waits=[t_ot_all if c == 0 else None,
                                   st.wres_free if c == 0 else None])
        ztoks = []
        for it in range(16):
            use_pC = (it % 2 == 0)
            ps = pC if use_pC else pD
            ps_free = st.pC_free if use_pC else st.pD_free
            mm = None
            for c in range(4):
                for half in range(2):
                    mm = em(T, lambda e, ps=ps, c=c, it=it, half=half: e.matmul(
                        ps[:, half * 512:(half + 1) * 512], ot[:, c, it * 128:(it + 1) * 128],
                        wres[:, c, half * 512:(half + 1) * 512], start=(c == 0), stop=(c == 3)),
                        waits=[ps_free, t_psz, wo_dma[c], t_ot_all],
                        inc=(c == 3 and half == 1))
            st.wres_free = mm
            t_e = em(V, lambda e, ps=ps, it=it: e.tensor_tensor(
                resid[:, it, :], resid[:, it, :], ps[:, 0:1024], ALU.add),
                waits=[mm, st.spill_toks[it // 4] if l == 0 else None], inc=True)
            ztoks.append(t_e)
            if use_pC:
                st.pC_free = t_e
            else:
                st.pD_free = t_e
        return ztoks

    def load_colvec(dst, src1d, r, waits=()):
        t_l = dma(G, stage_b2[0:r, 0:128], src1d.rearrange("(r p) -> r p", r=r),
                  waits=list(waits) + [st.stageb2_free])

        def wr(e, dst=dst, r=r):
            return e.tensor_copy(dst[:, 0:r], pF[0:128, 0:r])
        t = trans_group([stage_b2[0:r, 0:128]], [wr], waits=[t_l])
        st.stageb2_free = t
        return t

    def s4_ffn(seq, l, rtoks):
        t_xt = build_xt(rtoks, shift=True)
        t_gc = load_colvec(g_c, W["f1_lng"][l], 8)
        t_bc = load_colvec(b_c, W["f1_lnb"][l], 8)
        t_b1a = load_colvec(b1a_c, W["f1_b1a"][l], 22)
        t_b1g = load_colvec(b1g_c, W["f1_b1g"][l], 22)
        t_b2 = bcast_row(bc1[:, :], W["f1_b2"][l], waits=[st.bc1_free, rtoks[0]])
        t_app = None
        for hf in range(2):
            mm_s = mm_q = None
            for w2 in range(2):
                for c in range(8):
                    mm_s = em(T, lambda e, w2=w2, c=c, hf=hf: e.matmul(
                        pC[0:1, w2 * 512:(w2 + 1) * 512], ones_c[:, 0:1],
                        xt[:, c, hf * 1024 + w2 * 512:hf * 1024 + (w2 + 1) * 512],
                        start=(c == 0), stop=(c == 7)),
                        waits=[t_xt, t_ones, st.pC_free if w2 == 0 and c == 0 else None,
                               t_psz], inc=(c == 7))
            for w2 in range(2):
                for c in range(8):
                    t_sq = em(S, lambda e, w2=w2, c=c, hf=hf: e.activation(
                        stage_b2[:, (c % 2) * 512:(c % 2 + 1) * 512],
                        xt[:, c, hf * 1024 + w2 * 512:hf * 1024 + (w2 + 1) * 512],
                        AF.Square), waits=[t_xt, st.stageb2_free], inc="ss")
                    mm_q = em(T, lambda e, w2=w2, c=c: e.matmul(
                        pD[0:1, w2 * 512:(w2 + 1) * 512], ones_c[:, 0:1],
                        stage_b2[:, (c % 2) * 512:(c % 2 + 1) * 512],
                        start=(c == 0), stop=(c == 7)),
                        waits=[t_sq, st.pD_free if w2 == 0 and c == 0 else None, t_psz],
                        inc=True)
                    st.stageb2_free = mm_q
            t_mu = em(V, lambda e: e.tensor_scalar(rowf[0:1, 0:1024], pC[0:1, 0:1024],
                                                   1.0 / 1024.0, None, ALU.mult),
                      waits=[mm_s, st.rowf_free])
            t_ss = em(V, lambda e: e.tensor_copy(stage_f[0:1, 0:1024], pD[0:1, 0:1024]),
                      waits=[mm_q, st.stagef_free])
            st.pC_free = t_mu
            st.pD_free = t_ss
            t_m2 = em(V, lambda e: e.tensor_tensor(rowb[0:1, 0:1024], rowf[0:1, 0:1024],
                                                   rowf[0:1, 0:1024], ALU.mult),
                      waits=[t_mu])
            t_va = em(V, lambda e: e.scalar_tensor_tensor(
                stage_f[0:1, 0:1024], stage_f[0:1, 0:1024], 1.0 / 1024.0, rowb[0:1, 0:1024],
                ALU.mult, ALU.subtract), waits=[t_ss, t_m2])
            t_ve = em(V, lambda e: e.tensor_scalar(stage_f[0:1, 0:1024], stage_f[0:1, 0:1024],
                                                   1e-5, None, ALU.add), waits=[t_va], inc=True)
            t_sg = em(S, lambda e: e.activation(stage_f[0:1, 0:1024], stage_f[0:1, 0:1024],
                                                AF.Sqrt), waits=[t_ve], inc="ss")
            t_rc = em(V, lambda e: e.reciprocal(stage_f[0:1, 0:1024], stage_f[0:1, 0:1024]),
                      waits=[t_sg])
            t_mb = em(V, lambda e: e.tensor_copy(rowb[0:1, 0:1024], rowf[0:1, 0:1024]),
                      waits=[t_m2], inc=True)
            t_rb2 = em(V, lambda e: e.tensor_copy(rbt[0:1, 0:1024], stage_f[0:1, 0:1024]),
                      waits=[t_rc, st.rbt_w_free], inc=True)
            st.stagef_free = t_rb2
            st.rowf_free = t_mb
            t_bm = t_br = None
            for w2 in range(2):
                t1 = em(T, lambda e, w2=w2: e.matmul(
                    pC[:, w2 * 512:(w2 + 1) * 512], ones_r[0:1, 0:128],
                    rowb[0:1, w2 * 512:(w2 + 1) * 512], start=True, stop=True),
                    waits=[t_mb, st.pC_free, t_ones, t_psz], inc=True)
                t_bm = em(V, lambda e, w2=w2: e.tensor_copy(
                    stage_b[:, w2 * 512:(w2 + 1) * 512], pC[:, w2 * 512:(w2 + 1) * 512]),
                    waits=[t1, st.stageb_free], inc=True)
                t2 = em(T, lambda e, w2=w2: e.matmul(
                    pD[:, w2 * 512:(w2 + 1) * 512], ones_r[0:1, 0:128],
                    rbt[0:1, w2 * 512:(w2 + 1) * 512], start=True, stop=True),
                    waits=[t_rb2, st.pD_free, t_ones, t_psz], inc=True)
                st.rbt_free = t2
                t_br = em(V, lambda e, w2=w2: e.tensor_copy(
                    stage_b2[:, w2 * 512:(w2 + 1) * 512], pD[:, w2 * 512:(w2 + 1) * 512]),
                    waits=[t2, st.stageb2_free], inc=True)
            st.pC_free = t_bm
            st.pD_free = t_br
            for c in range(8):
                ta = em(V, lambda e, c=c, hf=hf: e.tensor_tensor(
                    xt[:, c, hf * 1024:(hf + 1) * 1024], xt[:, c, hf * 1024:(hf + 1) * 1024],
                    stage_b[:, :], ALU.subtract), waits=[t_bm, t_br])
                tb = em(V, lambda e, c=c, hf=hf: e.tensor_tensor(
                    xt[:, c, hf * 1024:(hf + 1) * 1024], xt[:, c, hf * 1024:(hf + 1) * 1024],
                    stage_b2[:, :], ALU.mult), waits=[ta])
                t_app = em(V, lambda e, c=c, hf=hf: e.tensor_scalar(
                    xt[:, c, hf * 1024:(hf + 1) * 1024], xt[:, c, hf * 1024:(hf + 1) * 1024],
                    g_c[:, c:c + 1], b_c[:, c:c + 1], ALU.mult, ALU.add),
                    waits=[tb, t_gc, t_bc], inc=True)
            st.stageb_free = t_app
            st.stageb2_free = t_app
        st.xt_free = t_app
        ztoks = {}
        w1_free = [st.wres_free, st.wres_free]
        for hgrp in range(2):
            t_fb = None
            for hl in range(11):
                hc = hgrp * 11 + hl
                wb = w1buf[hc % 2]
                t_wa = dma(G, wb[:, :, 0:128],
                           W["f1_w1a"][l][:, hc * 128:(hc + 1) * 128]
                           .rearrange("(c p) n -> p c n", p=128), waits=[w1_free[hc % 2]])
                t_wg = dma(G, wb[:, :, 128:256],
                           W["f1_w1g"][l][:, hc * 128:(hc + 1) * 128]
                           .rearrange("(c p) n -> p c n", p=128))
                pG = pD if hl % 2 == 0 else pE
                for tc in range(4):
                    par = tc % 2
                    mm_g = mm_a = None
                    for c in range(8):
                        mm_g = em(T, lambda e, wb=wb, tc=tc, c=c, par=par, pG=pG: e.matmul(
                            pG[:, par * 512:(par + 1) * 512], wb[:, c, 128:256],
                            xt[:, c, tc * 512:(tc + 1) * 512], start=(c == 0), stop=(c == 7)),
                            waits=[t_wg, t_app, t_psz,
                                   (st.pD_free if hl % 2 == 0 else st.pE_free)
                                   if c == 0 else None], inc=(c == 7))
                    t_gel = em(S, lambda e, tc=tc, par=par, hc=hc, pG=pG: e.activation(
                        stage_b[:, par * 512:(par + 1) * 512], pG[:, par * 512:(par + 1) * 512],
                        AF.Gelu, bias=b1g_c[:, hc:hc + 1]),
                        waits=[mm_g, t_b1g, st.stageb_free], inc="ss")
                    if hl % 2 == 0:
                        st.pD_free = t_gel
                    else:
                        st.pE_free = t_gel
                    for c in range(8):
                        mm_a = em(T, lambda e, wb=wb, tc=tc, c=c, par=par: e.matmul(
                            pC[:, par * 512:(par + 1) * 512], wb[:, c, 0:128],
                            xt[:, c, tc * 512:(tc + 1) * 512], start=(c == 0), stop=(c == 7)),
                            waits=[t_wa, t_app, t_psz,
                                   st.pC_free if c == 0 else None], inc=(c == 7))
                    t_fb = em(V, lambda e, tc=tc, par=par, hl=hl, hc=hc: e.scalar_tensor_tensor(
                        fbuf[:, hl, tc * 512:(tc + 1) * 512], pC[:, par * 512:(par + 1) * 512],
                        b1a_c[:, hc:hc + 1], stage_b[:, par * 512:(par + 1) * 512],
                        ALU.add, ALU.mult),
                        waits=[mm_a, t_gel, t_b1a,
                               st.fbuf_free if hl == 0 and tc == 0 else None], inc=True)
                    st.stageb_free = t_fb
                    st.pC_free = t_fb
                w1_free[hc % 2] = t_fb
            for wg0, wgn in ((0, 6), (6, 5)):
                w2d = {}
                for i in range(wgn):
                    hc = hgrp * 11 + wg0 + i
                    w2d[i] = dma(G, w2res[:, i, :],
                                 W["f1_w2"][l][hc * 128:(hc + 1) * 128, :],
                                 waits=[st.w2res_free if i == 0 else None])
                for t8 in range(16):
                    use_pC = (t8 % 2 == 0)
                    ps = pC if use_pC else pD
                    ps_free = st.pC_free if use_pC else st.pD_free
                    mm = None
                    for i in range(wgn):
                        for half in range(2):
                            mm = em(T, lambda e, ps=ps, i=i, t8=t8, half=half, wg0=wg0,
                                    wgn=wgn: e.matmul(
                                ps[:, half * 512:(half + 1) * 512],
                                fbuf[:, wg0 + i, t8 * 128:(t8 + 1) * 128],
                                w2res[:, i, half * 512:(half + 1) * 512],
                                start=(i == 0), stop=(i == wgn - 1)),
                                waits=[ps_free, t_psz, w2d[i], t_fb],
                                inc=(i == wgn - 1 and half == 1))
                    st.w2res_free = mm
                    st.fbuf_free = mm
                    t_e = em(V, lambda e, ps=ps, t8=t8: e.tensor_tensor(
                        resid[:, t8, :], resid[:, t8, :], ps[:, 0:1024], ALU.add),
                        waits=[mm, rtoks[t8] if hgrp == 0 and wg0 == 0 else ztoks.get(t8)],
                        inc=True)
                    ztoks[t8] = t_e
                    if use_pC:
                        st.pC_free = t_e
                    else:
                        st.pD_free = t_e
        out = []
        for t8 in range(16):
            out.append(em(V, lambda e, t8=t8: e.tensor_tensor(
                resid[:, t8, :], resid[:, t8, :], bc1[:, :], ALU.add),
                waits=[ztoks[t8], t_b2], inc=True))
        st.bc1_free = out[-1]
        st.vaug_free = st.w2res_free
        st.qk_free = st.w2res_free
        st.eb_free = [st.w2res_free, st.w2res_free]
        st.wres_free = st.w2res_free
        st.xt_free = t_fb
        return out

    def run_stage4(seq, rtoks):
        for l in (0, 1):
            ztoks = s4_attention(seq, l, rtoks)
            rtoks = s4_ffn(seq, l, ztoks)
        toks = []
        for it in range(16):
            xb = xosb if it % 2 == 0 else stage_f
            gate = toks[it - 2] if it >= 2 else (st.rbt_free if it % 2 == 0
                                                 else st.stagef_free)
            t_l = dma(SY, xb[:, :], xout_sp.rearrange("(t p) d -> p t d", p=128)[:, it, :],
                      waits=[st.spill_toks[it // 4], gate,
                             st.rbt_free if it == 0 else None])
            toks.append(em(V, lambda e, it=it, xb=xb: e.tensor_tensor(
                resid[:, it, :], resid[:, it, :], xb[:, :], ALU.add),
                waits=[t_l, rtoks[it]], inc=True))
        return ln_apply_inplace(ln_stats(toks, dummy=fbuf[:, 0, 0:1024]))

    for seq in (0, 1):
        rtoks = load_x(seq)
        if 1 in stages:
            rtoks = run_stage1(seq, rtoks)
        if 2 in stages:
            rtoks = run_stage2(seq, rtoks)
        if 3 in stages:
            rtoks = run_stage3(seq, rtoks)
        if 4 in stages:
            rtoks = run_stage4(seq, rtoks)
        store_out(seq, rtoks)

    mk.wait_only(SY, final_toks)
    mk.run_block()
    n_ins = {e: len(mk.prog[e]) for e in ENGS}
    print("program sizes:", n_ins, "total", sum(n_ins.values()))
    return nc


def make_consts():
    c = {}
    c["c_ident"] = np.eye(128, dtype=ml_dtypes.bfloat16)
    i = np.arange(128)
    c["c_m0"] = (i[None, :] >= i[:, None]).astype(ml_dtypes.bfloat16)
    c["c_mleft"] = (i[None, :] <= i[:, None]).astype(ml_dtypes.bfloat16)
    jeff = np.concatenate([np.arange(2048, dtype=np.float64), [-2.0, -1.0]])
    al = np.zeros((16, 5, 2050), dtype=ml_dtypes.bfloat16)
    for h in range(16):
        r = ALIBI16[h] * jeff
        for t in range(4):
            hi = r.astype(ml_dtypes.bfloat16)
            al[h, t] = hi
            r = r - hi.astype(np.float64)
        al[h, 4] = 1.0
    c["c_alibi"] = al
    ci = np.zeros((16, 2048), dtype=ml_dtypes.bfloat16)
    for h in range(16):
        ci[h] = (-(ALIBI16[h] * np.arange(2048, dtype=np.float64) + 8.0)).astype(ml_dtypes.bfloat16)
    c["c_ci"] = ci
    DH, WIN, N = 64, 512, 2048
    half = np.arange(0, DH, 2, dtype=np.float64)
    inv_freq = 1.0 / 10000.0 ** (half / DH)
    t = np.arange(N, dtype=np.float64)
    freqs = t[:, None] * inv_freq[None, :]
    freqs = np.concatenate([freqs, freqs], axis=-1)
    base = (half + 0.4 * DH) / (1.4 * DH)
    power = (t - N // 2) / (WIN // 2)
    scale = base[None, :] ** power[:, None]
    scale = np.concatenate([scale, scale], axis=-1)
    cos, sin = np.cos(freqs), np.sin(freqs)
    rot = np.zeros((4, 2048, 64), dtype=ml_dtypes.bfloat16)
    rot[0] = (8.0 * cos * scale).astype(ml_dtypes.bfloat16)
    rot[1] = (8.0 * sin * scale).astype(ml_dtypes.bfloat16)
    rot[2] = (cos / scale).astype(ml_dtypes.bfloat16)
    rot[3] = (sin / scale).astype(ml_dtypes.bfloat16)
    c["c_rot"] = rot
    return c


def pad_w1(inputs):
    w1 = np.asarray(inputs["L1_ffn_w1"])
    b1 = np.asarray(inputs["L1_ffn_b1"])
    w2 = np.asarray(inputs["L1_ffn_w2"])
    HID = 2730
    w1a = np.zeros((2, 1024, 2816), np.float32)
    w1g = np.zeros((2, 1024, 2816), np.float32)
    w1a[:, :, :HID] = w1[:, :, :HID]
    w1g[:, :, :HID] = w1[:, :, HID:]
    b1a = np.zeros((2, 2816), np.float32)
    b1g = np.zeros((2, 2816), np.float32)
    b1a[:, :HID] = b1[:, :HID]
    b1g[:, :HID] = b1[:, HID:]
    w2p = np.zeros((2, 2816, 1024), np.float32)
    w2p[:, :HID, :] = w2
    return dict(w1a_pad=w1a, w1g_pad=w1g, b1a_pad=b1a, b1g_pad=b1g, w2_pad=w2p)


def kernel(**inputs):
    from concourse.bass_utils import run_bass_kernel_spmd
    x = np.ascontiguousarray(np.asarray(inputs["L0_attn_x"]))
    consts = make_consts()
    pads = pad_w1(inputs)
    nc = build(stages=(1, 2, 3, 4))
    base = {}
    for k in inputs:
        if k not in ("L0_attn_x", "L1_ffn_w1", "L1_ffn_b1", "L1_ffn_w2"):
            base[k] = np.ascontiguousarray(np.asarray(inputs[k]))
    base.update(consts)
    base.update(pads)
    in_maps = []
    for core in range(8):
        m = dict(base)
        m["x"] = np.ascontiguousarray(x[2 * core:2 * core + 2])
        in_maps.append(m)
    res = run_bass_kernel_spmd(nc, in_maps, core_ids=list(range(8)))
    out = np.concatenate([np.asarray(r["out"]) for r in res.results], axis=0)
    return out.astype(np.float32)


# version 04 (final)
# speedup vs baseline: 1.2172x; 1.0070x over previous
import math
import numpy as np
import ml_dtypes
import concourse.bass as bass
import concourse.mybir as mybir
from concourse.bass_utils import run_bass_kernel_spmd

F32 = mybir.dt.float32
BF16 = mybir.dt.bfloat16
I8 = mybir.dt.int8
AF = mybir.ActivationFunctionType
ALU = mybir.AluOpType
AX = mybir.AxisListType

RND = 12582912.0
T, V, S, G, SY = "T", "V", "S", "G", "SY"
ENGS = (T, V, S, G, SY)
_s0 = 2.0 ** (-2.0 ** -1.0)
ALIBI16 = [_s0 * _s0 ** i for i in range(16)]
SKIP = 98.0
DBG = {}


class MK:
    DPOOL = 24

    def __init__(self, nc):
        self.nc = nc
        self.prog = {e: [] for e in ENGS}
        self.cnt = {}
        self.sem = {}
        self.mark = {e: {} for e in ENGS}
        for name in ("st", "sv", "ss"):
            self.sem[name] = nc.alloc_semaphore(name)
            self.cnt[name] = 0
        self.esem = {T: "st", V: "sv", S: "ss"}
        self.dpool = {}
        self.didx = {G: 0, SY: 0}
        for eng, pfx in ((G, "g"), (SY, "y")):
            for i in range(self.DPOOL):
                name = f"d{pfx}{i}"
                self.sem[name] = nc.alloc_semaphore(name)
                self.cnt[name] = 0
            self.dpool[eng] = [f"d{pfx}{i}" for i in range(self.DPOOL)]

    def emit(self, eng, fn, waits=(), inc=None, inc_n=1):
        if DBG.get("serialize") and eng in (T, V, S):
            own = self.esem[eng]
            waits = list(waits) + [(own, self.cnt[own])]
            if inc is None or inc is False:
                inc = True
        wl = []
        m = self.mark[eng]
        for tok in waits:
            if tok is None:
                continue
            semn, val = tok
            assert val <= self.cnt[semn], f"future token {semn}={val}>{self.cnt[semn]}"
            if m.get(semn, 0) < val:
                m[semn] = val
                wl.append((self.sem[semn], val))
        tok = None
        semh = None
        n = 0
        if inc is None and eng in (V, S):
            inc = True
        if inc is True:
            inc = self.esem[eng]
        if inc:
            self.cnt[inc] += inc_n
            tok = (inc, self.cnt[inc])
            semh = self.sem[inc]
            n = inc_n

        if DBG.get("trace_em") and eng == DBG.get("trace_eng", "T"):
            print(f"[{eng} #{len(self.prog[eng])}] waits={[(s, v) for (s, v) in [(tok2[0], tok2[1]) for tok2 in []]] }", end="")
            print(f" wl={[(h.name if hasattr(h, 'name') else h, v) for (h, v) in wl]} inc={inc}")
        def thunk(e, wl=wl, fn=fn, semh=semh, n=n):
            for s, v in wl:
                e.wait_ge(s, v)
            ins = fn(e)
            if semh is not None:
                ins.then_inc(semh, n)
        self.prog[eng].append(thunk)
        return tok

    def dma(self, eng, out, in_, waits=()):
        semn = self.dpool[eng][self.didx[eng] % self.DPOOL]
        self.didx[eng] += 1
        waits = list(waits) + ([(semn, self.cnt[semn])] if self.cnt[semn] else [])
        try:
            nm = out.tensor.name
        except Exception:
            nm = None
        if nm is not None:
            if not hasattr(self, "last_dma"):
                self.last_dma = {}
            waits.append(self.last_dma.get(nm))
        tok = self.emit(eng, lambda e: e.dma_start(out, in_), waits=waits, inc=semn, inc_n=16)
        if nm is not None:
            self.last_dma[nm] = tok
        return tok

    def wait_only(self, eng, toks):
        m = self.mark[eng]
        wl = []
        for tok in toks:
            if tok is None:
                continue
            semn, val = tok
            assert val <= self.cnt[semn]
            if m.get(semn, 0) < val:
                m[semn] = val
                wl.append((self.sem[semn], val))
        if wl:
            def thunk(e, wl=wl):
                for s, v in wl:
                    e.wait_ge(s, v)
            self.prog[eng].append(thunk)

    def run_block(self):
        nc = self.nc
        with nc.Block() as blk:
            @blk.tensor
            def _(e):
                for f in self.prog[T]:
                    f(e)

            @blk.vector
            def _(e):
                for f in self.prog[V]:
                    f(e)

            @blk.scalar
            def _(e):
                for f in self.prog[S]:
                    f(e)

            @blk.gpsimd
            def _(e):
                for f in self.prog[G]:
                    f(e)

            @blk.sync
            def _(e):
                for f in self.prog[SY]:
                    f(e)


def build(stages=(1, 2, 3, 4)):
    stages = set(stages)
    nc = bass.Bass("TRN2", target_bir_lowering=False, detect_race_conditions=not DBG.get("no_rd", True))
    mk = MK(nc)
    em = mk.emit
    dma = mk.dma

    def din(name, shape, dt=F32):
        return nc.dram_tensor(name, shape, dt, kind="ExternalInput").ap()

    x_in = din("x", [2, 2048, 1024])
    out_t = nc.dram_tensor("out", [2, 2048, 1024], F32, kind="ExternalOutput").ap()
    W = dict(
        l0_wq=din("L0_attn_Wq", [1024, 1024]), l0_wk=din("L0_attn_Wk", [1024, 1024]),
        l0_wv=din("L0_attn_Wv", [1024, 1024]), l0_wo=din("L0_attn_Wo", [1024, 1024]),
        l0_bo=din("L0_attn_bo", [1024]),
        f0_wq=din("L0_ffn_weight_q", [1024, 1024], I8), f0_ws=din("L0_ffn_weight_scale", [1024]),
        f0_b=din("L0_ffn_bias", [1024]),
        l1_g=din("L1_attn_gamma", [1024]), l1_wq=din("L1_attn_w_q", [1024, 1024]),
        l1_wkv=din("L1_attn_w_kv", [1024, 2048]), l1_nkv=din("L1_attn_null_kv", [16, 4, 64]),
        l1_qs=din("L1_attn_q_scale", [64]), l1_ks=din("L1_attn_k_scale", [64]),
        l1_wo=din("L1_attn_w_out", [1024, 1024]),
        f1_wqkv=din("L1_ffn_wqkv", [2, 1024, 1536]), f1_qs=din("L1_ffn_q_scale", [2, 64]),
        f1_ks=din("L1_ffn_k_scale", [2, 64]), f1_wo=din("L1_ffn_wo", [2, 512, 1024]),
        f1_lng=din("L1_ffn_ln_g", [2, 1024]), f1_lnb=din("L1_ffn_ln_b", [2, 1024]),
        f1_w1a=din("w1a_pad", [2, 1024, 2816]), f1_w1g=din("w1g_pad", [2, 1024, 2816]),
        f1_b1a=din("b1a_pad", [2, 2816]), f1_b1g=din("b1g_pad", [2, 2816]),
        f1_w2=din("w2_pad", [2, 2816, 1024]), f1_b2=din("L1_ffn_b2", [2, 1024]),
        c_ident=din("c_ident", [128, 128], BF16),
        c_m0=din("c_m0", [128, 128], BF16),
        c_mleft=din("c_mleft", [128, 128], BF16),
        c_alibi=din("c_alibi", [16, 5, 2050], BF16),
        c_ci=din("c_ci", [16, 2048], BF16),
        c_rot=din("c_rot", [4, 2048, 64], BF16),
    )
    xout_sp = nc.dram_tensor("xouter_scratch", [2048, 1024], F32, kind="Internal").ap()

    sb = nc.alloc_sbuf_tensor

    def at(name, shape, dt, off):
        return nc.alloc_sbuf_tensor_at(name, shape, dt, offset=off)

    resid = sb("resid", [128, 16, 1024], F32)
    xt = sb("xt", [128, 8, 2048], BF16)
    stage_b = sb("stage_b", [128, 1024], BF16)
    stage_b2 = sb("stage_b2", [128, 1024], BF16)
    stage_f = sb("stage_f", [128, 1024], F32)
    wld = sb("wld", [128, 3072], BF16)
    bc1 = sb("bc1", [128, 1024], F32)
    rotc = sb("rotc", [128, 4, 16, 64], BF16)
    rowf = sb("rowf", [128, 1024], F32)
    rowb = sb("rowb", [128, 1024], BF16)
    ones_r = sb("ones_r", [1, 128], BF16)
    ones_c = sb("ones_c", [128, 2], BF16)
    m0_t = sb("m0_t", [128, 128], BF16)
    mleft_t = sb("mleft_t", [128, 128], BF16)
    ident_t = sb("ident_t", [128, 128], BF16)
    stats = sb("stats", [128, 4, 16], F32)
    statw = sb("statw", [128, 32], F32)
    sqb = sb("sqb", [128, 16, 4], F32)
    knt = sb("knt", [64, 32], BF16)
    nkst = sb("nkst", [32, 64], F32)
    vnaug = sb("vnaug", [2, 2, 128], BF16)
    qs_bc = sb("qs_bc", [128, 64], F32)
    ks_bc = sb("ks_bc", [128, 64], F32)
    b1a_c = sb("b1a_c", [128, 22], F32)
    b1g_c = sb("b1g_c", [128, 22], F32)
    g_c = sb("g_c", [128, 8], F32)
    b_c = sb("b_c", [128, 8], F32)

    U1 = nc.bump_sbuf(61536)[0]
    ot = at("ot", [128, 8, 2048], BF16, U1)
    vaug = at("vaug", [128, 16, 2, 128], BF16, U1 + 32768)
    qt = at("qt", [128, 2048], BF16, U1 + 40960)
    kt = at("kt", [128, 2080], BF16, U1 + 45056)
    ebuf0 = at("ebuf0", [128, 1024], BF16, U1 + 49248)
    ebuf1 = at("ebuf1", [128, 1024], BF16, U1 + 51296)
    wt1 = at("wt1", [128, 8, 128], BF16, U1 + 53344)
    wt2 = at("wt2", [128, 8, 128], BF16, U1 + 55392)
    wt3 = at("wt3", [128, 8, 128], BF16, U1 + 57440)
    rbt = at("rbt", [128, 1024], BF16, U1 + 59488)
    ebuf = [ebuf0, ebuf1]
    fbuf = at("fbuf", [128, 11, 2048], BF16, U1)
    w2res = at("w2res", [128, 6, 1024], BF16, U1 + 45056)
    xosb = at("xosb", [128, 1024], F32, U1 + 57344)
    bc2 = at("bc2", [128, 1024], F32, U1)
    bc3 = at("bc3", [128, 1024], F32, U1 + 4096)

    U2 = nc.bump_sbuf(16384)[0]
    wres = at("wres", [128, 8, 1024], BF16, U2)
    qtok = at("qtok", [128, 16, 128], BF16, U2)
    ktok = at("ktok", [128, 16, 128], BF16, U2 + 4096)
    w1buf0 = at("w1buf0", [128, 8, 256], BF16, U2 + 8192)
    w1buf1 = at("w1buf1", [128, 8, 256], BF16, U2 + 12288)
    w1buf = [w1buf0, w1buf1]
    cbias = sb("cbias", [128, 1], F32)
    print("SBUF remaining bytes/partition:", nc.sbuf_bytes_remaining // 128)

    pA = nc.alloc_psum_tensor("pA", [128, 512], F32)
    pC = nc.alloc_psum_tensor("pC", [128, 1024], F32)
    pD = nc.alloc_psum_tensor("pD", [128, 1024], F32)
    pE = nc.alloc_psum_tensor("pE", [128, 1024], F32)
    pF = nc.alloc_psum_tensor("pF", [128, 1024], BF16)

    class St:
        pA_free = None
        pC_free = None
        pD_free = None
        pE_free = None
        eb_free = [None, None]
        vaug_free = None
        qk_free = None
        xt_free = None
        wres_free = None
        wld_free = None
        scb_free = None
        rbt_free = None
        fbuf_free = None
        stageb_free = None
        stageb2_free = None
        stagef_free = None
        rowf_free = None
        rbt_w_free = None
        w2res_free = None
        bc1_free = None
        spill_toks = [None] * 4
        tr_free = [None, None, None]
    st = St()

    t_ident = dma(SY, ident_t[:, :], W["c_ident"])
    t_m0 = dma(SY, m0_t[:, :], W["c_m0"])
    t_mleft = dma(SY, mleft_t[:, :], W["c_mleft"])
    t_rot = dma(SY, rotc[:, :, :, :], W["c_rot"].rearrange("t (i p) d -> p t i d", p=128))
    em(V, lambda e: e.memset(ones_r[:, :], 1.0))
    t_ones = em(V, lambda e: e.memset(ones_c[:, :], 1.0), inc=True)
    for p in (pA, pC, pD):
        em(V, lambda e, p=p: e.memset(p[:, :], 0.0))
    t_psz = em(V, lambda e: e.memset(pE[:, :], 0.0), inc=True)
    em(V, lambda e: e.memset(cbias[:, :], -12.0))
    em(V, lambda e: e.memset(vnaug[:, :, :], 0.0))
    em(V, lambda e: e.memset(vnaug[:, 0, 64:66], 1.0))
    t_vn0 = em(V, lambda e: e.memset(vnaug[:, 1, 32:34], 1.0), inc=True)

    final_toks = []

    def init_attn_arena():
        em(V, lambda e: e.memset(vaug[:, :, :, :], 0.0), waits=[st.vaug_free])
        em(V, lambda e: e.memset(vaug[:, :, 0, 64:66], 1.0))
        t_a = em(V, lambda e: e.memset(vaug[:, :, 1, 32:34], 1.0), inc=True)
        t_b = em(V, lambda e: e.memset(qt[64:68, 0:2048], 1.0), waits=[st.qk_free], inc=True)
        return t_b, t_a

    def bcast_row(dst, src_row, waits=()):
        return dma(SY, dst, src_row.unsqueeze(0).partition_broadcast(128), waits=waits)

    def trans_group(srcs, writers, waits=(), n_out=128):
        assert len(srcs) <= 8
        t_t = None
        for i, s in enumerate(srcs):
            kp = s.shape[0]
            ncol = s.shape[-1]
            t_t = em(T, lambda e, s=s, i=i, kp=kp, ncol=ncol: e.transpose(
                pF[0:ncol, i * 128:i * 128 + kp], s, ident_t[0:kp, 0:kp]),
                waits=list(waits) + [st.tr_free[0], t_ident],
                inc=(i == len(srcs) - 1))
        t_e = None
        for w in writers:
            t_e = em(V, lambda e, w=w: w(e), waits=[t_t], inc=True)
        st.tr_free[0] = t_e
        return t_e

    def ln_stats(tile_toks, dummy=None):
        if dummy is None:
            dummy = stage_f[:, :]
        t_red = None
        for it in range(16):
            w = [tile_toks[it]] if tile_toks else []
            em(V, lambda e, it=it: e.tensor_reduce(stats[:, 0, it:it + 1], resid[:, it, :],
                                                   AX.X, ALU.add), waits=w, inc=True)
        for it in range(16):
            t_red = em(V, lambda e, it=it, dummy=dummy: e.scalar_tensor_tensor(
                dummy, resid[:, it, :], 1.0, resid[:, it, :], ALU.mult, ALU.mult,
                accum_out=stats[:, 1, it:it + 1]), inc=True)
        t_mu = em(V, lambda e: e.tensor_scalar(stats[:, 2, :], stats[:, 0, :], 1.0 / 1024.0,
                                               None, ALU.mult), waits=[t_red])
        t_m2 = em(V, lambda e: e.tensor_tensor(statw[:, 0:16], stats[:, 2, :],
                                               stats[:, 2, :], ALU.mult), waits=[t_mu])
        t_va = em(V, lambda e: e.scalar_tensor_tensor(statw[:, 16:32], stats[:, 1, :],
                                                      1.0 / 1024.0, statw[:, 0:16],
                                                      ALU.mult, ALU.subtract), waits=[t_m2])
        t_v = em(V, lambda e: e.tensor_scalar(statw[:, 16:32], statw[:, 16:32], 1e-5, None,
                                              ALU.add), waits=[t_va], inc=True)
        t_sq = em(S, lambda e: e.activation(statw[:, 0:16], statw[:, 16:32], AF.Sqrt),
                  waits=[t_v], inc=True)
        return em(V, lambda e: e.reciprocal(stats[:, 3, :], statw[:, 0:16]),
                  waits=[t_sq], inc=True)

    def ln_apply_inplace(t_rs):
        out = []
        for it in range(16):
            out.append(em(V, lambda e, it=it: e.tensor_scalar(
                resid[:, it, :], resid[:, it, :], stats[:, 2, it:it + 1], stats[:, 3, it:it + 1],
                ALU.subtract, ALU.mult), waits=[t_rs], inc=True))
        return out

    def build_xt(tile_toks, shift=False, pre=None):
        last = None
        t_shift = None
        if shift:
            t_shift = em(V, lambda e: e.memset(xt[:, 4:8, 0:2], 0.0), waits=[st.xt_free], inc=True)
        for it in range(16):
            if pre is not None:
                t_c = pre(it)
            else:
                w = [tile_toks[it]] if tile_toks else []
                t_c = em(V, lambda e, it=it: e.tensor_copy(stage_b[:, :], resid[:, it, :]),
                         waits=w, inc=True)
            srcs = [stage_b[:, c * 128:(c + 1) * 128] for c in range(8)]
            if not shift:
                def w0(e, it=it):
                    return e.tensor_copy(xt[:, 0:8, it * 128:(it + 1) * 128],
                                         pF[:, 0:1024].rearrange("p (c n) -> p c n", n=128))
                writers = [w0]
            else:
                def w0(e, it=it):
                    return e.tensor_copy(xt[:, 0:4, it * 128:(it + 1) * 128],
                                         pF[:, 0:512].rearrange("p (c n) -> p c n", n=128))
                n2 = 128 if it < 15 else 127

                def w1(e, it=it, n2=n2):
                    return e.tensor_copy(
                        xt[:, 4:8, it * 128 + 1:it * 128 + 1 + n2],
                        pF[:, 512:1024].rearrange("p (c n) -> p c n", n=128)[:, :, 0:n2])
                writers = [w0, w1]
            last = trans_group(srcs, writers, waits=[t_c, t_shift, st.xt_free])
        return last

    def load_x(seq):
        toks = []
        prev_stores = final_toks[-4:] if final_toks else []
        for q in range(4):
            src = x_in[seq].rearrange("(t p) d -> p t d", p=128)[:, 4 * q:4 * q + 4, :]
            tk = dma(SY, resid[:, 4 * q:4 * q + 4, :], src, waits=list(prev_stores))
            toks += [tk] * 4
        return toks

    def store_out(seq, tile_toks):
        for q in range(4):
            w = [tile_toks[it] for it in range(4 * q, 4 * q + 4)]
            dst = out_t[seq].rearrange("(t p) d -> p t d", p=128)[:, 4 * q:4 * q + 4, :]
            final_toks.append(dma(SY, dst, resid[:, 4 * q:4 * q + 4, :], waits=w))

    def build_wresT(src_ap_rows, gate=None):
        last = None
        for rc in range(8):
            t_l = dma(G, wld[:, 0:1024], src_ap_rows(rc), waits=[last, gate, st.wld_free])
            srcs = [wld[:, c * 128:(c + 1) * 128] for c in range(8)]

            def w0(e, rc=rc):
                return e.tensor_copy(wres[:, :, rc * 128:(rc + 1) * 128],
                                     pF[:, 0:1024].rearrange("p (c n) -> p c n", n=128))
            last = trans_group(srcs, [w0], waits=[t_l, st.wres_free])
            st.wld_free = last
        return last

    def attn_head(*, kbuf_ap, qbuf_ap, chunks, parity, ot_dst_fn,
                  gate_tok=None, exp_bias=0.0):
        prow = 65 if parity == 0 else 128
        lrow = 64 if parity == 0 else 32
        orows = (0, 64) if parity == 0 else (64, 128)
        enull = stage_b2[0:2, :]
        def pieces(col_lo, col_n):
            out = []
            off = col_lo
            end = col_lo + col_n
            while off < end:
                n = min(end - off, 512 - (off % 512))
                out.append((off, n))
                off += n
            return out

        for ici, (ic_off, ic_n, jt_list, do_null) in enumerate(chunks):
            n_halves = (ic_n + 511) // 512
            seq_banks = []
            for (_, cl, cn, _) in jt_list:
                seq_banks += [off // 512 for off, n in pieces(cl, cn)]
            if do_null:
                seq_banks += list(range(n_halves))
            n_out_mms = len(seq_banks)
            last_of_bank = {}
            for i_b, b in enumerate(seq_banks):
                last_of_bank[b] = i_b
            box = {"oi": 0, "out_last": None}
            touched = set()

            def emit_out(idx, jt, eb, pcs, t_mask):
                for off, n in pcs:
                    bank = off // 512
                    fo = bank not in touched
                    touched.add(bank)
                    oi = box["oi"]
                    box["out_last"] = em(T, lambda e, jt=jt, eb=eb, off=off, n=n,
                                         fo=fo, lo=(last_of_bank[bank] == oi): e.matmul(
                        pE[0:prow, off:off + n], vaug[:, jt, parity, 0:prow],
                        eb[:, off:off + n], start=fo, stop=lo),
                        waits=[t_mask, st.pE_free if oi == 0 else None], inc=True)
                    box["oi"] += 1
                st.eb_free[idx % 2] = box["out_last"]

            pend = None
            for idx, (jt, col_lo, col_n, mask_kind) in enumerate(jt_list):
                psim = pC if idx % 2 == 0 else pD
                psim_free = st.pC_free if idx % 2 == 0 else st.pD_free
                eb = ebuf[idx % 2]
                mm = None
                pcs = pieces(col_lo, col_n)
                for pi, (off, n) in enumerate(pcs):
                    mm = em(T, lambda e, psim=psim, jt=jt, off=off, n=n, ic_off=ic_off: e.matmul(
                        psim[:, off:off + n], kbuf_ap[:, jt * 128:(jt + 1) * 128],
                        qbuf_ap[:, ic_off + off:ic_off + off + n], start=True, stop=True),
                        waits=[psim_free, gate_tok, t_psz], inc=(pi == len(pcs) - 1))
                bias_ap = cbias[:, 0:1] if exp_bias != 0.0 else 0.0
                t_exp = em(S, lambda e, psim=psim, eb=eb, col_lo=col_lo, col_n=col_n,
                           bias_ap=bias_ap: e.activation(
                    eb[:, col_lo:col_lo + col_n], psim[:, col_lo:col_lo + col_n], AF.Exp,
                    bias=bias_ap), waits=[mm, st.eb_free[idx % 2]], inc="ss")
                if idx % 2 == 0:
                    st.pC_free = t_exp
                else:
                    st.pD_free = t_exp
                t_mask = t_exp
                if mask_kind is not None:
                    mt = m0_t if mask_kind == "m0" else mleft_t
                    t_mld = t_m0 if mask_kind == "m0" else t_mleft
                    mlo = col_lo if mask_kind == "m0" else col_lo + col_n - 128
                    t_mask = em(V, lambda e, eb=eb, mlo=mlo, mt=mt: e.tensor_tensor(
                        eb[:, mlo:mlo + 128], eb[:, mlo:mlo + 128], mt[:, :], ALU.mult),
                        waits=[t_exp, t_mld], inc=True)
                if pend is not None:
                    emit_out(*pend)
                pend = (idx, jt, eb, pcs, t_mask)
            if pend is not None:
                emit_out(*pend)
            out_last = box["out_last"]
            oi = box["oi"]
            if do_null:
                mmn = []
                for half in range(n_halves):
                    mmn.append(em(T, lambda e, half=half, ic_off=ic_off: e.matmul(
                        pA[half * 32:half * 32 + 2, 0:512], kbuf_ap[:, 2048:2050],
                        qbuf_ap[:, ic_off + half * 512:ic_off + (half + 1) * 512],
                        start=True, stop=True), waits=[st.pA_free, t_psz, gate_tok], inc=True))
                t_en = None
                for half in range(n_halves):
                    nbias = cbias[:, 0:1] if exp_bias != 0.0 else 0.0
                    t_en = em(S, lambda e, half=half, nbias=nbias: e.activation(
                        enull[:, half * 512:(half + 1) * 512],
                        pA[half * 32:half * 32 + 2, 0:512], AF.Exp, bias=nbias),
                        waits=[mmn[-1]], inc="ss")
                st.pA_free = t_en
                for half in range(n_halves):
                    fo = half not in touched
                    touched.add(half)
                    out_last = em(T, lambda e, half=half, fo=fo,
                                  lo=(last_of_bank[half] == oi): e.matmul(
                        pE[0:prow, half * 512:(half + 1) * 512], vnaug[:, parity, 0:prow],
                        enull[:, half * 512:(half + 1) * 512], start=fo, stop=lo),
                        waits=[t_en, st.pE_free if oi == 0 else None], inc=True)
                    oi += 1
            st.vaug_free = out_last
            st.qk_free = out_last
            t_rc = em(V, lambda e, ic_n=ic_n, lrow=lrow: e.reciprocal(
                rowf[0:1, 0:ic_n], pE[lrow:lrow + 1, 0:ic_n]), waits=[out_last], inc=True)
            t_rb = em(V, lambda e, ic_n=ic_n: e.tensor_copy(rowb[0:1, 0:ic_n], rowf[0:1, 0:ic_n]),
                      waits=[t_rc], inc=True)
            for half in range(n_halves):
                n = min(512, ic_n - half * 512)
                t_bc = em(T, lambda e, half=half, n=n: e.matmul(
                    pA[:, 0:n], ones_r[0:1, 0:128], rowb[0:1, half * 512:half * 512 + n],
                    start=True, stop=True), waits=[t_rb, st.pA_free, t_ones], inc=True)
                t_rbt = em(V, lambda e, half=half, n=n: e.tensor_copy(
                    rbt[:, half * 512:half * 512 + n], pA[:, 0:n]), waits=[t_bc], inc=True)
                st.pA_free = t_rbt
            ot_ap = ot_dst_fn(ici)
            if DBG.get("trace_evac"):
                print("EVAC ici", ici, "ot offset", ot_ap.offset, "pA_free", st.pA_free,
                      "next sv", mk.cnt["sv"] + 1)
            if DBG.get("skip_evac_ici") == ici:
                t_ot = em(V, lambda e: e.memset(stage_f[0:1, 0:4], 0.0), waits=[st.pA_free], inc=True)
            else:
                t_ot = em(V, lambda e, ot_ap=ot_ap, ic_n=ic_n: e.tensor_tensor(
                    ot_ap, pE[orows[0]:orows[1], 0:ic_n], rbt[orows[0]:orows[1], 0:ic_n], ALU.mult),
                    waits=[st.pA_free], inc=True)
            st.pE_free = t_ot
            if DBG.get("chunk_barrier"):
                mk.wait_only(T, [t_ot])
                mk.wait_only(S, [t_ot])

    def proj_tok_collect(rhs_ap, ncols, evac, gate=(), lhs_buf=None):
        if lhs_buf is None:
            lhs_buf = xt
        toks = []
        for it in range(16):
            use_pC = (it % 2 == 0)
            ps = pC if use_pC else pD
            ps_free = st.pC_free if use_pC else st.pD_free
            mm = None
            for c in range(8):
                off, left = 0, ncols
                while left > 0:
                    n = min(512, left)
                    mm = em(T, lambda e, ps=ps, c=c, it=it, off=off, n=n, lhs_buf=lhs_buf: e.matmul(
                        ps[:, off:off + n], lhs_buf[:, c, it * 128:(it + 1) * 128],
                        rhs_ap(c)[:, off:off + n], start=(c == 0), stop=(c == 7)),
                        waits=[ps_free, t_psz] + list(gate), inc=(c == 7 and left == n))
                    off += n
                    left -= n
            if lhs_buf is xt:
                st.xt_free = mm
            st.wres_free = mm
            t_e = evac(it, ps, mm)
            toks.append(t_e)
            if use_pC:
                st.pC_free = t_e
            else:
                st.pD_free = t_e
        return toks

    def run_stage1(seq, rtoks):
        t_arena = init_attn_arena()
        if len(tuple(DBG.get("hps", range(8)))) < 8 or DBG.get("memset_ot"):
            em(V, lambda e: e.memset(ot[:, :, :], 0.0), waits=[st.pE_free])
        if DBG.get("no_xt"):
            t_xt = None
        else:
            t_xt = build_xt(rtoks, shift=False)
        t_bo = bcast_row(bc1[:, :], W["l0_bo"], waits=[st.bc1_free])
        if DBG.get("no_wres"):
            t_wo = em(V, lambda e: e.memset(wres[:, :, :], 0.0), waits=[st.wres_free], inc=True)
        else:
            t_wo = build_wresT(lambda rc: W["l0_wo"][rc * 128:(rc + 1) * 128, :])
        slot_free = [st.wld_free] * 3
        wt_last = None
        for hp in DBG.get("hps", range(8)):
            for widx, (wname, wbuf) in enumerate(() if hp in DBG.get("wt_off_hps", ()) else (("l0_wq", wt1), ("l0_wk", wt2), ("l0_wv", wt3))):
                base = widx * 1024
                t_l = dma(G, wld[:, base:base + 1024], W[wname][hp * 128:(hp + 1) * 128, :],
                          waits=[slot_free[widx], t_wo if hp == 0 else None])
                srcs = [wld[:, base + c * 128:base + (c + 1) * 128] for c in range(8)]

                def w0(e, wbuf=wbuf):
                    return e.tensor_copy(wbuf[:, :, :],
                                         pF[:, 0:1024].rearrange("p (c n) -> p c n", n=128))
                wt_last = trans_group(srcs, [w0], waits=[t_l, st.qk_free])
                slot_free[widx] = wt_last
            st.wld_free = wt_last
            t_wt = wt_last
            prj_last = None
            for which, wbuf in (() if hp in DBG.get("proj_off_hps", ()) else (("q", wt1), ("k", wt2), ("v", wt3))):
                for nchunk in range(4):
                    use_pC = (nchunk % 2 == 0)
                    ps = pC if use_pC else pD
                    ps_free = st.pC_free if use_pC else st.pD_free
                    mm = None
                    for c in range(8):
                        mm = em(T, lambda e, ps=ps, wbuf=wbuf, c=c, nchunk=nchunk: e.matmul(
                            ps[:, 0:512], wbuf[:, c, :], xt[:, c, nchunk * 512:(nchunk + 1) * 512],
                            start=(c == 0), stop=(c == 7)),
                            waits=[t_wt, t_xt, ps_free, t_psz,
                                   st.qk_free if c == 0 else None], inc=(c == 7))
                    st.xt_free = mm
                    if which == "q":
                        t_e = em(S, lambda e, ps=ps, nchunk=nchunk: e.activation(
                            qt[0:128, nchunk * 512:(nchunk + 1) * 512], ps[:, 0:512], AF.Copy,
                            scale=0.125), waits=[mm, t_arena[0]], inc="ss")
                    elif which == "k":
                        t_e = em(S, lambda e, ps=ps, nchunk=nchunk: e.copy(
                            kt[0:128, nchunk * 512:(nchunk + 1) * 512], ps[:, 0:512]),
                            waits=[mm], inc="ss")
                    else:
                        t_e = em(S, lambda e, ps=ps: e.copy(stage_b2[:, 0:512], ps[:, 0:512]),
                                 waits=[mm, prj_last], inc="ss")
                        it0 = nchunk * 4
                        srcs = [stage_b2[:, k * 128:(k + 1) * 128] for k in range(4)]

                        def w0(e, it0=it0):
                            return e.tensor_copy(
                                vaug[:, it0:it0 + 4, 0, 0:64],
                                pF[:, 0:512].rearrange("p (c n) -> p c n", n=128)[:, :, 0:64])

                        def w1(e, it0=it0):
                            return e.tensor_copy(
                                vaug[:, it0:it0 + 4, 1, 64:128],
                                pF[:, 0:512].rearrange("p (c n) -> p c n", n=128)[:, :, 64:128])
                        t_e = trans_group(srcs, [w0, w1],
                                          waits=[t_e, t_arena[1], st.vaug_free])
                    if use_pC:
                        st.pC_free = t_e
                    else:
                        st.pD_free = t_e
                    prj_last = t_e
            for h in (DBG.get("hs", range(2)) if hp not in DBG.get("attn_off_hps", ()) else ()):
                jtl = [(jt, 0, 1024, None) for jt in DBG.get("jts", range(16))]
                chunks = [(0, 1024, jtl, False), (1024, 1024, jtl, False)]
                chunks = [chunks[i] for i in DBG.get("ics", range(2))]

                def ot_dst(ici, hp=hp, h=h):
                    r0 = 0 if h == 0 else 64
                    return ot[r0:r0 + 64, hp, ici * 1024:(ici + 1) * 1024]
                attn_head(kbuf_ap=kt[64 * h:64 * h + 64, 0:2048],
                          qbuf_ap=qt[64 * h:64 * h + 64, 0:2048],
                          chunks=chunks, parity=h, ot_dst_fn=ot_dst, gate_tok=prj_last)
        t_ot_all = st.pE_free

        def evac_z(it, ps, mm):
            t_s = em(V, lambda e, ps=ps: e.tensor_tensor(stage_f[:, :], ps[:, 0:1024], bc1[:, :],
                                                         ALU.add), waits=[mm, t_bo])
            return em(V, lambda e, it=it: e.tensor_tensor(resid[:, it, :], resid[:, it, :],
                                                          stage_f[:, :], ALU.add),
                      waits=[t_s], inc=True)
        ztoks = proj_tok_collect(lambda c: wres[:, c, 0:1024], 1024, evac_z,
                                 gate=[t_wo, t_ot_all], lhs_buf=ot)
        if DBG.get("no_ln"):
            return ztoks
        return ln_apply_inplace(ln_stats(ztoks))

    def run_stage2(seq, rtoks):
        t_ws = bcast_row(bc2[:, :], W["f0_ws"], waits=[st.vaug_free, rtoks[0]])
        t_bi = bcast_row(bc3[:, :], W["f0_b"], waits=[rtoks[0]])
        t_wq = build_wresT(lambda rc: W["f0_wq"][rc * 128:(rc + 1) * 128, :])
        ztoks = []
        for it in range(16):
            w = [rtoks[it]]
            t_am = em(V, lambda e, it=it: e.tensor_reduce(statw[:, it:it + 1], resid[:, it, :],
                                                          AX.X, ALU.max,
                                                          apply_absolute_value=True),
                      waits=w, inc=True)
            t_ts = em(V, lambda e, it=it: e.tensor_scalar(statw[:, it:it + 1],
                                                          statw[:, it:it + 1],
                                                          1e-6, 1.0 / 127.0, ALU.max, ALU.mult),
                      waits=[t_am])
            t_r = em(V, lambda e, it=it: e.reciprocal(statw[:, 16 + it:17 + it],
                                                      statw[:, it:it + 1]),
                     waits=[t_ts], inc=True)
            t_m = em(V, lambda e, it=it: e.tensor_scalar(stage_f[:, :], resid[:, it, :],
                                                         statw[:, 16 + it:17 + it], None,
                                                         ALU.mult), waits=[t_r], inc=True)
            t_rnd = em(S, lambda e: e.activation(stage_f[:, :], stage_f[:, :], AF.Copy, bias=RND),
                       waits=[t_m], inc=True)
            t_rnd = em(S, lambda e: e.activation(stage_b[:, :], stage_f[:, :], AF.Copy, bias=-RND),
                       waits=[t_rnd], inc=True)
            srcs = [stage_b[:, c * 128:(c + 1) * 128] for c in range(8)]

            def wxq(e, it=it):
                return e.tensor_copy(xt[:, 0:8, it * 128:(it + 1) * 128],
                                     pF[:, 0:1024].rearrange("p (c n) -> p c n", n=128))
            t_tr = trans_group(srcs, [wxq], waits=[t_rnd, st.xt_free])
            use_pC = (it % 2 == 0)
            ps = pC if use_pC else pD
            ps_free = st.pC_free if use_pC else st.pD_free
            mm = None
            for c in range(8):
                for half in range(2):
                    mm = em(T, lambda e, ps=ps, c=c, it=it, half=half: e.matmul(
                        ps[:, half * 512:(half + 1) * 512], xt[:, c, it * 128:(it + 1) * 128],
                        wres[:, c, half * 512:(half + 1) * 512], start=(c == 0), stop=(c == 7)),
                        waits=[t_tr, t_wq, ps_free, t_psz], inc=(c == 7 and half == 1))
            st.xt_free = mm
            st.wres_free = mm
            t_dq = em(V, lambda e, ps=ps, it=it: e.scalar_tensor_tensor(
                stage_f[:, :], ps[:, 0:1024], statw[:, it:it + 1], bc2[:, :], ALU.mult, ALU.mult),
                waits=[mm, t_ws, t_bi])
            t_b3 = em(V, lambda e: e.tensor_tensor(stage_f[:, :], stage_f[:, :], bc3[:, :],
                                                   ALU.add), waits=[t_dq])
            t_e = em(V, lambda e, it=it: e.tensor_tensor(resid[:, it, :], resid[:, it, :],
                                                         stage_f[:, :], ALU.add),
                     waits=[t_b3], inc=True)
            ztoks.append(t_e)
            if use_pC:
                st.pC_free = t_e
            else:
                st.pD_free = t_e
        return ln_apply_inplace(ln_stats(ztoks))

    wldv = wld[:, 0:3072].rearrange("p (c n) -> p c n", n=384)

    def qkv_pair(wq_ap, wk_ap, wv_ap, t_xt, t_arena, first_pair, t_qsb, t_ksb):
        t_l1 = dma(G, wldv[:, :, 0:128], wq_ap.rearrange("(c p) n -> p c n", p=128),
                   waits=[st.qk_free, st.wld_free, st.xt_free if first_pair else None])
        t_l2 = dma(G, wldv[:, :, 128:256], wk_ap.rearrange("(c p) n -> p c n", p=128))
        t_l3 = dma(G, wldv[:, :, 256:384], wv_ap.rearrange("(c p) n -> p c n", p=128),
                   waits=[st.vaug_free])
        prj_last = None
        for it in range(16):
            use_pC = (it % 2 == 0)
            ps = pC if use_pC else pD
            ps_free = st.pC_free if use_pC else st.pD_free
            mm = None
            for c in range(8):
                mm = em(T, lambda e, ps=ps, c=c, it=it: e.matmul(
                    ps[:, 0:384], xt[:, c, it * 128:(it + 1) * 128], wldv[:, c, :],
                    start=(c == 0), stop=(c == 7)),
                    waits=[t_l1, t_l2, t_l3, t_xt, ps_free, t_psz], inc=(c == 7))
            st.xt_free = mm
            st.wld_free = mm
            em(V, lambda e, ps=ps, it=it: e.tensor_copy(qtok[:, it, :], ps[:, 0:128]),
               waits=[mm])
            t_kc = em(V, lambda e, ps=ps, it=it: e.tensor_copy(ktok[:, it, :], ps[:, 128:256]),
                      inc=True)
            em(V, lambda e, ps=ps, it=it: e.tensor_copy(vaug[:, it, 0, 0:64], ps[:, 256:320]),
               waits=[mm, t_arena[1], st.vaug_free if it == 0 else None])
            t_v = em(V, lambda e, ps=ps, it=it: e.tensor_copy(vaug[:, it, 1, 64:128],
                                                              ps[:, 320:384]),
                     waits=[t_kc], inc=True)
            t_e = t_v
            if use_pC:
                st.pC_free = t_e
            else:
                st.pD_free = t_e
            prj_last = t_e
        t_red2 = None
        for which, tokb, col0 in (("q", qtok, 0), ("k", ktok, 2)):
            for it in range(16):
                t_sb = em(V, lambda e, tokb=tokb, it=it: e.tensor_tensor(
                    stage_b[:, 0:128], tokb[:, it, :], tokb[:, it, :], ALU.mult),
                    waits=[prj_last])
                t_red2 = em(V, lambda e, it=it, col0=col0: e.tensor_reduce(
                    sqb[:, it, col0:col0 + 2],
                    stage_b[:, 0:128].rearrange("p (s c) -> p s c", c=64), AX.X, ALU.add),
                    waits=[t_sb], inc=True)
        sqf = sqb[:, :, :].rearrange("p a b -> p (a b)")
        t_sq2 = em(S, lambda e: e.activation(sqf, sqf, AF.Sqrt), waits=[t_red2], inc=True)
        t_mx = em(V, lambda e: e.tensor_scalar(sqf, sqf, 1e-12, None, ALU.max),
                  waits=[t_sq2])
        t_rcp = em(V, lambda e: e.reciprocal(sqf, sqf), waits=[t_mx], inc=True)
        t_nq = {}
        for which, tokb, col0, scb in (("q", qtok, 0, qs_bc), ("k", ktok, 2, ks_bc)):
            lastn = None
            for it in range(16):
                t_n1 = em(V, lambda e, tokb=tokb, it=it, col0=col0: e.tensor_tensor(
                    tokb[:, it, :].rearrange("p (s c) -> p s c", c=64),
                    tokb[:, it, :].rearrange("p (s c) -> p s c", c=64),
                    sqb[:, it, col0:col0 + 2].rearrange("p (s c) -> p s c", c=1)
                    .broadcast_to([128, 2, 64]), ALU.mult), waits=[t_rcp])
                lastn = em(V, lambda e, tokb=tokb, it=it, scb=scb: e.tensor_tensor(
                    tokb[:, it, :].rearrange("p (s c) -> p s c", c=64),
                    tokb[:, it, :].rearrange("p (s c) -> p s c", c=64),
                    scb[:, 0:64].rearrange("p (a c) -> p a c", a=1).broadcast_to([128, 2, 64]),
                    ALU.mult), waits=[t_qsb, t_ksb, t_n1], inc=True)
            t_nq[which] = lastn
        st.scb_free = t_nq["k"]
        return prj_last, t_nq

    def run_stage3(seq, rtoks):
        t_arena = init_attn_arena()
        t_g = bcast_row(bc1[:, :], W["l1_g"], waits=[rtoks[0]])
        t_qsb = bcast_row(qs_bc[:, :], W["l1_qs"])
        t_ksb = bcast_row(ks_bc[:, :], W["l1_ks"])
        t_rs = ln_stats(rtoks)

        def pre_ln(it):
            t_p = em(V, lambda e, it=it: e.tensor_scalar(stage_f[:, :], resid[:, it, :],
                                                         stats[:, 2, it:it + 1],
                                                         stats[:, 3, it:it + 1],
                                                         ALU.subtract, ALU.mult),
                     waits=[t_rs, t_g])
            return em(V, lambda e: e.tensor_tensor(stage_b[:, :], stage_f[:, :], bc1[:, :],
                                                   ALU.mult), waits=[t_p], inc=True)
        t_xt = build_xt(None, shift=False, pre=pre_ln)

        t_nk = dma(G, nkst[0:32, 0:64],
                   W["l1_nkv"].rearrange("h (j c) d -> (h j) c d", c=2)[:, 0, :])
        t_nacc = em(V, lambda e: e.scalar_tensor_tensor(stage_f[0:32, 0:64], nkst[0:32, :], 1.0,
                                                        nkst[0:32, :], ALU.mult, ALU.mult,
                                                        accum_out=statw[0:32, 0:1]),
                    waits=[t_nk], inc=True)
        t_sqn = em(S, lambda e: e.activation(statw[0:32, 1:2], statw[0:32, 0:1], AF.Sqrt),
                   waits=[t_nacc], inc=True)
        t_nm = em(V, lambda e: e.tensor_scalar(statw[0:32, 1:2], statw[0:32, 1:2], 1e-12, None,
                                               ALU.max), waits=[t_sqn])
        t_nr = em(V, lambda e: e.reciprocal(statw[0:32, 2:3], statw[0:32, 1:2]), waits=[t_nm])
        t_ns = em(V, lambda e: e.tensor_scalar(stage_f[0:32, 0:64], nkst[0:32, :],
                                               statw[0:32, 2:3], None, ALU.mult), waits=[t_nr])
        t_khn = em(V, lambda e: e.tensor_tensor(stage_b2[0:32, 0:64], stage_f[0:32, 0:64],
                                                ks_bc[0:32, :], ALU.mult),
                   waits=[t_ksb, t_ns], inc=True)
        def wknt(e):
            return e.tensor_copy(knt[0:64, 0:32], pF[0:64, 0:32])
        t_knt = trans_group([stage_b2[0:32, 0:64]], [wknt], waits=[t_khn], n_out=64)

        for pair in range(8):
            prj_last, t_nq = qkv_pair(
                W["l1_wq"][:, pair * 128:(pair + 1) * 128],
                W["l1_wkv"][:, pair * 128:(pair + 1) * 128],
                W["l1_wkv"][:, 1024 + pair * 128:1024 + (pair + 1) * 128],
                t_xt, t_arena, pair == 0, t_qsb, t_ksb)
            for hh in range(2):
                h = pair * 2 + hh
                parity = hh
                t_al = dma(SY, kt[64:69, 0:2050], W["c_alibi"][h], waits=[st.qk_free])
                t_ci = dma(SY, qt[68:69, 0:2048], W["c_ci"][h].unsqueeze(0))
                t_kn = em(V, lambda e, h=h: e.tensor_copy(kt[0:64, 2048:2050],
                                                          knt[0:64, 2 * h:2 * h + 2]),
                          waits=[t_knt, st.qk_free], inc=True)
                tq_last = None
                for g8 in range(2):
                    srcs = [qtok[:, g8 * 8 + i, hh * 64:hh * 64 + 64] for i in range(8)]

                    def wq8(e, g8=g8):
                        return e.tensor_scalar(
                            qt[0:64, g8 * 1024:(g8 + 1) * 1024], pF[0:64, 0:1024],
                            8.0, None, ALU.mult)
                    tq_last = trans_group(srcs, [wq8], waits=[t_nq["q"], st.qk_free], n_out=64)
                    srcs = [ktok[:, g8 * 8 + i, hh * 64:hh * 64 + 64] for i in range(8)]

                    def wk8(e, g8=g8):
                        return e.tensor_copy(kt[0:64, g8 * 1024:(g8 + 1) * 1024], pF[0:64, 0:1024])
                    tq_last = trans_group(srcs, [wk8], waits=[t_nq["k"], st.qk_free], n_out=64)
                t_vn = dma(G, vnaug[:, parity, (0 if parity == 0 else 64):
                                    (64 if parity == 0 else 128)],
                           W["l1_nkv"].rearrange("h (j c) d -> c j h d", c=2)[1][:, h, :],
                           waits=[t_vn0, st.vaug_free])
                mk.wait_only(T, [t_al, t_ci, t_kn, t_vn, t_arena[0]])
                slope = ALIBI16[h]
                chunks = []
                for ic in range(2):
                    i_lo, i_hi = ic * 1024, ic * 1024 + 1023
                    jtl = []
                    for jt in range(16):
                        if 128 * jt > i_hi:
                            continue
                        if slope * (i_lo - (128 * jt + 127)) > SKIP:
                            continue
                        col_lo = max(0, 128 * jt - i_lo)
                        mask = "m0" if 128 * jt >= i_lo else None
                        jtl.append((jt, col_lo, 1024 - col_lo, mask))
                    do_null = slope * (i_lo + 2) <= SKIP
                    chunks.append((ic * 1024, 1024, jtl, do_null))

                def ot_dst(ici, h=h, parity=parity):
                    r0 = 0 if parity == 0 else 64
                    return ot[r0:r0 + 64, h // 2, ici * 1024:(ici + 1) * 1024]
                attn_head(kbuf_ap=kt[0:69, 0:2080], qbuf_ap=qt[0:69, 0:2048],
                          chunks=chunks, parity=parity, ot_dst_fn=ot_dst, gate_tok=tq_last)

        t_ot_all = st.pE_free
        wo_dma = {}
        for c in range(8):
            wo_dma[c] = dma(G, wres[:, c, 0:1024], W["l1_wo"][c * 128:(c + 1) * 128, :],
                            waits=[t_ot_all if c == 0 else None])

        def evac_z3(it, ps, mm):
            return em(V, lambda e, ps=ps, it=it: e.tensor_tensor(
                resid[:, it, :], resid[:, it, :], ps[:, 0:1024], ALU.add), waits=[mm], inc=True)

        ztoks = []
        for it in range(16):
            use_pC = (it % 2 == 0)
            ps = pC if use_pC else pD
            ps_free = st.pC_free if use_pC else st.pD_free
            mm = None
            for c in range(8):
                for half in range(2):
                    mm = em(T, lambda e, ps=ps, c=c, it=it, half=half: e.matmul(
                        ps[:, half * 512:(half + 1) * 512], ot[:, c, it * 128:(it + 1) * 128],
                        wres[:, c, half * 512:(half + 1) * 512], start=(c == 0), stop=(c == 7)),
                        waits=[ps_free, t_psz, wo_dma[c], t_ot_all],
                        inc=(c == 7 and half == 1))
            st.wres_free = mm
            t_e = evac_z3(it, ps, mm)
            ztoks.append(t_e)
            if use_pC:
                st.pC_free = t_e
            else:
                st.pD_free = t_e
        out_toks = ln_apply_inplace(ln_stats(ztoks))
        st.spill_toks = []
        if 4 in stages:
            for q in range(4):
                w = [out_toks[i] for i in range(4 * q, 4 * q + 4)]
                t_sp = dma(SY, xout_sp.rearrange("(t p) d -> p t d", p=128)[:, 4 * q:4 * q + 4, :],
                           resid[:, 4 * q:4 * q + 4, :], waits=w)
                st.spill_toks.append(t_sp)
        return out_toks

    def s4_rotary(t_nq):
        toks = {}
        for which, tokb, tc, ts_ in (("q", qtok, 0, 1), ("k", ktok, 2, 3)):
            last = None
            for g8 in range(2):
                tv = tokb[:, g8 * 8:(g8 + 1) * 8, :].rearrange("p t (s c) -> p t s c", c=64)
                sv = stage_b[:, :].rearrange("p (t s c) -> p t s c", s=2, c=64)

                def rv(tab, g8=g8):
                    return rotc[:, tab, g8 * 8:(g8 + 1) * 8, :] \
                        .rearrange("p t (o c) -> p t o c", o=1).broadcast_to([128, 8, 2, 64])
                t_a = em(V, lambda e, tv=tv, sv=sv: e.tensor_scalar(
                    sv[:, :, :, 0:32], tv[:, :, :, 32:64], -1.0, None, ALU.mult),
                    waits=[t_nq[which], st.stageb_free])
                t_b = em(V, lambda e, tv=tv, sv=sv: e.tensor_copy(
                    sv[:, :, :, 32:64], tv[:, :, :, 0:32]))
                rvs = rv(ts_)
                rvc = rv(tc)
                t_c = em(V, lambda e, sv=sv, rvs=rvs: e.tensor_tensor(
                    sv[:, :, :, :], sv[:, :, :, :], rvs, ALU.mult),
                    waits=[t_a, t_b, t_rot])
                t_d = em(V, lambda e, tv=tv, rvc=rvc: e.tensor_tensor(
                    tv[:, :, :, :], tv[:, :, :, :], rvc, ALU.mult), waits=[t_rot])
                last = em(V, lambda e, tv=tv, sv=sv: e.tensor_tensor(
                    tv[:, :, :, :], tv[:, :, :, :], sv[:, :, :, :], ALU.add),
                    waits=[t_c, t_d], inc=True)
                st.stageb_free = last
            toks[which] = last
        return toks

    def s4_attention(seq, l, rtoks):
        t_arena = init_attn_arena()
        t_qsb = bcast_row(qs_bc[:, :], W["f1_qs"][l], waits=[st.scb_free])
        t_ksb = bcast_row(ks_bc[:, :], W["f1_ks"][l])
        t_xt = build_xt(rtoks, shift=True)
        for pair in range(4):
            _, t_nq = qkv_pair(
                W["f1_wqkv"][l][:, pair * 128:(pair + 1) * 128],
                W["f1_wqkv"][l][:, 512 + pair * 128:512 + (pair + 1) * 128],
                W["f1_wqkv"][l][:, 1024 + pair * 128:1024 + (pair + 1) * 128],
                t_xt, t_arena, pair == 0, t_qsb, t_ksb)
            t_rq = s4_rotary(t_nq)
            for hh in range(2):
                tq_last = None
                for g8 in range(2):
                    srcs = [qtok[:, g8 * 8 + i, hh * 64:hh * 64 + 64] for i in range(8)]

                    def wq4(e, g8=g8):
                        return e.tensor_copy(qt[0:64, g8 * 1024:(g8 + 1) * 1024],
                                             pF[0:64, 0:1024])
                    tq_last = trans_group(srcs, [wq4], waits=[t_rq["q"], st.qk_free], n_out=64)
                    srcs = [ktok[:, g8 * 8 + i, hh * 64:hh * 64 + 64] for i in range(8)]

                    def wk4(e, g8=g8):
                        return e.tensor_copy(kt[0:64, g8 * 1024:(g8 + 1) * 1024],
                                             pF[0:64, 0:1024])
                    tq_last = trans_group(srcs, [wk4], waits=[t_rq["k"], st.qk_free], n_out=64)
                chunks = []
                for w in range(4):
                    jtl = []
                    for jb in [4 * w] + [j for j in range(max(0, 4 * w - 4), 4 * w + 4)
                                         if j != 4 * w]:
                        col_lo = max(0, 128 * jb - 512 * w)
                        col_hi = min(512, 128 * jb + 640 - 512 * w)
                        mask = "mleft" if jb < 4 * w else "m0"
                        jtl.append((jb, col_lo, col_hi - col_lo, mask))
                    chunks.append((512 * w, 512, jtl, False))

                def ot_dst(ici, pair=pair, hh=hh):
                    r0 = 0 if hh == 0 else 64
                    return ot[r0:r0 + 64, pair, ici * 512:(ici + 1) * 512]
                attn_head(kbuf_ap=kt[0:64, 0:2048], qbuf_ap=qt[0:64, 0:2048],
                          chunks=chunks, parity=hh, ot_dst_fn=ot_dst, gate_tok=tq_last,
                          exp_bias=-12.0)
        t_ot_all = st.pE_free
        wo_dma = {}
        for c in range(4):
            wo_dma[c] = dma(G, wres[:, c, 0:1024], W["f1_wo"][l][c * 128:(c + 1) * 128, :],
                            waits=[t_ot_all if c == 0 else None,
                                   st.wres_free if c == 0 else None])
        ztoks = []
        for it in range(16):
            use_pC = (it % 2 == 0)
            ps = pC if use_pC else pD
            ps_free = st.pC_free if use_pC else st.pD_free
            mm = None
            for c in range(4):
                for half in range(2):
                    mm = em(T, lambda e, ps=ps, c=c, it=it, half=half: e.matmul(
                        ps[:, half * 512:(half + 1) * 512], ot[:, c, it * 128:(it + 1) * 128],
                        wres[:, c, half * 512:(half + 1) * 512], start=(c == 0), stop=(c == 3)),
                        waits=[ps_free, t_psz, wo_dma[c], t_ot_all],
                        inc=(c == 3 and half == 1))
            st.wres_free = mm
            t_e = em(V, lambda e, ps=ps, it=it: e.tensor_tensor(
                resid[:, it, :], resid[:, it, :], ps[:, 0:1024], ALU.add),
                waits=[mm, st.spill_toks[it // 4] if l == 0 else None], inc=True)
            ztoks.append(t_e)
            if use_pC:
                st.pC_free = t_e
            else:
                st.pD_free = t_e
        return ztoks

    def load_colvec(dst, src1d, r, waits=()):
        t_l = dma(G, stage_b2[0:r, 0:128], src1d.rearrange("(r p) -> r p", r=r),
                  waits=list(waits) + [st.stageb2_free])

        def wr(e, dst=dst, r=r):
            return e.tensor_copy(dst[:, 0:r], pF[0:128, 0:r])
        t = trans_group([stage_b2[0:r, 0:128]], [wr], waits=[t_l])
        st.stageb2_free = t
        return t

    def s4_ffn(seq, l, rtoks):
        t_xt = build_xt(rtoks, shift=True)
        t_gc = load_colvec(g_c, W["f1_lng"][l], 8)
        t_bc = load_colvec(b_c, W["f1_lnb"][l], 8)
        t_b1a = load_colvec(b1a_c, W["f1_b1a"][l], 22)
        t_b1g = load_colvec(b1g_c, W["f1_b1g"][l], 22)
        t_b2 = bcast_row(bc1[:, :], W["f1_b2"][l], waits=[st.bc1_free, rtoks[0]])
        t_app = None
        for hf in range(2):
            mm_s = mm_q = None
            for w2 in range(2):
                for c in range(8):
                    mm_s = em(T, lambda e, w2=w2, c=c, hf=hf: e.matmul(
                        pC[0:1, w2 * 512:(w2 + 1) * 512], ones_c[:, 0:1],
                        xt[:, c, hf * 1024 + w2 * 512:hf * 1024 + (w2 + 1) * 512],
                        start=(c == 0), stop=(c == 7)),
                        waits=[t_xt, t_ones, st.pC_free if w2 == 0 and c == 0 else None,
                               t_psz], inc=(c == 7))
            for w2 in range(2):
                for c in range(8):
                    t_sq = em(S, lambda e, w2=w2, c=c, hf=hf: e.activation(
                        stage_b2[:, (c % 2) * 512:(c % 2 + 1) * 512],
                        xt[:, c, hf * 1024 + w2 * 512:hf * 1024 + (w2 + 1) * 512],
                        AF.Square), waits=[t_xt, st.stageb2_free], inc="ss")
                    mm_q = em(T, lambda e, w2=w2, c=c: e.matmul(
                        pD[0:1, w2 * 512:(w2 + 1) * 512], ones_c[:, 0:1],
                        stage_b2[:, (c % 2) * 512:(c % 2 + 1) * 512],
                        start=(c == 0), stop=(c == 7)),
                        waits=[t_sq, st.pD_free if w2 == 0 and c == 0 else None, t_psz],
                        inc=True)
                    st.stageb2_free = mm_q
            t_mu = em(V, lambda e: e.tensor_scalar(rowf[0:1, 0:1024], pC[0:1, 0:1024],
                                                   1.0 / 1024.0, None, ALU.mult),
                      waits=[mm_s, st.rowf_free])
            t_ss = em(V, lambda e: e.tensor_copy(stage_f[0:1, 0:1024], pD[0:1, 0:1024]),
                      waits=[mm_q, st.stagef_free])
            st.pC_free = t_mu
            st.pD_free = t_ss
            t_m2 = em(V, lambda e: e.tensor_tensor(rowb[0:1, 0:1024], rowf[0:1, 0:1024],
                                                   rowf[0:1, 0:1024], ALU.mult),
                      waits=[t_mu])
            t_va = em(V, lambda e: e.scalar_tensor_tensor(
                stage_f[0:1, 0:1024], stage_f[0:1, 0:1024], 1.0 / 1024.0, rowb[0:1, 0:1024],
                ALU.mult, ALU.subtract), waits=[t_ss, t_m2])
            t_ve = em(V, lambda e: e.tensor_scalar(stage_f[0:1, 0:1024], stage_f[0:1, 0:1024],
                                                   1e-5, None, ALU.add), waits=[t_va], inc=True)
            t_sg = em(S, lambda e: e.activation(stage_f[0:1, 0:1024], stage_f[0:1, 0:1024],
                                                AF.Sqrt), waits=[t_ve], inc="ss")
            t_rc = em(V, lambda e: e.reciprocal(stage_f[0:1, 0:1024], stage_f[0:1, 0:1024]),
                      waits=[t_sg])
            t_mb = em(V, lambda e: e.tensor_copy(rowb[0:1, 0:1024], rowf[0:1, 0:1024]),
                      waits=[t_m2], inc=True)
            t_rb2 = em(V, lambda e: e.tensor_copy(rbt[0:1, 0:1024], stage_f[0:1, 0:1024]),
                      waits=[t_rc, st.rbt_w_free], inc=True)
            st.stagef_free = t_rb2
            st.rowf_free = t_mb
            t_bm = t_br = None
            for w2 in range(2):
                t1 = em(T, lambda e, w2=w2: e.matmul(
                    pC[:, w2 * 512:(w2 + 1) * 512], ones_r[0:1, 0:128],
                    rowb[0:1, w2 * 512:(w2 + 1) * 512], start=True, stop=True),
                    waits=[t_mb, st.pC_free, t_ones, t_psz], inc=True)
                t_bm = em(V, lambda e, w2=w2: e.tensor_copy(
                    stage_b[:, w2 * 512:(w2 + 1) * 512], pC[:, w2 * 512:(w2 + 1) * 512]),
                    waits=[t1, st.stageb_free], inc=True)
                t2 = em(T, lambda e, w2=w2: e.matmul(
                    pD[:, w2 * 512:(w2 + 1) * 512], ones_r[0:1, 0:128],
                    rbt[0:1, w2 * 512:(w2 + 1) * 512], start=True, stop=True),
                    waits=[t_rb2, st.pD_free, t_ones, t_psz], inc=True)
                st.rbt_free = t2
                t_br = em(V, lambda e, w2=w2: e.tensor_copy(
                    stage_b2[:, w2 * 512:(w2 + 1) * 512], pD[:, w2 * 512:(w2 + 1) * 512]),
                    waits=[t2, st.stageb2_free], inc=True)
            st.pC_free = t_bm
            st.pD_free = t_br
            for c in range(8):
                ta = em(V, lambda e, c=c, hf=hf: e.tensor_tensor(
                    xt[:, c, hf * 1024:(hf + 1) * 1024], xt[:, c, hf * 1024:(hf + 1) * 1024],
                    stage_b[:, :], ALU.subtract), waits=[t_bm, t_br])
                tb = em(V, lambda e, c=c, hf=hf: e.tensor_tensor(
                    xt[:, c, hf * 1024:(hf + 1) * 1024], xt[:, c, hf * 1024:(hf + 1) * 1024],
                    stage_b2[:, :], ALU.mult), waits=[ta])
                t_app = em(V, lambda e, c=c, hf=hf: e.tensor_scalar(
                    xt[:, c, hf * 1024:(hf + 1) * 1024], xt[:, c, hf * 1024:(hf + 1) * 1024],
                    g_c[:, c:c + 1], b_c[:, c:c + 1], ALU.mult, ALU.add),
                    waits=[tb, t_gc, t_bc], inc=True)
            st.stageb_free = t_app
            st.stageb2_free = t_app
        st.xt_free = t_app
        ztoks = {}
        w1_free = [st.wres_free, st.wres_free]
        for hgrp in range(2):
            t_fb = None
            for hl in range(11):
                hc = hgrp * 11 + hl
                wb = w1buf[hc % 2]
                t_wa = dma(G, wb[:, :, 0:128],
                           W["f1_w1a"][l][:, hc * 128:(hc + 1) * 128]
                           .rearrange("(c p) n -> p c n", p=128), waits=[w1_free[hc % 2]])
                t_wg = dma(G, wb[:, :, 128:256],
                           W["f1_w1g"][l][:, hc * 128:(hc + 1) * 128]
                           .rearrange("(c p) n -> p c n", p=128))
                pG = pD if hl % 2 == 0 else pE
                for tc in range(4):
                    par = tc % 2
                    mm_g = mm_a = None
                    for c in range(8):
                        mm_g = em(T, lambda e, wb=wb, tc=tc, c=c, par=par, pG=pG: e.matmul(
                            pG[:, par * 512:(par + 1) * 512], wb[:, c, 128:256],
                            xt[:, c, tc * 512:(tc + 1) * 512], start=(c == 0), stop=(c == 7)),
                            waits=[t_wg, t_app, t_psz,
                                   (st.pD_free if hl % 2 == 0 else st.pE_free)
                                   if c == 0 else None], inc=(c == 7))
                    t_gel = em(S, lambda e, tc=tc, par=par, hc=hc, pG=pG: e.activation(
                        stage_b[:, par * 512:(par + 1) * 512], pG[:, par * 512:(par + 1) * 512],
                        AF.Gelu, bias=b1g_c[:, hc:hc + 1]),
                        waits=[mm_g, t_b1g, st.stageb_free], inc="ss")
                    if hl % 2 == 0:
                        st.pD_free = t_gel
                    else:
                        st.pE_free = t_gel
                    for c in range(8):
                        mm_a = em(T, lambda e, wb=wb, tc=tc, c=c, par=par: e.matmul(
                            pC[:, par * 512:(par + 1) * 512], wb[:, c, 0:128],
                            xt[:, c, tc * 512:(tc + 1) * 512], start=(c == 0), stop=(c == 7)),
                            waits=[t_wa, t_app, t_psz,
                                   st.pC_free if c == 0 else None], inc=(c == 7))
                    t_fb = em(V, lambda e, tc=tc, par=par, hl=hl, hc=hc: e.scalar_tensor_tensor(
                        fbuf[:, hl, tc * 512:(tc + 1) * 512], pC[:, par * 512:(par + 1) * 512],
                        b1a_c[:, hc:hc + 1], stage_b[:, par * 512:(par + 1) * 512],
                        ALU.add, ALU.mult),
                        waits=[mm_a, t_gel, t_b1a,
                               st.fbuf_free if hl == 0 and tc == 0 else None], inc=True)
                    st.stageb_free = t_fb
                    st.pC_free = t_fb
                w1_free[hc % 2] = t_fb
            for wg0, wgn in ((0, 6), (6, 5)):
                w2d = {}
                for i in range(wgn):
                    hc = hgrp * 11 + wg0 + i
                    w2d[i] = dma(G, w2res[:, i, :],
                                 W["f1_w2"][l][hc * 128:(hc + 1) * 128, :],
                                 waits=[st.w2res_free if i == 0 else None])
                for t8 in range(16):
                    use_pC = (t8 % 2 == 0)
                    ps = pC if use_pC else pD
                    ps_free = st.pC_free if use_pC else st.pD_free
                    mm = None
                    for i in range(wgn):
                        for half in range(2):
                            mm = em(T, lambda e, ps=ps, i=i, t8=t8, half=half, wg0=wg0,
                                    wgn=wgn: e.matmul(
                                ps[:, half * 512:(half + 1) * 512],
                                fbuf[:, wg0 + i, t8 * 128:(t8 + 1) * 128],
                                w2res[:, i, half * 512:(half + 1) * 512],
                                start=(i == 0), stop=(i == wgn - 1)),
                                waits=[ps_free, t_psz, w2d[i], t_fb],
                                inc=(i == wgn - 1 and half == 1))
                    st.w2res_free = mm
                    st.fbuf_free = mm
                    t_e = em(V, lambda e, ps=ps, t8=t8: e.tensor_tensor(
                        resid[:, t8, :], resid[:, t8, :], ps[:, 0:1024], ALU.add),
                        waits=[mm, rtoks[t8] if hgrp == 0 and wg0 == 0 else ztoks.get(t8)],
                        inc=True)
                    ztoks[t8] = t_e
                    if use_pC:
                        st.pC_free = t_e
                    else:
                        st.pD_free = t_e
        out = []
        for t8 in range(16):
            out.append(em(V, lambda e, t8=t8: e.tensor_tensor(
                resid[:, t8, :], resid[:, t8, :], bc1[:, :], ALU.add),
                waits=[ztoks[t8], t_b2], inc=True))
        st.bc1_free = out[-1]
        st.vaug_free = st.w2res_free
        st.qk_free = st.w2res_free
        st.eb_free = [st.w2res_free, st.w2res_free]
        st.wres_free = st.w2res_free
        st.xt_free = t_fb
        return out

    def run_stage4(seq, rtoks):
        for l in (0, 1):
            ztoks = s4_attention(seq, l, rtoks)
            rtoks = s4_ffn(seq, l, ztoks)
        toks = []
        for it in range(16):
            xb = xosb if it % 2 == 0 else stage_f
            gate = toks[it - 2] if it >= 2 else (st.rbt_free if it % 2 == 0
                                                 else st.stagef_free)
            t_l = dma(SY, xb[:, :], xout_sp.rearrange("(t p) d -> p t d", p=128)[:, it, :],
                      waits=[st.spill_toks[it // 4], gate,
                             st.rbt_free if it == 0 else None])
            toks.append(em(V, lambda e, it=it, xb=xb: e.tensor_tensor(
                resid[:, it, :], resid[:, it, :], xb[:, :], ALU.add),
                waits=[t_l, rtoks[it]], inc=True))
        return ln_apply_inplace(ln_stats(toks, dummy=fbuf[:, 0, 0:1024]))

    for seq in (0, 1):
        rtoks = load_x(seq)
        if 1 in stages:
            rtoks = run_stage1(seq, rtoks)
        if 2 in stages:
            rtoks = run_stage2(seq, rtoks)
        if 3 in stages:
            rtoks = run_stage3(seq, rtoks)
        if 4 in stages:
            rtoks = run_stage4(seq, rtoks)
        store_out(seq, rtoks)

    mk.wait_only(SY, final_toks)
    mk.run_block()
    n_ins = {e: len(mk.prog[e]) for e in ENGS}
    print("program sizes:", n_ins, "total", sum(n_ins.values()))
    return nc


def make_consts():
    c = {}
    c["c_ident"] = np.eye(128, dtype=ml_dtypes.bfloat16)
    i = np.arange(128)
    c["c_m0"] = (i[None, :] >= i[:, None]).astype(ml_dtypes.bfloat16)
    c["c_mleft"] = (i[None, :] <= i[:, None]).astype(ml_dtypes.bfloat16)
    jeff = np.concatenate([np.arange(2048, dtype=np.float64), [-2.0, -1.0]])
    al = np.zeros((16, 5, 2050), dtype=ml_dtypes.bfloat16)
    for h in range(16):
        r = ALIBI16[h] * jeff
        for t in range(4):
            hi = r.astype(ml_dtypes.bfloat16)
            al[h, t] = hi
            r = r - hi.astype(np.float64)
        al[h, 4] = 1.0
    c["c_alibi"] = al
    ci = np.zeros((16, 2048), dtype=ml_dtypes.bfloat16)
    for h in range(16):
        ci[h] = (-(ALIBI16[h] * np.arange(2048, dtype=np.float64) + 8.0)).astype(ml_dtypes.bfloat16)
    c["c_ci"] = ci
    DH, WIN, N = 64, 512, 2048
    half = np.arange(0, DH, 2, dtype=np.float64)
    inv_freq = 1.0 / 10000.0 ** (half / DH)
    t = np.arange(N, dtype=np.float64)
    freqs = t[:, None] * inv_freq[None, :]
    freqs = np.concatenate([freqs, freqs], axis=-1)
    base = (half + 0.4 * DH) / (1.4 * DH)
    power = (t - N // 2) / (WIN // 2)
    scale = base[None, :] ** power[:, None]
    scale = np.concatenate([scale, scale], axis=-1)
    cos, sin = np.cos(freqs), np.sin(freqs)
    rot = np.zeros((4, 2048, 64), dtype=ml_dtypes.bfloat16)
    rot[0] = (8.0 * cos * scale).astype(ml_dtypes.bfloat16)
    rot[1] = (8.0 * sin * scale).astype(ml_dtypes.bfloat16)
    rot[2] = (cos / scale).astype(ml_dtypes.bfloat16)
    rot[3] = (sin / scale).astype(ml_dtypes.bfloat16)
    c["c_rot"] = rot
    return c


def pad_w1(inputs):
    w1 = np.asarray(inputs["L1_ffn_w1"])
    b1 = np.asarray(inputs["L1_ffn_b1"])
    w2 = np.asarray(inputs["L1_ffn_w2"])
    HID = 2730
    w1a = np.zeros((2, 1024, 2816), np.float32)
    w1g = np.zeros((2, 1024, 2816), np.float32)
    w1a[:, :, :HID] = w1[:, :, :HID]
    w1g[:, :, :HID] = w1[:, :, HID:]
    b1a = np.zeros((2, 2816), np.float32)
    b1g = np.zeros((2, 2816), np.float32)
    b1a[:, :HID] = b1[:, :HID]
    b1g[:, :HID] = b1[:, HID:]
    w2p = np.zeros((2, 2816, 1024), np.float32)
    w2p[:, :HID, :] = w2
    return dict(w1a_pad=w1a, w1g_pad=w1g, b1a_pad=b1a, b1g_pad=b1g, w2_pad=w2p)


def kernel(**inputs):
    from concourse.bass_utils import run_bass_kernel_spmd
    x = np.ascontiguousarray(np.asarray(inputs["L0_attn_x"]))
    consts = make_consts()
    pads = pad_w1(inputs)
    nc = build(stages=(1, 2, 3, 4))
    base = {}
    for k in inputs:
        if k not in ("L0_attn_x", "L1_ffn_w1", "L1_ffn_b1", "L1_ffn_w2"):
            base[k] = np.ascontiguousarray(np.asarray(inputs[k]))
    base.update(consts)
    base.update(pads)
    in_maps = []
    for core in range(8):
        m = dict(base)
        m["x"] = np.ascontiguousarray(x[2 * core:2 * core + 2])
        in_maps.append(m)
    res = run_bass_kernel_spmd(nc, in_maps, core_ids=list(range(8)))
    out = np.concatenate([np.asarray(r["out"]) for r in res.results], axis=0)
    return out.astype(np.float32)
